# version 01
# baseline (speedup 1.0000x reference)
import numpy as np
import concourse.bass as bass
import concourse.mybir as mybir
from concourse.bass_utils import run_bass_kernel_spmd

F32 = mybir.dt.float32
F32R = mybir.dt.float32r
BF16 = mybir.dt.bfloat16
I32 = mybir.dt.int32
AF = mybir.ActivationFunctionType
ALU = mybir.AluOpType

D = 4096
TOK = 2048
NCORE = 8
TG = 512
NG = TOK // TG
FC = D // 128
B, S = 4, 4096
EPS = 1e-5
NBIG = -1.0e9
NH, NKV, HD = 32, 8, 128
NREP = NH // NKV
THETA = 10000.0
FH, FD, FM = 16, 256, 1024
NE, NR = 8, 16
MOE_SCALING = 2.0
MAX_THR = 0.125

ARENA_COLS = 44544
ARENAB_COLS = 16384

DEBUG_STOP = None
DEBUG_TAP = None
CC_GROUPS = [[0, 1], [2, 3], [4, 5], [6, 7]]
DEBUG_MAXPH = None


def _token_slices(core):
    b = core // 2
    if core % 2 == 0:
        return [(b * S + 0, b * S + 1024), (b * S + 3072, b * S + 4096)]
    return [(b * S + 1024, b * S + 3072)]


def _qpos(core):
    if core % 2 == 0:
        return np.concatenate([np.arange(0, 1024), np.arange(3072, 4096)])
    return np.arange(1024, 3072)


KPOS_GLOBAL = np.concatenate(
    [np.arange(0, 1024), np.arange(3072, 4096), np.arange(1024, 3072)]
).astype(np.float32)


def _rope_tables(pos):
    freqs = 1.0 / THETA ** (np.arange(0, HD, 2, dtype=np.float64) / HD)
    ang = np.outer(freqs, pos.astype(np.float64))
    cs, sn = np.cos(ang), np.sin(ang)
    C = np.repeat(cs, 2, axis=0)
    Sm = np.empty_like(C)
    Sm[0::2] = -sn
    Sm[1::2] = sn
    return C.astype(np.float32), Sm.astype(np.float32)


def make_consts(core):
    c = {}
    c["c_ident"] = np.eye(128, dtype=np.float32)
    c["c_onescol"] = np.ones((128, 1), np.float32)
    c["c_onesrow"] = np.ones((1, 128), np.float32)
    c["c_eps"] = np.full((1, 1), EPS, np.float32)
    c["c_rsqhd"] = np.full((128, 1), 1.0 / np.sqrt(HD), np.float32)
    o = np.zeros((128, FC * 128), np.float32)
    for ncc in range(FC):
        for kl in range(128):
            o[kl, ncc * 128 + 4 * ncc + kl // 32] = 1.0
    c["c_o32a"] = o
    r = np.zeros((128, FC * 128), np.float32)
    for ncc in range(FC):
        for p in range(128):
            r[4 * ncc + p // 32, ncc * 128 + p] = 1.0
    c["c_rsel"] = r
    x = np.zeros((128, D), np.float32)
    for k in range(D):
        x[k // 32, k] = 1.0
    c["c_xex"] = x
    r16 = np.zeros((8, 128), np.float32)
    for p in range(128):
        r16[p // 16, p] = 1.0
    c["c_r16"] = r16
    ps = np.zeros((128, 128), np.float32)
    for i in range(0, 128, 2):
        ps[i + 1, i] = 1.0
        ps[i, i + 1] = 1.0
    c["c_pswap"] = ps
    qp = _qpos(core)
    c["c_qpos"] = qp.astype(np.float32).reshape(1, TOK)
    c["c_kpos"] = KPOS_GLOBAL.reshape(S, 1).copy()
    C_, S_ = _rope_tables(qp)
    c["c_ropec"] = C_
    c["c_ropes"] = S_
    return c


CONST_SHAPES = {
    "c_ident": [128, 128], "c_onescol": [128, 1], "c_onesrow": [1, 128],
    "c_eps": [1, 1], "c_rsqhd": [128, 1],
    "c_o32a": [128, FC * 128], "c_rsel": [128, FC * 128], "c_xex": [128, D],
    "c_pswap": [128, 128], "c_qpos": [1, TOK], "c_kpos": [S, 1],
    "c_r16": [8, 128],
    "c_ropec": [128, TOK], "c_ropes": [128, TOK],
}
SMALL_CONSTS = ("c_ident", "c_onescol", "c_onesrow", "c_eps", "c_rsqhd")

DMA_SEMS = ("ld0", "ld1", "lda", "gp0", "gp1", "gp2", "gp3", "gpd", "sc0", "sc1", "scd")
ENG_SEMS = ("pe", "dve", "act", "gpc", "cc")


class Counts:
    def __init__(self):
        for n in DMA_SEMS + ENG_SEMS:
            setattr(self, n, 0)
        self.ph = 0


ENGS = ("sync", "tensor", "vector", "scalar", "gpsimd")


class GroupedDram:

    def __init__(self, base, g):
        self.base = base
        self.g = g

    def __getitem__(self, idx):
        rows, cols = idx
        g = self.g
        return self.base[g * D + rows.start:g * D + rows.stop,
                         cols.start - g * TG:cols.stop - g * TG]


class Arena:
    def __init__(self, ap, cols):
        self.ap = ap
        self.cols = cols
        self.off = 0

    def alloc(self, w):
        assert self.off + w <= self.cols, f"arena overflow {self.off}+{w}"
        sl = self.ap[:, self.off:self.off + w]
        self.off += w
        return sl

    def mark(self):
        return self.off

    def reset(self, off):
        self.off = off


class Ctx:
    def __init__(self, nc, io, dr, arena, arenab, small, psum, c, sems):
        self.nc = nc
        self.io = io
        self.dr = dr
        self.A = arena
        self.AB = arenab
        self.small = small
        self.psum = psum
        self.c = c
        self.sems = sems

    def phase(self, body):
        if not hasattr(self, "_nph"):
            self._nph = 0
        self._nph += 1
        if DEBUG_MAXPH is not None and self._nph > DEBUG_MAXPH:
            return
        nc, c, sems = self.nc, self.c, self.sems
        target = c.ph
        pre = (c.pe, c.dve, c.act, c.gpc)
        with nc.Block() as block:
            emitters = body(block) or {}
            for name in ENGS:
                fn = emitters.get(name)
                sec = getattr(block, name)

                def _make(name=name, fn=fn):
                    def _(eng):
                        eng.wait_ge(sems["ph"], target)
                        if fn is not None:
                            for i, s in enumerate(("pe", "dve", "act", "gpc")):
                                if pre[i]:
                                    eng.wait_ge(sems[s], pre[i])
                            fn(eng)
                        if name == "sync":
                            for s in ("ld0", "ld1", "lda"):
                                if getattr(c, s):
                                    eng.wait_ge(sems[s], getattr(c, s))
                        if name == "gpsimd":
                            for s in ("gp0", "gp1", "gp2", "gp3", "gpd"):
                                if getattr(c, s):
                                    eng.wait_ge(sems[s], getattr(c, s))
                        if name == "scalar":
                            for s in ("sc0", "sc1", "scd"):
                                if getattr(c, s):
                                    eng.wait_ge(sems[s], getattr(c, s))
                        eng.nop().then_inc(sems["ph"], 1)
                    return _
                sec(_make())
        c.ph = target + 5


def emit_tap_sbuf(ctx, sl, r0=0):
    c, sems, io = ctx.c, ctx.sems, ctx.io
    if "dbg" not in io:
        return

    def body(block):
        def gpsimd(eng):
            eng.dma_start(out=io["dbg"][r0:r0 + sl.shape[0], 0:sl.shape[1]],
                          in_=sl).then_inc(sems["gpd"], 16)
        c.gpd += 16
        return {"gpsimd": gpsimd}
    ctx.phase(body)


def emit_tap_dram(ctx, src, rows, cols):
    c, sems, io = ctx.c, ctx.sems, ctx.io
    if "dbg" not in io:
        return

    def body(block):
        def gpsimd(eng):
            eng.dma_start(out=io["dbg"][0:rows, 0:cols],
                          in_=src[0:rows, 0:cols]).then_inc(sems["gpd"], 16)
        c.gpd += 16
        return {"gpsimd": gpsimd}
    ctx.phase(body)


def emit_entry_transpose(ctx, src_ap, dst_dram):
    c, sems, small, psum = ctx.c, ctx.sems, ctx.small, ctx.psum
    mk = ctx.A.mark()
    xt = [ctx.A.alloc(D) for _ in range(2)]
    ev = [ctx.A.alloc(512) for _ in range(2)]
    NT = TOK // 128

    def body(block):
        b_ld = [c.ld0, c.ld1]
        b_gp = [c.gp0, c.gp1]
        b_pe, b_act = c.pe, c.act

        def sync(eng):
            for tt in range(NT):
                if tt >= 2:
                    eng.wait_ge(sems["pe"], b_pe + (tt - 1) * FC)
                eng.dma_start(out=xt[tt % 2], in_=src_ap[tt * 128:(tt + 1) * 128, :]
                              ).then_inc(sems["ld" + str(tt % 2)], 16)
        c.ld0 += 16 * (NT // 2)
        c.ld1 += 16 * (NT // 2)

        def tensor(eng):
            for tt in range(NT):
                eng.wait_ge(sems["ld" + str(tt % 2)],
                            b_ld[tt % 2] + 16 * (tt // 2 + 1))
                for fc in range(FC):
                    g = tt * 8 + fc // 4
                    if g >= 2 and fc % 4 == 0:
                        eng.wait_ge(sems["act"], b_act + g - 1)
                    eng.transpose(psum[g % 2][:, (fc % 4) * 128:(fc % 4 + 1) * 128],
                                  xt[tt % 2][:, fc * 128:(fc + 1) * 128],
                                  small["c_ident"]).then_inc(sems["pe"], 1)
        c.pe += NT * FC

        def scalar(eng):
            for g in range(NT * 8):
                eng.wait_ge(sems["pe"], b_pe + (g + 1) * 4)
                if g >= 2:
                    eng.wait_ge(sems["gp" + str(g % 2)],
                                b_gp[g % 2] + 64 * (g // 2))
                eng.activation(ev[g % 2], psum[g % 2][:, :], AF.Copy
                               ).then_inc(sems["act"], 1)
        c.act += NT * 8

        def gpsimd(eng):
            for g in range(NT * 8):
                tt, fg = divmod(g, 8)
                eng.wait_ge(sems["act"], b_act + g + 1)
                for q in range(4):
                    fc = fg * 4 + q
                    eng.dma_start(
                        out=dst_dram[fc * 128:(fc + 1) * 128, tt * 128:(tt + 1) * 128],
                        in_=ev[g % 2][:, q * 128:(q + 1) * 128],
                    ).then_inc(sems["gp" + str(g % 2)], 16)
        c.gp0 += 64 * (NT * 8 // 2)
        c.gp1 += 64 * (NT * 8 // 2)

        return {"sync": sync, "tensor": tensor, "scalar": scalar, "gpsimd": gpsimd}

    ctx.phase(body)
    ctx.A.reset(mk)


def emit_exit_transpose(ctx, src_dram, dst_ap):
    c, sems, small, psum = ctx.c, ctx.sems, ctx.small, ctx.psum
    mk = ctx.A.mark()
    ft = [ctx.A.alloc(512) for _ in range(2)]
    ot = [ctx.A.alloc(D) for _ in range(2)]
    NT = TOK // 128

    def body(block):
        b_ld = [c.ld0, c.ld1]
        b_gp = [c.gp0, c.gp1]
        b_pe, b_act = c.pe, c.act

        def sync(eng):
            i = 0
            for tt in range(NT):
                for fg in range(8):
                    if i >= 2:
                        eng.wait_ge(sems["pe"], b_pe + (i - 1) * 4)
                    eng.dma_start(
                        out=ft[i % 2].rearrange("p (q t) -> p q t", q=4),
                        in_=src_dram[fg * 512:(fg + 1) * 512, tt * 128:(tt + 1) * 128]
                            .rearrange("(q p) t -> p q t", p=128),
                    ).then_inc(sems["ld" + str(i % 2)], 16)
                    i += 1
        c.ld0 += 16 * (NT * 8 // 2)
        c.ld1 += 16 * (NT * 8 // 2)

        def tensor(eng):
            i = 0
            for tt in range(NT):
                for fg in range(8):
                    eng.wait_ge(sems["ld" + str(i % 2)],
                                b_ld[i % 2] + 16 * (i // 2 + 1))
                    if i >= 2:
                        eng.wait_ge(sems["act"], b_act + i - 1)
                    for q in range(4):
                        eng.transpose(psum[i % 2][:, q * 128:(q + 1) * 128],
                                      ft[i % 2][:, q * 128:(q + 1) * 128],
                                      small["c_ident"]).then_inc(sems["pe"], 1)
                    i += 1
        c.pe += NT * 8 * 4

        def scalar(eng):
            i = 0
            for tt in range(NT):
                if tt >= 2:
                    eng.wait_ge(sems["gp" + str(tt % 2)],
                                b_gp[tt % 2] + 16 * (tt // 2))
                for fg in range(8):
                    eng.wait_ge(sems["pe"], b_pe + (i + 1) * 4)
                    eng.activation(ot[tt % 2][:, fg * 512:(fg + 1) * 512],
                                   psum[i % 2][:, :], AF.Copy).then_inc(sems["act"], 1)
                    i += 1
        c.act += NT * 8

        def gpsimd(eng):
            for tt in range(NT):
                eng.wait_ge(sems["act"], b_act + (tt + 1) * 8)
                eng.dma_start(out=dst_ap[tt * 128:(tt + 1) * 128, :],
                              in_=ot[tt % 2]).then_inc(sems["gp" + str(tt % 2)], 16)
        c.gp0 += 16 * (NT // 2)
        c.gp1 += 16 * (NT // 2)

        return {"sync": sync, "tensor": tensor, "scalar": scalar, "gpsimd": gpsimd}

    ctx.phase(body)
    ctx.A.reset(mk)


def _row_math(eng, sems, c, b_dve, b_act, psum, m_row, v_row, t_row, s_row, FCn):
    eng.tensor_scalar_mul(m_row, psum[2][0:1, :TG], 1.0 / D).then_inc(sems["dve"], 1)
    eng.tensor_scalar_mul(v_row, psum[3][0:1, :TG], 1.0 / D).then_inc(sems["dve"], 1)
    eng.wait_ge(sems["dve"], b_dve + 2)
    eng.tensor_tensor(t_row, m_row, m_row, ALU.mult).then_inc(sems["dve"], 1)
    eng.wait_ge(sems["dve"], b_dve + 3)
    eng.tensor_tensor(v_row, v_row, t_row, ALU.subtract).then_inc(sems["dve"], 1)
    eng.wait_ge(sems["act"], b_act + FCn + 1)
    eng.reciprocal(s_row, s_row).then_inc(sems["dve"], 1)


def emit_ln_stats_dram(ctx, src_dram, g, mB, sB, rows):
    c, sems, small, psum = ctx.c, ctx.sems, ctx.small, ctx.psum
    mk = ctx.A.mark()
    xb = ctx.A.alloc(2 * TG)
    sq = ctx.A.alloc(2 * TG)
    m_row = rows[0:1, 0:TG]
    v_row = rows[0:1, TG:2 * TG]
    t_row = rows[0:1, 2 * TG:3 * TG]
    s_row = v_row

    def body(block):
        b_ld = [c.ld0, c.ld1]
        b_pe, b_act, b_dve = c.pe, c.act, c.dve

        def sync(eng):
            for fc in range(FC):
                if fc >= 2:
                    eng.wait_ge(sems["pe"], b_pe + 2 * (fc - 1))
                eng.dma_start(out=xb[:, (fc % 2) * TG:(fc % 2 + 1) * TG],
                              in_=src_dram[fc * 128:(fc + 1) * 128, g * TG:(g + 1) * TG]
                              ).then_inc(sems["ld" + str(fc % 2)], 16)
        c.ld0 += 16 * (FC // 2)
        c.ld1 += 16 * (FC // 2)

        def scalar(eng):
            for fc in range(FC):
                eng.wait_ge(sems["ld" + str(fc % 2)],
                            b_ld[fc % 2] + 16 * (fc // 2 + 1))
                if fc >= 2:
                    eng.wait_ge(sems["pe"], b_pe + 2 * (fc - 1))
                eng.activation(sq[:, (fc % 2) * TG:(fc % 2 + 1) * TG],
                               xb[:, (fc % 2) * TG:(fc % 2 + 1) * TG],
                               AF.Square).then_inc(sems["act"], 1)
            eng.wait_ge(sems["dve"], b_dve + 4)
            eng.activation(s_row, v_row, AF.Sqrt, bias=small["c_eps"]
                           ).then_inc(sems["act"], 1)
            eng.wait_ge(sems["pe"], b_pe + 2 * FC + 2)
            eng.activation(mB, psum[4][:, :TG], AF.Copy)
            eng.activation(sB, psum[5][:, :TG], AF.Copy).then_inc(sems["act"], 2)
        c.act += FC + 3

        def tensor(eng):
            for fc in range(FC):
                eng.wait_ge(sems["act"], b_act + fc + 1)
                xsl = xb[:, (fc % 2) * TG:(fc % 2 + 1) * TG]
                qsl = sq[:, (fc % 2) * TG:(fc % 2 + 1) * TG]
                eng.matmul(psum[2][0:1, :TG], small["c_onescol"], xsl,
                           start=(fc == 0), stop=(fc == FC - 1))
                eng.matmul(psum[3][0:1, :TG], small["c_onescol"], qsl,
                           start=(fc == 0), stop=(fc == FC - 1)).then_inc(sems["pe"], 2)
            eng.wait_ge(sems["dve"], b_dve + 5)
            eng.matmul(psum[4][:, :TG], small["c_onesrow"], m_row, start=True, stop=True)
            eng.matmul(psum[5][:, :TG], small["c_onesrow"], s_row,
                       start=True, stop=True).then_inc(sems["pe"], 2)
        c.pe += 2 * FC + 2

        def vector(eng):
            eng.wait_ge(sems["pe"], b_pe + 2 * FC)
            _row_math(eng, sems, c, b_dve, b_act, psum, m_row, v_row, t_row, s_row, FC)
        c.dve += 5

        return {"sync": sync, "scalar": scalar, "tensor": tensor, "vector": vector}

    ctx.phase(body)
    ctx.A.reset(mk)


def emit_ln_apply_dram(ctx, src_dram, g, mB, sB, xhat, store_dram=None):
    c, sems = ctx.c, ctx.sems
    mk = ctx.A.mark()
    xb = ctx.A.alloc(2 * TG)
    tb = ctx.A.alloc(2 * TG)

    def body(block):
        b_ld = [c.ld0, c.ld1]
        b_dve, b_gpc = c.dve, c.gpc

        def sync(eng):
            for fc in range(FC):
                if fc >= 2:
                    eng.wait_ge(sems["dve"], b_dve + fc - 1)
                eng.dma_start(out=xb[:, (fc % 2) * TG:(fc % 2 + 1) * TG],
                              in_=src_dram[fc * 128:(fc + 1) * 128, g * TG:(g + 1) * TG]
                              ).then_inc(sems["ld" + str(fc % 2)], 16)
        c.ld0 += 16 * (FC // 2)
        c.ld1 += 16 * (FC // 2)

        def vector(eng):
            for fc in range(FC):
                eng.wait_ge(sems["ld" + str(fc % 2)],
                            b_ld[fc % 2] + 16 * (fc // 2 + 1))
                if fc >= 2:
                    eng.wait_ge(sems["gpc"], b_gpc + fc - 1)
                eng.tensor_tensor(tb[:, (fc % 2) * TG:(fc % 2 + 1) * TG],
                                  xb[:, (fc % 2) * TG:(fc % 2 + 1) * TG], mB,
                                  ALU.subtract).then_inc(sems["dve"], 1)
        c.dve += FC

        def gpsimd(eng):
            for fc in range(FC):
                eng.wait_ge(sems["dve"], b_dve + fc + 1)
                eng.tensor_tensor(xhat[:, fc * TG:(fc + 1) * TG],
                                  tb[:, (fc % 2) * TG:(fc % 2 + 1) * TG], sB,
                                  ALU.mult).then_inc(sems["gpc"], 1)
                if store_dram is not None:
                    eng.wait_ge(sems["gpc"], b_gpc + fc + 1)
                    eng.dma_start(
                        out=store_dram[fc * 128:(fc + 1) * 128, g * TG:(g + 1) * TG],
                        in_=xhat[:, fc * TG:(fc + 1) * TG]).then_inc(sems["gpd"], 16)
        c.gpc += FC
        if store_dram is not None:
            c.gpd += 16 * FC

        return {"sync": sync, "vector": vector, "gpsimd": gpsimd}

    ctx.phase(body)
    ctx.A.reset(mk)


def emit_residual_from_dram(ctx, x_dram, y_dram, g, m2B, s2B, out_dram):
    c, sems = ctx.c, ctx.sems
    mk = ctx.A.mark()
    xb = ctx.A.alloc(2 * TG)
    yb = ctx.A.alloc(2 * TG)
    tb = ctx.A.alloc(2 * TG)

    def body(block):
        b_ld = [c.ld0, c.ld1]
        b_sc = [c.sc0, c.sc1]
        b_gp = [c.gp0, c.gp1]
        b_dve, b_gpc = c.dve, c.gpc

        def sync(eng):
            for fc in range(FC):
                if fc >= 2:
                    eng.wait_ge(sems["gpc"], b_gpc + fc - 1)
                eng.dma_start(out=xb[:, (fc % 2) * TG:(fc % 2 + 1) * TG],
                              in_=x_dram[fc * 128:(fc + 1) * 128, g * TG:(g + 1) * TG]
                              ).then_inc(sems["ld" + str(fc % 2)], 16)
        c.ld0 += 16 * (FC // 2)
        c.ld1 += 16 * (FC // 2)

        def scalar(eng):
            for fc in range(FC):
                if fc >= 2:
                    eng.wait_ge(sems["dve"], b_dve + 2 * (fc - 1))
                eng.dma_start(out=yb[:, (fc % 2) * TG:(fc % 2 + 1) * TG],
                              in_=y_dram[fc * 128:(fc + 1) * 128, g * TG:(g + 1) * TG]
                              ).then_inc(sems["sc" + str(fc % 2)], 16)
        c.sc0 += 16 * (FC // 2)
        c.sc1 += 16 * (FC // 2)

        def vector(eng):
            for fc in range(FC):
                eng.wait_ge(sems["sc" + str(fc % 2)],
                            b_sc[fc % 2] + 16 * (fc // 2 + 1))
                if fc >= 2:
                    eng.wait_ge(sems["gp" + str(fc % 2)],
                                b_gp[fc % 2] + 16 * (fc // 2))
                eng.tensor_tensor(tb[:, (fc % 2) * TG:(fc % 2 + 1) * TG],
                                  yb[:, (fc % 2) * TG:(fc % 2 + 1) * TG], m2B,
                                  ALU.subtract).then_inc(sems["dve"], 1)
                eng.wait_ge(sems["dve"], b_dve + 2 * fc + 1)
                eng.tensor_tensor(tb[:, (fc % 2) * TG:(fc % 2 + 1) * TG],
                                  tb[:, (fc % 2) * TG:(fc % 2 + 1) * TG], s2B,
                                  ALU.mult).then_inc(sems["dve"], 1)
        c.dve += 2 * FC

        def gpsimd(eng):
            for fc in range(FC):
                eng.wait_ge(sems["dve"], b_dve + 2 * (fc + 1))
                eng.wait_ge(sems["ld" + str(fc % 2)],
                            b_ld[fc % 2] + 16 * (fc // 2 + 1))
                eng.tensor_tensor(tb[:, (fc % 2) * TG:(fc % 2 + 1) * TG],
                                  tb[:, (fc % 2) * TG:(fc % 2 + 1) * TG],
                                  xb[:, (fc % 2) * TG:(fc % 2 + 1) * TG],
                                  ALU.add).then_inc(sems["gpc"], 1)
                eng.wait_ge(sems["gpc"], b_gpc + fc + 1)
                eng.dma_start(out=out_dram[fc * 128:(fc + 1) * 128, g * TG:(g + 1) * TG],
                              in_=tb[:, (fc % 2) * TG:(fc % 2 + 1) * TG]
                              ).then_inc(sems["gp" + str(fc % 2)], 16)
        c.gpc += FC
        c.gp0 += 16 * (FC // 2)
        c.gp1 += 16 * (FC // 2)

        return {"sync": sync, "scalar": scalar, "vector": vector, "gpsimd": gpsimd}

    ctx.phase(body)
    ctx.A.reset(mk)


def emit_post_ln_residual(ctx, x_dram, g, y, rows, out_dram):
    c, sems, small, psum = ctx.c, ctx.sems, ctx.small, ctx.psum
    mk = ctx.A.mark()
    sq = ctx.A.alloc(2 * TG)
    m2B = ctx.A.alloc(TG)
    s2B = ctx.A.alloc(TG)
    m_row = rows[0:1, 0:TG]
    v_row = rows[0:1, TG:2 * TG]
    t_row = rows[0:1, 2 * TG:3 * TG]
    s_row = v_row

    def body(block):
        b_pe, b_act, b_dve = c.pe, c.act, c.dve

        def scalar(eng):
            for fc in range(FC):
                if fc >= 2:
                    eng.wait_ge(sems["pe"], b_pe + 2 * (fc - 1))
                eng.activation(sq[:, (fc % 2) * TG:(fc % 2 + 1) * TG],
                               y[:, fc * TG:(fc + 1) * TG],
                               AF.Square).then_inc(sems["act"], 1)
            eng.wait_ge(sems["dve"], b_dve + 4)
            eng.activation(s_row, v_row, AF.Sqrt, bias=small["c_eps"]
                           ).then_inc(sems["act"], 1)
            eng.wait_ge(sems["pe"], b_pe + 2 * FC + 2)
            eng.activation(m2B, psum[4][:, :TG], AF.Copy)
            eng.activation(s2B, psum[5][:, :TG], AF.Copy).then_inc(sems["act"], 2)
        c.act += FC + 3

        def tensor(eng):
            for fc in range(FC):
                eng.wait_ge(sems["act"], b_act + fc + 1)
                eng.matmul(psum[2][0:1, :TG], small["c_onescol"],
                           y[:, fc * TG:(fc + 1) * TG],
                           start=(fc == 0), stop=(fc == FC - 1))
                eng.matmul(psum[3][0:1, :TG], small["c_onescol"],
                           sq[:, (fc % 2) * TG:(fc % 2 + 1) * TG],
                           start=(fc == 0), stop=(fc == FC - 1)).then_inc(sems["pe"], 2)
            eng.wait_ge(sems["dve"], b_dve + 5)
            eng.matmul(psum[4][:, :TG], small["c_onesrow"], m_row, start=True, stop=True)
            eng.matmul(psum[5][:, :TG], small["c_onesrow"], s_row,
                       start=True, stop=True).then_inc(sems["pe"], 2)
        c.pe += 2 * FC + 2

        def vector(eng):
            eng.wait_ge(sems["pe"], b_pe + 2 * FC)
            _row_math(eng, sems, c, b_dve, b_act, psum, m_row, v_row, t_row, s_row, FC)
        c.dve += 5

        return {"scalar": scalar, "tensor": tensor, "vector": vector}

    ctx.phase(body)

    xb = ctx.A.alloc(2 * TG)
    tb = ctx.A.alloc(2 * TG)

    def body2(block):
        b_ld = [c.ld0, c.ld1]
        b_dve, b_gpc = c.dve, c.gpc

        def sync(eng):
            for fc in range(FC):
                if fc >= 2:
                    eng.wait_ge(sems["gpc"], b_gpc + fc - 1)
                eng.dma_start(out=xb[:, (fc % 2) * TG:(fc % 2 + 1) * TG],
                              in_=x_dram[fc * 128:(fc + 1) * 128, g * TG:(g + 1) * TG]
                              ).then_inc(sems["ld" + str(fc % 2)], 16)
        c.ld0 += 16 * (FC // 2)
        c.ld1 += 16 * (FC // 2)

        def vector(eng):
            for fc in range(FC):
                if fc >= 2:
                    eng.wait_ge(sems["gpc"], b_gpc + fc - 1)
                eng.tensor_tensor(tb[:, (fc % 2) * TG:(fc % 2 + 1) * TG],
                                  y[:, fc * TG:(fc + 1) * TG], m2B, ALU.subtract
                                  ).then_inc(sems["dve"], 1)
                eng.wait_ge(sems["dve"], b_dve + 2 * fc + 1)
                eng.tensor_tensor(tb[:, (fc % 2) * TG:(fc % 2 + 1) * TG],
                                  tb[:, (fc % 2) * TG:(fc % 2 + 1) * TG], s2B,
                                  ALU.mult).then_inc(sems["dve"], 1)
        c.dve += 2 * FC

        def gpsimd(eng):
            for fc in range(FC):
                eng.wait_ge(sems["dve"], b_dve + 2 * (fc + 1))
                eng.wait_ge(sems["ld" + str(fc % 2)],
                            b_ld[fc % 2] + 16 * (fc // 2 + 1))
                eng.tensor_tensor(y[:, fc * TG:(fc + 1) * TG],
                                  tb[:, (fc % 2) * TG:(fc % 2 + 1) * TG],
                                  xb[:, (fc % 2) * TG:(fc % 2 + 1) * TG],
                                  ALU.add).then_inc(sems["gpc"], 1)
                eng.wait_ge(sems["gpc"], b_gpc + fc + 1)
                eng.dma_start(out=out_dram[fc * 128:(fc + 1) * 128, g * TG:(g + 1) * TG],
                              in_=y[:, fc * TG:(fc + 1) * TG]).then_inc(sems["gpd"], 16)
        c.gpc += FC
        c.gpd += 16 * FC

        return {"sync": sync, "vector": vector, "gpsimd": gpsimd}

    ctx.phase(body2)
    ctx.A.reset(mk)


def emit_proj_sweep(ctx, xhat, wt_dram, n_out_chunks, out_cb, bias=None,
                    func=None, extra_dram=None, extra_rhs=None):
    c, sems, psum = ctx.c, ctx.sems, ctx.psum
    mk = ctx.A.mark()
    wcols = FC * 128 + (128 if extra_dram is not None else 0)
    wb = [ctx.A.alloc(wcols) for _ in range(2)]
    NOC = n_out_chunks
    PEO = FC + (1 if extra_dram is not None else 0)
    NDMA = 1 + (1 if extra_dram is not None else 0)

    def body(block):
        b_ld = [c.ld0, c.ld1]
        b_pe, b_act = c.pe, c.act

        def sync(eng):
            for oc in range(NOC):
                if oc >= 2:
                    eng.wait_ge(sems["pe"], b_pe + (oc - 1) * PEO)
                eng.dma_start(
                    out=wb[oc % 2][:, 0:FC * 128]
                        .rearrange("p (kc n) -> p kc n", kc=FC),
                    in_=wt_dram[:, oc * 128:(oc + 1) * 128]
                        .rearrange("(kc p) n -> p kc n", p=128),
                ).then_inc(sems["ld" + str(oc % 2)], 16)
                if extra_dram is not None:
                    eng.dma_start(
                        out=wb[oc % 2][:, FC * 128:FC * 128 + 128],
                        in_=extra_dram[:, oc * 128:(oc + 1) * 128],
                    ).then_inc(sems["ld" + str(oc % 2)], 16)
        c.ld0 += 16 * NDMA * ((NOC + 1) // 2)
        c.ld1 += 16 * NDMA * (NOC // 2)

        def tensor(eng):
            for oc in range(NOC):
                eng.wait_ge(sems["ld" + str(oc % 2)],
                            b_ld[oc % 2] + 16 * NDMA * (oc // 2 + 1))
                if oc >= 2:
                    eng.wait_ge(sems["act"], b_act + oc - 1)
                for kc in range(FC):
                    eng.matmul(psum[oc % 2][:, :TG],
                               wb[oc % 2][:, kc * 128:(kc + 1) * 128],
                               xhat[:, kc * TG:(kc + 1) * TG],
                               start=(kc == 0),
                               stop=(kc == FC - 1 and extra_dram is None)
                               ).then_inc(sems["pe"], 1)
                if extra_dram is not None:
                    eng.matmul(psum[oc % 2][:, :TG],
                               wb[oc % 2][:, FC * 128:FC * 128 + 128],
                               extra_rhs,
                               start=False, stop=True).then_inc(sems["pe"], 1)
        c.pe += NOC * PEO

        def scalar(eng):
            for oc in range(NOC):
                eng.wait_ge(sems["pe"], b_pe + (oc + 1) * PEO)
                f = func.get(oc, AF.Copy) if func else AF.Copy
                kw = {}
                if bias is not None:
                    kw["bias"] = bias[:, oc:oc + 1]
                    f = AF.Identity if f == AF.Copy else f
                eng.activation(out_cb(oc), psum[oc % 2][:, :TG], f, **kw
                               ).then_inc(sems["act"], 1)
        c.act += NOC

        return {"sync": sync, "tensor": tensor, "scalar": scalar}

    ctx.phase(body)
    ctx.A.reset(mk)


def emit_l0a_prep(ctx):
    c, sems, small, psum = ctx.c, ctx.sems, ctx.small, ctx.psum
    io, dr = ctx.io, ctx.dr

    bm01 = ctx.A.alloc(128)
    mk = ctx.A.mark()
    o32a = ctx.A.alloc(FC * 128)
    mtile = ctx.A.alloc(2 * 512)
    mtile_f = ctx.A.alloc(2 * 512)
    t1 = ctx.A.alloc(512)
    tf = ctx.A.alloc(256 + 128 + 64 + 32)

    def body(block):
        b_ld = [c.ld0, c.ld1]
        b_lda = c.lda
        b_pe, b_dve, b_act = c.pe, c.dve, c.act

        def sync(eng):
            eng.dma_start(out=o32a, in_=io["c_o32a"]).then_inc(sems["lda"], 16)
            i = 0
            for kb in range(8):
                for ncc in range(FC):
                    if i >= 2:
                        j = i - 2
                        eng.wait_ge(sems["dve"], b_dve + (j + 1) + 6 * (j // FC))
                    eng.dma_start(
                        out=mtile[:, (i % 2) * 512:(i % 2 + 1) * 512].bitcast(I32),
                        in_=io["w_mask"][ncc * 128:(ncc + 1) * 128,
                                         kb * 512:(kb + 1) * 512],
                    ).then_inc(sems["ld" + str(i % 2)], 16)
                    i += 1
        c.lda += 16
        c.ld0 += 16 * (8 * FC // 2)
        c.ld1 += 16 * (8 * FC // 2)

        def vector(eng):
            i = 0
            for kb in range(8):
                for ncc in range(FC):
                    eng.wait_ge(sems["ld" + str(i % 2)],
                                b_ld[i % 2] + 16 * (i // 2 + 1))
                    if i >= 2:
                        eng.wait_ge(sems["pe"], b_pe + i - 1)
                    eng.tensor_copy(mtile_f[:, (i % 2) * 512:(i % 2 + 1) * 512],
                                    mtile[:, (i % 2) * 512:(i % 2 + 1) * 512]
                                    .bitcast(I32)).then_inc(sems["dve"], 1)
                    i += 1
                eng.wait_ge(sems["act"], b_act + kb + 1)
                dvn = b_dve + (kb + 1) * FC + 6 * kb
                srcs = [t1, tf[:, 0:256], tf[:, 256:384], tf[:, 384:448],
                        tf[:, 448:480]]
                for st in range(5):
                    h = 16 >> st
                    a3 = srcs[st].rearrange("p (k s) -> p k s", s=2 * h)
                    if st > 0:
                        eng.wait_ge(sems["dve"], dvn)
                    if st < 4:
                        d3 = srcs[st + 1].rearrange("p (k s) -> p k s", s=h)
                        eng.tensor_tensor(d3, a3[:, :, 0:h], a3[:, :, h:2 * h],
                                          ALU.add).then_inc(sems["dve"], 1)
                    else:
                        eng.tensor_tensor(bm01[:, kb * 16:(kb + 1) * 16],
                                          a3[:, :, 0:1].squeeze(2),
                                          a3[:, :, 1:2].squeeze(2), ALU.add
                                          ).then_inc(sems["dve"], 1)
                    dvn += 1
                eng.wait_ge(sems["dve"], dvn)
                eng.tensor_scalar(bm01[:, kb * 16:(kb + 1) * 16],
                                  bm01[:, kb * 16:(kb + 1) * 16],
                                  0.0, None, ALU.is_gt).then_inc(sems["dve"], 1)
        c.dve += 8 * FC + 8 * 6

        def tensor(eng):
            eng.wait_ge(sems["lda"], b_lda + 16)
            i = 0
            for kb in range(8):
                if kb >= 1:
                    eng.wait_ge(sems["act"], b_act + kb)
                for ncc in range(FC):
                    eng.wait_ge(sems["dve"], b_dve + i + 1 + 6 * kb)
                    eng.matmul(psum[2][:, :], o32a[:, ncc * 128:(ncc + 1) * 128],
                               mtile_f[:, (i % 2) * 512:(i % 2 + 1) * 512],
                               start=(ncc == 0), stop=(ncc == FC - 1)
                               ).then_inc(sems["pe"], 1)
                    i += 1
        c.pe += 8 * FC

        def scalar(eng):
            for kb in range(8):
                eng.wait_ge(sems["pe"], b_pe + (kb + 1) * FC)
                if kb >= 1:
                    eng.wait_ge(sems["dve"], b_dve + 38 * kb)
                eng.activation(t1, psum[2][:, :], AF.Copy).then_inc(sems["act"], 1)
        c.act += 8

        return {"sync": sync, "vector": vector, "tensor": tensor, "scalar": scalar}

    ctx.phase(body)
    ctx.A.reset(mk)
    if DEBUG_TAP == "bm01":
        emit_tap_sbuf(ctx, bm01)
        return
    if DEBUG_STOP == "bm":
        return

    mk = ctx.A.mark()
    rsel = ctx.A.alloc(FC * 128)
    xex = ctx.A.alloc(D)
    wc = [ctx.A.alloc(D) for _ in range(2)]
    e1t = ctx.A.alloc(128)
    bmE = ctx.A.alloc(D)
    wm = ctx.A.alloc(D)
    tb = [ctx.A.alloc(512) for _ in range(2)]

    def body2(block):
        b_ld = [c.ld0, c.ld1]
        b_lda = c.lda
        b_gp = [c.gp0, c.gp1]
        b_pe, b_dve, b_act = c.pe, c.dve, c.act

        def sync(eng):
            eng.dma_start(out=rsel, in_=io["c_rsel"]).then_inc(sems["lda"], 16)
            eng.dma_start(out=xex, in_=io["c_xex"]).then_inc(sems["lda"], 16)
            for ncc in range(FC):
                if ncc >= 2:
                    eng.wait_ge(sems["dve"], b_dve + ncc - 1)
                eng.dma_start(out=wc[ncc % 2],
                              in_=io["w_weight"][ncc * 128:(ncc + 1) * 128, :]
                              ).then_inc(sems["ld" + str(ncc % 2)], 16)
        c.lda += 32
        c.ld0 += 16 * (FC // 2)
        c.ld1 += 16 * (FC // 2)

        def tensor(eng):
            eng.wait_ge(sems["lda"], b_lda + 32)
            for ncc in range(FC):
                pe0 = b_pe + ncc * 41
                act0 = b_act + ncc * 17
                eng.wait_ge(sems["ld" + str(ncc % 2)],
                            b_ld[ncc % 2] + 16 * (ncc // 2 + 1))
                if ncc >= 1:
                    eng.wait_ge(sems["act"], act0)
                eng.matmul(psum[2][:, 0:128], bm01,
                           rsel[:, ncc * 128:(ncc + 1) * 128],
                           start=True, stop=True).then_inc(sems["pe"], 1)
                for kb in range(8):
                    eng.wait_ge(sems["act"], act0 + 1)
                    if kb >= 2:
                        eng.wait_ge(sems["act"], act0 + kb)
                    eng.matmul(psum[3 + kb % 2][:, :], e1t,
                               xex[:, kb * 512:(kb + 1) * 512],
                               start=True, stop=True).then_inc(sems["pe"], 1)
                eng.wait_ge(sems["dve"], b_dve + ncc + 1)
                for sub in range(FC):
                    g = sub // 4
                    if g >= 2 and sub % 4 == 0:
                        eng.wait_ge(sems["act"], act0 + 9 + g - 1)
                    eng.transpose(psum[g % 2][:, (sub % 4) * 128:(sub % 4 + 1) * 128],
                                  wm[:, sub * 128:(sub + 1) * 128],
                                  small["c_ident"]).then_inc(sems["pe"], 1)
        c.pe += FC * 41

        def scalar(eng):
            for ncc in range(FC):
                pe0 = b_pe + ncc * 41
                eng.wait_ge(sems["pe"], pe0 + 1)
                eng.activation(e1t, psum[2][:, 0:128], AF.Copy).then_inc(sems["act"], 1)
                for kb in range(8):
                    eng.wait_ge(sems["pe"], pe0 + 1 + kb + 1)
                    eng.activation(bmE[:, kb * 512:(kb + 1) * 512],
                                   psum[3 + kb % 2][:, :], AF.Copy
                                   ).then_inc(sems["act"], 1)
                for g in range(8):
                    eng.wait_ge(sems["pe"], pe0 + 9 + (g + 1) * 4)
                    sg = ncc * 8 + g
                    if sg >= 2:
                        eng.wait_ge(sems["gp" + str(sg % 2)],
                                    b_gp[sg % 2] + 64 * (sg // 2))
                    eng.activation(tb[g % 2], psum[g % 2][:, :], AF.Copy
                                   ).then_inc(sems["act"], 1)
        c.act += FC * 17

        def vector(eng):
            for ncc in range(FC):
                act0 = b_act + ncc * 17
                eng.wait_ge(sems["act"], act0 + 9)
                if ncc >= 1:
                    eng.wait_ge(sems["pe"], b_pe + ncc * 41)
                eng.tensor_tensor(wm, wc[ncc % 2], bmE, ALU.mult).then_inc(sems["dve"], 1)
        c.dve += FC

        def gpsimd(eng):
            for ncc in range(FC):
                act0 = b_act + ncc * 17
                for g in range(8):
                    sg = ncc * 8 + g
                    eng.wait_ge(sems["act"], act0 + 9 + g + 1)
                    for q in range(4):
                        sub = g * 4 + q
                        eng.dma_start(
                            out=dr["wtdram"][sub * 128:(sub + 1) * 128,
                                             ncc * 128:(ncc + 1) * 128],
                            in_=tb[g % 2][:, q * 128:(q + 1) * 128],
                        ).then_inc(sems["gp" + str(sg % 2)], 16)
        c.gp0 += 64 * (FC * 8 // 2)
        c.gp1 += 64 * (FC * 8 // 2)

        return {"sync": sync, "tensor": tensor, "scalar": scalar,
                "vector": vector, "gpsimd": gpsimd}

    ctx.phase(body2)
    ctx.A.reset(mk)
    if DEBUG_TAP == "wt":
        emit_tap_dram(ctx, dr["wtdram"], D, TOK)


def emit_l0a(ctx):
    c = ctx.c
    io, dr, sems = ctx.io, ctx.dr, ctx.sems
    mk_pre = ctx.A.mark()
    emit_l0a_prep(ctx)
    ctx.A.reset(mk_pre)
    if DEBUG_STOP in ("prep", "bm") or DEBUG_TAP in ("bm01", "wt"):
        return

    mk0 = ctx.A.mark()
    rows = ctx.A.alloc(3 * TG)[0:1, :]
    biasb = ctx.A.alloc(FC)
    mB = ctx.A.alloc(TG)
    sB = ctx.A.alloc(TG)
    xhat = ctx.A.alloc(FC * TG)
    y = ctx.A.alloc(FC * TG)

    def body_bias(block):
        def sync(eng):
            for fcb in range(FC):
                eng.dma_start(out=biasb[:, fcb:fcb + 1],
                              in_=io["w_bias"][fcb * 128:(fcb + 1) * 128, :]
                              ).then_inc(sems["lda"], 16)
        c.lda += 16 * FC
        return {"sync": sync}
    ctx.phase(body_bias)

    for g in range(NG):
        emit_ln_stats_dram(ctx, dr["xdram"], g, mB, sB, rows)
        emit_ln_apply_dram(ctx, dr["xdram"], g, mB, sB, xhat)
        emit_proj_sweep(ctx, xhat, dr["wtdram"], FC,
                        lambda oc: y[:, oc * TG:(oc + 1) * TG], bias=biasb)
        if DEBUG_TAP == "y" and g == 0:
            for fc in range(FC):
                emit_tap_sbuf(ctx, y[:, fc * TG:(fc + 1) * TG], r0=fc * 128)
            return
        emit_post_ln_residual(ctx, dr["xdram"], g, y, rows, dr["xdram"])
    ctx.A.reset(mk0)


def emit_l0f(ctx):
    c = ctx.c
    io, dr, sems, small, psum = ctx.io, ctx.dr, ctx.sems, ctx.small, ctx.psum

    mk0 = ctx.A.mark()
    rows = ctx.A.alloc(3 * TG)[0:1, :]
    mB = ctx.A.alloc(TG)
    sB = ctx.A.alloc(TG)
    idxb = ctx.A.alloc(FC)
    xhat = ctx.A.alloc(FC * TG)

    def body_idx(block):
        def sync(eng):
            for ncc in range(FC):
                eng.dma_start(out=idxb[:, ncc:ncc + 1].bitcast(I32),
                              in_=io["w_perm"][ncc * 128:(ncc + 1) * 128, :]
                              ).then_inc(sems["lda"], 16)
        c.lda += 16 * FC
        return {"sync": sync}
    ctx.phase(body_idx)

    for g in range(NG):
        emit_ln_stats_dram(ctx, dr["xdram"], g, mB, sB, rows)
        emit_ln_apply_dram(ctx, dr["xdram"], g, mB, sB, xhat,
                           store_dram=GroupedDram(dr["sdram"], g))
        _emit_l0f_heads(ctx, g, idxb)
        emit_ln_stats_dram(ctx, GroupedDram(dr["s2dram"], g), g, mB, sB, rows)
        emit_residual_from_dram(ctx, dr["xdram"], GroupedDram(dr["s2dram"], g),
                                g, mB, sB, dr["xdram"])
    ctx.A.reset(mk0)


def _emit_l0f_heads(ctx, g, idxb):
    c, sems, small, psum = ctx.c, ctx.sems, ctx.small, ctx.psum
    io, dr = ctx.io, ctx.dr
    mk = ctx.A.mark()
    xg = [ctx.A.alloc(2 * TG) for _ in range(2)]
    hid = ctx.A.alloc(8 * TG)
    yh = [ctx.A.alloc(2 * TG) for _ in range(2)]
    wub = [ctx.A.alloc(2 * FM) for _ in range(2)]
    wdb = [ctx.A.alloc(8 * FD) for _ in range(2)]

    def body(block):
        b_ld = [c.ld0, c.ld1]
        b_sc = [c.sc0, c.sc1]
        b_gp = [c.gp0, c.gp1, c.gp2, c.gp3]
        b_pe, b_act, b_gpc = c.pe, c.act, c.gpc

        def sync(eng):
            for h in range(FH):
                if h >= 2:
                    eng.wait_ge(sems["pe"], b_pe + (h - 1) * 32)
                eng.dma_start(
                    out=wub[h % 2].rearrange("p (kc m) -> p kc m", kc=2),
                    in_=io["w_up"][h * 256:(h + 1) * 256, :]
                        .rearrange("(kc p) m -> p kc m", p=128),
                ).then_inc(sems["ld" + str(h % 2)], 16)
        c.ld0 += 16 * (FH // 2)
        c.ld1 += 16 * (FH // 2)

        def scalar_dma_and_act(eng):
            for h in range(FH):
                if h >= 2:
                    eng.wait_ge(sems["pe"], b_pe + h * 32)
                eng.dma_start(
                    out=wdb[h % 2].rearrange("p (mc d) -> p mc d", mc=8),
                    in_=io["w_down"][h * 1024:(h + 1) * 1024, :]
                        .rearrange("(mc p) d -> p mc d", p=128),
                ).then_inc(sems["sc" + str(h % 2)], 16)
                pe0 = b_pe + h * 32
                act0 = b_act + h * 10
                for mo in range(8):
                    eng.wait_ge(sems["pe"], pe0 + 2 * (mo + 1))
                    eng.activation(hid[:, mo * TG:(mo + 1) * TG],
                                   psum[mo % 2][:, :TG], AF.Gelu
                                   ).then_inc(sems["act"], 1)
                for oc in range(2):
                    eng.wait_ge(sems["pe"], pe0 + 16 + 8 * (oc + 1))
                    if h >= 2 and oc == 0:
                        eng.wait_ge(sems["gp" + str(2 + h % 2)],
                                    b_gp[2 + h % 2] + 32 * (h // 2))
                    eng.activation(yh[h % 2][:, oc * TG:(oc + 1) * TG],
                                   psum[2 + oc][:, :TG], AF.Copy
                                   ).then_inc(sems["act"], 1)
        c.sc0 += 16 * (FH // 2)
        c.sc1 += 16 * (FH // 2)
        c.act += FH * 10

        def tensor(eng):
            for h in range(FH):
                pe0 = b_pe + h * 32
                act0 = b_act + h * 10
                eng.wait_ge(sems["ld" + str(h % 2)],
                            b_ld[h % 2] + 16 * (h // 2 + 1))
                eng.wait_ge(sems["gp" + str(h % 2)],
                            b_gp[h % 2] + 32 * (h // 2) + 32)
                if h >= 1:
                    eng.wait_ge(sems["act"], act0)
                for mo in range(8):
                    if mo >= 2:
                        eng.wait_ge(sems["act"], act0 + mo - 1)
                    for kc in range(2):
                        eng.matmul(psum[mo % 2][:, :TG],
                                   wub[h % 2][:, (kc * 8 + mo) * 128:
                                              (kc * 8 + mo + 1) * 128],
                                   xg[h % 2][:, kc * TG:(kc + 1) * TG],
                                   start=(kc == 0), stop=(kc == 1)
                                   ).then_inc(sems["pe"], 1)
                eng.wait_ge(sems["sc" + str(h % 2)],
                            b_sc[h % 2] + 16 * (h // 2 + 1))
                eng.wait_ge(sems["act"], act0 + 8)
                for oc in range(2):
                    for mc in range(8):
                        eng.matmul(psum[2 + oc][:, :TG],
                                   wdb[h % 2][:, (mc * 2 + oc) * 128:
                                              (mc * 2 + oc + 1) * 128],
                                   hid[:, mc * TG:(mc + 1) * TG],
                                   start=(mc == 0), stop=(mc == 7)
                                   ).then_inc(sems["pe"], 1)
        c.pe += FH * 32

        def gpsimd(eng):
            for h in range(FH):
                act0 = b_act + h * 10
                if h >= 2:
                    eng.wait_ge(sems["pe"], b_pe + (h - 1) * 32)
                for kc in range(2):
                    eng.indirect_dma_start(
                        out=xg[h % 2][:, kc * TG:(kc + 1) * TG],
                        out_offset=None,
                        in_=dr["sdram"],
                        in_offset=bass.IndirectOffsetOnAxis(
                            idxb[:, 2 * h + kc:2 * h + kc + 1].bitcast(I32), 0),
                        element_offset=g * D * TG,
                    ).then_inc(sems["gp" + str(h % 2)], 16)
                eng.wait_ge(sems["act"], act0 + 10)
                for oc in range(2):
                    eng.indirect_dma_start(
                        out=dr["s2dram"],
                        out_offset=bass.IndirectOffsetOnAxis(
                            idxb[:, 2 * h + oc:2 * h + oc + 1].bitcast(I32), 0),
                        in_=yh[h % 2][:, oc * TG:(oc + 1) * TG],
                        in_offset=None,
                        element_offset=g * D * TG,
                    ).then_inc(sems["gp" + str(2 + h % 2)], 16)
        c.gp0 += 32 * (FH // 2)
        c.gp1 += 32 * (FH // 2)
        c.gp2 += 32 * (FH // 2)
        c.gp3 += 32 * (FH // 2)

        return {"sync": sync, "scalar": scalar_dma_and_act, "tensor": tensor,
                "gpsimd": gpsimd}

    ctx.phase(body)
    ctx.A.reset(mk)


def emit_load_arena(ctx, src_gview, g, dst):
    c, sems = ctx.c, ctx.sems

    def body(block):
        b_ld = [c.ld0, c.ld1]

        def sync(eng):
            for fc in range(FC):
                eng.dma_start(out=dst[:, fc * TG:(fc + 1) * TG],
                              in_=src_gview[fc * 128:(fc + 1) * 128,
                                            g * TG:(g + 1) * TG]
                              ).then_inc(sems["ld" + str(fc % 2)], 16)
        c.ld0 += 16 * (FC // 2)
        c.ld1 += 16 * (FC // 2)
        return {"sync": sync}
    ctx.phase(body)


def emit_l1a(ctx):
    c = ctx.c
    io, dr, sems, small, psum = ctx.io, ctx.dr, ctx.sems, ctx.small, ctx.psum

    mk0 = ctx.A.mark()
    rows = ctx.A.alloc(3 * TG)[0:1, :]
    mB = ctx.A.alloc(TG)
    sB = ctx.A.alloc(TG)
    xhat = ctx.A.alloc(FC * TG)
    mk_attn = ctx.A.mark()
    ropec = ctx.A.alloc(TOK)
    ropes = ctx.A.alloc(TOK)
    pswap = ctx.A.alloc(128)
    kposb = ctx.A.alloc(FC)
    qposb = ctx.A.alloc(TOK)[0:1, :]
    qn2 = ctx.A.alloc(TOK)[0:1, :]
    kn2 = ctx.A.alloc(8)[0:1, :]
    kfm = ctx.A.alloc(4)[0:1, 0:1]

    def body_pre(block):
        def sync(eng):
            eng.dma_start(out=ropec, in_=io["c_ropec"]).then_inc(sems["lda"], 16)
            eng.dma_start(out=ropes, in_=io["c_ropes"]).then_inc(sems["lda"], 16)
            eng.dma_start(out=pswap, in_=io["c_pswap"]).then_inc(sems["lda"], 16)
            eng.dma_start(out=qposb, in_=io["c_qpos"]).then_inc(sems["lda"], 16)
            for fcb in range(FC):
                eng.dma_start(out=kposb[:, fcb:fcb + 1],
                              in_=io["c_kpos"][fcb * 128:(fcb + 1) * 128, :]
                              ).then_inc(sems["lda"], 16)
        c.lda += 16 * (4 + FC)

        def vector(eng):
            eng.memset(kn2, -1.0e30).then_inc(sems["dve"], 1)
        c.dve += 1
        return {"sync": sync, "vector": vector}
    ctx.phase(body_pre)

    for g in range(NG):
        emit_ln_stats_dram(ctx, dr["xdram"], g, mB, sB, rows)
        emit_ln_apply_dram(ctx, dr["xdram"], g, mB, sB, xhat)
        _emit_l1a_qkv(ctx, g, xhat, ropec, ropes, pswap, qn2, kn2)

    _emit_l1a_collective(ctx, kn2, kfm)

    cmask = ctx.AB.ap[:, 0:FC * TG]
    for g in range(NG):
        _emit_l1a_masks(ctx, g, qposb, kposb, qn2, kfm, cmask)
        for kvh in range(NKV):
            _emit_l1a_attn(ctx, g, kvh, cmask)

    ctx.A.reset(mk_attn)
    ahat = xhat
    y = ctx.A.alloc(FC * TG)
    for g in range(NG):
        emit_load_arena(ctx, GroupedDram(dr["s2dram"], g), g, ahat)
        emit_proj_sweep(ctx, ahat, io["w_wo"], FC,
                        lambda oc: y[:, oc * TG:(oc + 1) * TG])
        emit_post_ln_residual(ctx, dr["xdram"], g, y, rows, dr["xdram"])
    ctx.A.reset(mk0)


def _emit_l1a_qkv(ctx, g, xhat, ropec, ropes, pswap, qn2, kn2):
    c, sems, small, psum = ctx.c, ctx.sems, ctx.small, ctx.psum
    io, dr = ctx.io, ctx.dr
    mk = ctx.A.mark()
    wb = [ctx.A.alloc(FC * 128) for _ in range(2)]
    t0 = ctx.A.alloc(TG)
    t1 = ctx.A.alloc(TG)
    t2 = ctx.A.alloc(TG)
    tf = [ctx.A.alloc(TG) for _ in range(2)]
    sq = ctx.A.alloc(TG)
    nrowb = ctx.A.alloc(2 * TG)[0:1, :]
    nredb = ctx.A.alloc(4)[0:1, :]
    vt = [ctx.A.alloc(TG) for _ in range(2)]
    Cg = ropec[:, g * TG:(g + 1) * TG]
    Sg = ropes[:, g * TG:(g + 1) * TG]
    vv = dr["ccin"].rearrange("r (h c) -> (r h) c", h=2)

    items = ([("k", i) for i in range(NKV)] + [("v", i) for i in range(NKV)]
             + [("q", i) for i in range(FC)])
    def pe_ops(kind):
        return 32 + (2 if kind in ("q", "k") else 4)

    def act_ops(kind):
        return 4 if kind in ("q", "k") else 2

    def dve_ops(kind, it):
        if kind == "k":
            return 5
        if kind == "q":
            return 4
        return 0

    pe_bases = np.cumsum([0] + [pe_ops(k) for k, _ in items]).tolist()
    act_bases = np.cumsum([0] + [act_ops(k) for k, _ in items]).tolist()
    dve_bases = np.cumsum([0] + [dve_ops(k, i) for k, i in items]).tolist()
    st_pref = []
    tot = {0: 0, 1: 0, 2: 0, 3: 0}
    st_sem = []
    for i, (kind, it) in enumerate(items):
        s = (i % 2) if kind in ("q", "k") else (2 + i % 2)
        st_pref.append(tot[s])
        st_sem.append(s)
        tot[s] += 16 if kind in ("q", "k") else 64

    def wsrc(kind, it):
        w = {"q": io["w_wq"], "k": io["w_wk"], "v": io["w_wv"]}[kind]
        return w[:, it * 128:(it + 1) * 128]

    def body(block):
        b_ld = [c.ld0, c.ld1]
        b_gp = [c.gp0, c.gp1, c.gp2, c.gp3]
        b_pe, b_act, b_dve = c.pe, c.act, c.dve

        def sync(eng):
            for i, (kind, it) in enumerate(items):
                if i >= 2:
                    eng.wait_ge(sems["pe"], b_pe + pe_bases[i - 1])
                eng.dma_start(
                    out=wb[i % 2].rearrange("p (kc n) -> p kc n", kc=FC),
                    in_=wsrc(kind, it).rearrange("(kc p) n -> p kc n", p=128),
                ).then_inc(sems["ld" + str(i % 2)], 16)
        c.ld0 += 16 * ((len(items) + 1) // 2)
        c.ld1 += 16 * (len(items) // 2)

        def tensor(eng):
            for i, (kind, it) in enumerate(items):
                pe0 = b_pe + pe_bases[i]
                act0 = b_act + act_bases[i]
                eng.wait_ge(sems["ld" + str(i % 2)],
                            b_ld[i % 2] + 16 * (i // 2 + 1))
                if i >= 2:
                    eng.wait_ge(sems["act"], b_act + act_bases[i - 1] + 1)
                for kc in range(FC):
                    eng.matmul(psum[i % 2][:, :TG],
                               wb[i % 2][:, kc * 128:(kc + 1) * 128],
                               xhat[:, kc * TG:(kc + 1) * TG],
                               start=(kc == 0), stop=(kc == FC - 1)
                               ).then_inc(sems["pe"], 1)
                if kind in ("q", "k"):
                    eng.wait_ge(sems["act"], act0 + 1)
                    if i >= 2:
                        eng.wait_ge(sems["act"], b_act + act_bases[i - 1] + 2)
                    eng.matmul(psum[2 + i % 2][:, :TG], pswap, t0,
                               start=True, stop=True).then_inc(sems["pe"], 1)
                    eng.wait_ge(sems["act"], act0 + 3)
                    eng.matmul(psum[4][0:1, :TG], small["c_onescol"], sq,
                               start=True, stop=True).then_inc(sems["pe"], 1)
                else:
                    eng.wait_ge(sems["act"], act0 + 1)
                    if i >= 2:
                        eng.wait_ge(sems["act"], b_act + act_bases[i - 1] + 2)
                    for q4 in range(4):
                        eng.transpose(psum[5][:, q4 * 128:(q4 + 1) * 128],
                                      t0[:, q4 * 128:(q4 + 1) * 128],
                                      small["c_ident"]).then_inc(sems["pe"], 1)
        c.pe += pe_bases[-1]

        def scalar(eng):
            for i, (kind, it) in enumerate(items):
                pe0 = b_pe + pe_bases[i]
                act0 = b_act + act_bases[i]
                dve0 = b_dve + dve_bases[i]
                eng.wait_ge(sems["pe"], pe0 + FC)
                if i >= 1:
                    eng.wait_ge(sems["dve"], b_dve + dve_bases[i])
                if kind == "q":
                    eng.activation(t0, psum[i % 2][:, :TG], AF.Identity,
                                   scale=small["c_rsqhd"]).then_inc(sems["act"], 1)
                else:
                    eng.activation(t0, psum[i % 2][:, :TG], AF.Copy
                                   ).then_inc(sems["act"], 1)
                if kind in ("q", "k"):
                    eng.wait_ge(sems["pe"], pe0 + FC + 1)
                    eng.activation(t1, psum[2 + i % 2][:, :TG], AF.Copy
                                   ).then_inc(sems["act"], 1)
                    eng.activation(sq, t0, AF.Square).then_inc(sems["act"], 1)
                    eng.wait_ge(sems["pe"], pe0 + FC + 2)
                    eng.activation(nrowb[0:1, (i % 2) * TG:(i % 2) * TG + TG],
                                   psum[4][0:1, :TG], AF.Copy
                                   ).then_inc(sems["act"], 1)
                else:
                    eng.wait_ge(sems["pe"], pe0 + FC + 4)
                    if i >= 2 and items[i - 2][0] == "v":
                        eng.wait_ge(sems["gp" + str(st_sem[i])],
                                    b_gp[st_sem[i]] + st_pref[i - 2] + 64)
                    eng.activation(vt[i % 2], psum[5][:, :TG], AF.Copy
                                   ).then_inc(sems["act"], 1)
        c.act += act_bases[-1]

        def vector(eng):
            for i, (kind, it) in enumerate(items):
                act0 = b_act + act_bases[i]
                dvn = b_dve + dve_bases[i]
                if kind not in ("q", "k"):
                    continue
                nsl = nrowb[0:1, (i % 2) * TG:(i % 2) * TG + TG]
                nred = nredb[0:1, (i % 2):(i % 2) + 1]
                if dvn > b_dve:
                    eng.wait_ge(sems["dve"], dvn)
                eng.wait_ge(sems["act"], act0 + 2)
                if st_pref[i] > 0:
                    eng.wait_ge(sems["gp" + str(st_sem[i])],
                                b_gp[st_sem[i]] + st_pref[i])
                eng.tensor_tensor(t2, t0, Cg, ALU.mult).then_inc(sems["dve"], 1)
                dvn += 1
                eng.tensor_tensor(tf[i % 2], t1, Sg, ALU.mult).then_inc(sems["dve"], 1)
                dvn += 1
                eng.wait_ge(sems["dve"], dvn)
                eng.tensor_tensor(tf[i % 2], tf[i % 2], t2, ALU.add
                                  ).then_inc(sems["dve"], 1)
                dvn += 1
                eng.wait_ge(sems["act"], act0 + 4)
                if kind == "k":
                    eng.tensor_reduce(nred, nsl, mybir.AxisListType.X, ALU.max
                                      ).then_inc(sems["dve"], 1)
                    dvn += 1
                    eng.wait_ge(sems["dve"], dvn)
                    eng.tensor_tensor(kn2[0:1, it:it + 1], kn2[0:1, it:it + 1],
                                      nred, ALU.max).then_inc(sems["dve"], 1)
                    dvn += 1
                else:
                    if it == 0:
                        eng.tensor_copy(qn2[0:1, g * TG:(g + 1) * TG], nsl
                                        ).then_inc(sems["dve"], 1)
                    else:
                        eng.tensor_tensor(qn2[0:1, g * TG:(g + 1) * TG],
                                          qn2[0:1, g * TG:(g + 1) * TG], nsl,
                                          ALU.max).then_inc(sems["dve"], 1)
                    dvn += 1
        c.dve += dve_bases[-1]

        def gpsimd(eng):
            for i, (kind, it) in enumerate(items):
                act0 = b_act + act_bases[i]
                dve0 = b_dve + dve_bases[i]
                if kind == "q":
                    eng.wait_ge(sems["dve"], dve0 + 3)
                    eng.dma_start(out=dr["qdram"][it * 128:(it + 1) * 128,
                                                  g * TG:(g + 1) * TG],
                                  in_=tf[i % 2]).then_inc(sems["gp" + str(i % 2)], 16)
                elif kind == "k":
                    eng.wait_ge(sems["dve"], dve0 + 3)
                    eng.dma_start(out=dr["ccin"][it * 128:(it + 1) * 128,
                                                 g * TG:(g + 1) * TG],
                                  in_=tf[i % 2]).then_inc(sems["gp" + str(i % 2)], 16)
                else:
                    eng.wait_ge(sems["act"], act0 + 2)
                    for q4 in range(4):
                        vrow = 2 * 1024 + g * TG + q4 * 128
                        eng.dma_start(
                            out=vv[vrow:vrow + 128, it * 128:(it + 1) * 128],
                            in_=vt[i % 2][:, q4 * 128:(q4 + 1) * 128],
                        ).then_inc(sems["gp" + str(2 + i % 2)], 16)
        c.gp0 += tot[0]
        c.gp1 += tot[1]
        c.gp2 += tot[2]
        c.gp3 += tot[3]

        return {"sync": sync, "tensor": tensor, "scalar": scalar,
                "vector": vector, "gpsimd": gpsimd}

    ctx.phase(body)
    ctx.A.reset(mk)


def _emit_l1a_collective(ctx, kn2, kfm):
    c, sems = ctx.c, ctx.sems
    io, dr = ctx.io, ctx.dr
    mk = ctx.A.mark()
    knr = ctx.A.alloc(16)[0:1, :]
    zrowb = ctx.A.alloc(TOK)[0:1, :]

    def body_cc(block):
        b_gpd, b_cc = c.gpd, c.cc
        b_gpc = c.gpc

        def gpsimd(eng):
            eng.memset(zrowb, 0.0).then_inc(sems["gpc"], 1)
            eng.wait_ge(sems["gpc"], b_gpc + 1)
            eng.dma_start(out=dr["ccin"][2048:2049, 8:TOK], in_=zrowb[0:1, 8:TOK]
                          ).then_inc(sems["gpd"], 16)
            eng.dma_start(out=dr["ccin"][2048:2049, 0:8], in_=kn2
                          ).then_inc(sems["gpd"], 16)
            eng.wait_ge(sems["gpd"], b_gpd + 32)
            eng.collective_compute(
                "AllGather", ALU.bypass, replica_groups=CC_GROUPS,
                ins=[dr["ccin"]], outs=[dr["ccout"]],
            ).then_inc(sems["cc"], 1)
            eng.wait_ge(sems["cc"], b_cc + 1)
        c.gpc += 1
        c.gpd += 32
        c.cc += 1
        return {"gpsimd": gpsimd}
    ctx.phase(body_cc)

    def body_fin(block):
        b_lda = c.lda
        b_dve = c.dve
        b_act = c.act

        def sync(eng):
            eng.dma_start(out=knr[0:1, 0:8], in_=dr["ccout"][2048:2049, 0:8]
                          ).then_inc(sems["lda"], 16)
            eng.dma_start(out=knr[0:1, 8:16],
                          in_=dr["ccout"][2049 + 2048:2049 + 2049, 0:8]
                          ).then_inc(sems["lda"], 16)
        c.lda += 32

        def vector(eng):
            eng.wait_ge(sems["lda"], b_lda + 32)
            eng.tensor_tensor(knr[0:1, 0:8], knr[0:1, 0:8], knr[0:1, 8:16],
                              ALU.max).then_inc(sems["dve"], 1)
            eng.wait_ge(sems["dve"], b_dve + 1)
            eng.tensor_reduce(kfm, knr[0:1, 0:8], mybir.AxisListType.X,
                              ALU.max).then_inc(sems["dve"], 1)
        c.dve += 2

        def scalar(eng):
            eng.wait_ge(sems["dve"], b_dve + 2)
            eng.activation(kfm, kfm, AF.Sqrt).then_inc(sems["act"], 1)
        c.act += 1
        return {"sync": sync, "vector": vector, "scalar": scalar}
    ctx.phase(body_fin)
    ctx.A.reset(mk)


def _emit_l1a_masks(ctx, g, qposb, kposb, qn2, kfm, cmask):
    c, sems, small, psum = ctx.c, ctx.sems, ctx.small, ctx.psum
    mk = ctx.A.mark()
    qpb = ctx.A.alloc(TG)
    cB = ctx.A.alloc(TG)
    crow = ctx.A.alloc(TG)[0:1, :]
    m01 = ctx.A.alloc(TG)

    def body(block):
        b_pe, b_act, b_dve = c.pe, c.act, c.dve

        def scalar(eng):
            eng.activation(crow, qn2[0:1, g * TG:(g + 1) * TG], AF.Sqrt
                           ).then_inc(sems["act"], 1)
            eng.wait_ge(sems["pe"], b_pe + 2)
            eng.activation(qpb, psum[4][:, :TG], AF.Copy)
            eng.activation(cB, psum[5][:, :TG], AF.Copy).then_inc(sems["act"], 2)
        c.act += 3

        def vector(eng):
            eng.wait_ge(sems["act"], b_act + 1)
            eng.tensor_scalar(crow, crow, kfm, None, ALU.mult).then_inc(sems["dve"], 1)
            dvn = b_dve + 1
            eng.wait_ge(sems["act"], b_act + 3)
            for kc in range(FC):
                eng.wait_ge(sems["dve"], dvn)
                eng.tensor_scalar(m01, qpb, kposb[:, kc:kc + 1], None, ALU.is_ge
                                  ).then_inc(sems["dve"], 1)
                dvn += 1
                eng.wait_ge(sems["dve"], dvn)
                eng.tensor_scalar(m01, m01, 1.0, -NBIG, ALU.subtract, ALU.mult
                                  ).then_inc(sems["dve"], 1)
                dvn += 1
                eng.wait_ge(sems["dve"], dvn)
                eng.tensor_tensor(cmask[:, kc * TG:(kc + 1) * TG], m01, cB,
                                  ALU.subtract).then_inc(sems["dve"], 1)
                dvn += 1
        c.dve += 1 + 3 * FC

        def tensor(eng):
            eng.wait_ge(sems["dve"], b_dve + 1)
            eng.matmul(psum[4][:, :TG], small["c_onesrow"],
                       qposb[0:1, g * TG:(g + 1) * TG], start=True, stop=True)
            eng.matmul(psum[5][:, :TG], small["c_onesrow"], crow,
                       start=True, stop=True).then_inc(sems["pe"], 2)
        c.pe += 2

        return {"scalar": scalar, "vector": vector, "tensor": tensor}

    ctx.phase(body)
    ctx.A.reset(mk)


def _attn_kcs(g):
    qmax = {}
    qmax["A"] = [511, 1023, 3583, 4095][g]
    qmax["B"] = [1535, 2047, 2559, 3071][g]
    need = set()
    for pat in ("A", "B"):
        for kc in range(FC):
            if KPOS_GLOBAL[kc * 128:(kc + 1) * 128].min() <= qmax[pat]:
                need.add(kc)
    return sorted(need)


def _emit_l1a_attn(ctx, g, kvh, cmask):
    c, sems, small, psum = ctx.c, ctx.sems, ctx.small, ctx.psum
    io, dr = ctx.io, ctx.dr
    mk = ctx.A.mark()
    kh = ctx.A.alloc(FC * 128)
    vh = ctx.A.alloc(FC * 128)
    qh = [ctx.A.alloc(TG) for _ in range(2)]
    sS = [ctx.A.alloc(TG) for _ in range(2)]
    pT = [ctx.A.alloc(TG) for _ in range(2)]
    zrow = ctx.A.alloc(TG)[0:1, :]
    zb = ctx.A.alloc(TG)
    ao = [ctx.A.alloc(TG) for _ in range(2)]
    vv = dr["ccout"].rearrange("r (h c) -> (r h) c", h=2)
    kcs = _attn_kcs(g)
    nk = len(kcs)
    PEQ, ACQ, DVQ = 3 * nk + 1, nk + 1, nk + 2

    def body(block):
        b_ld = [c.ld0, c.ld1]
        b_lda = c.lda
        b_gp = [c.gp0, c.gp1]
        b_pe, b_act, b_dve = c.pe, c.act, c.dve

        def sync(eng):
            eng.dma_start(out=kh[:, 0:2048],
                          in_=dr["ccout"][kvh * 128:(kvh + 1) * 128, :]
                          ).then_inc(sems["lda"], 16)
            eng.dma_start(out=kh[:, 2048:4096],
                          in_=dr["ccout"][2049 + kvh * 128:2049 + (kvh + 1) * 128, :]
                          ).then_inc(sems["lda"], 16)
            for r in range(2):
                vr0 = 2 * (r * 2049 + 1024)
                eng.dma_start(
                    out=vh[:, r * 2048:(r + 1) * 2048]
                        .rearrange("p (s d) -> p s d", d=128),
                    in_=vv[vr0:vr0 + 2048, kvh * 128:(kvh + 1) * 128]
                        .rearrange("(s p) d -> p s d", p=128),
                ).then_inc(sems["lda"], 16)
            for qi in range(4):
                hh = kvh * NREP + qi
                if qi >= 2:
                    eng.wait_ge(sems["pe"], b_pe + (qi - 1) * PEQ)
                eng.dma_start(out=qh[qi % 2],
                              in_=dr["qdram"][hh * 128:(hh + 1) * 128,
                                              g * TG:(g + 1) * TG]
                              ).then_inc(sems["ld" + str(qi % 2)], 16)
        c.lda += 64
        c.ld0 += 32
        c.ld1 += 32

        def tensor(eng):
            eng.wait_ge(sems["lda"], b_lda + 64)
            for qi in range(4):
                pe0 = b_pe + qi * PEQ
                act0 = b_act + qi * ACQ
                dve0 = b_dve + qi * DVQ
                eng.wait_ge(sems["ld" + str(qi % 2)],
                            b_ld[qi % 2] + 16 * (qi // 2 + 1))
                if qi >= 1:
                    eng.wait_ge(sems["dve"], dve0)
                def emit_pvz(j):
                    kcj = kcs[j]
                    eng.wait_ge(sems["act"], act0 + j + 1)
                    eng.matmul(psum[2][:, :TG], vh[:, kcj * 128:(kcj + 1) * 128],
                               pT[j % 2], start=(j == 0), stop=(j == nk - 1)
                               ).then_inc(sems["pe"], 1)
                    eng.matmul(psum[3][0:1, :TG], small["c_onescol"],
                               pT[j % 2], start=(j == 0), stop=(j == nk - 1)
                               ).then_inc(sems["pe"], 1)

                for j, kc in enumerate(kcs):
                    if j >= 2:
                        eng.wait_ge(sems["dve"], dve0 + j - 1)
                    elif qi >= 1:
                        eng.wait_ge(sems["dve"], dve0)
                    eng.matmul(psum[j % 2][:, :TG], kh[:, kc * 128:(kc + 1) * 128],
                               qh[qi % 2], start=True, stop=True
                               ).then_inc(sems["pe"], 1)
                    if j >= 2:
                        emit_pvz(j - 2)
                emit_pvz(nk - 2)
                emit_pvz(nk - 1)
                eng.wait_ge(sems["dve"], dve0 + nk + 1)
                eng.matmul(psum[4][:, :TG], small["c_onesrow"], zrow,
                           start=True, stop=True).then_inc(sems["pe"], 1)
        c.pe += 4 * PEQ

        def vector(eng):
            for qi in range(4):
                pe0 = b_pe + qi * PEQ
                act0 = b_act + qi * ACQ
                dvn = b_dve + qi * DVQ
                for j, kc in enumerate(kcs):
                    spos = j + 1 if j < 2 else 3 * j - 3
                    eng.wait_ge(sems["pe"], pe0 + spos)
                    if j >= 2:
                        eng.wait_ge(sems["act"], act0 + j - 1)
                    eng.tensor_tensor(sS[j % 2], psum[j % 2][:, :TG],
                                      cmask[:, kc * TG:(kc + 1) * TG], ALU.add
                                      ).then_inc(sems["dve"], 1)
                    dvn += 1
                eng.wait_ge(sems["pe"], pe0 + 3 * nk)

                eng.reciprocal(zrow, psum[3][0:1, :TG]).then_inc(sems["dve"], 1)
                dvn += 1
                eng.wait_ge(sems["act"], act0 + ACQ)
                if qi >= 2:
                    eng.wait_ge(sems["gp" + str(qi % 2)],
                                b_gp[qi % 2] + 16 * (qi // 2))
                eng.tensor_tensor(ao[qi % 2], psum[2][:, :TG], zb, ALU.mult
                                  ).then_inc(sems["dve"], 1)
                dvn += 1
        c.dve += 4 * DVQ

        def scalar(eng):
            for qi in range(4):
                pe0 = b_pe + qi * PEQ
                act0 = b_act + qi * ACQ
                dve0 = b_dve + qi * DVQ
                for j, kc in enumerate(kcs):
                    eng.wait_ge(sems["dve"], dve0 + j + 1)
                    if j >= 2:
                        eng.wait_ge(sems["pe"], pe0 + 3 * j - 1)
                    eng.activation(pT[j % 2], sS[j % 2], AF.Exp
                                   ).then_inc(sems["act"], 1)
                eng.wait_ge(sems["pe"], pe0 + PEQ)
                eng.activation(zb, psum[4][:, :TG], AF.Copy).then_inc(sems["act"], 1)
        c.act += 4 * ACQ

        def gpsimd(eng):
            for qi in range(4):
                hh = kvh * NREP + qi
                dve0 = b_dve + qi * DVQ
                eng.wait_ge(sems["dve"], dve0 + DVQ)
                eng.dma_start(
                    out=dr["s2dram"][g * D + hh * 128:g * D + (hh + 1) * 128, :],
                    in_=ao[qi % 2]).then_inc(sems["gp" + str(qi % 2)], 16)
        c.gp0 += 32
        c.gp1 += 32

        return {"sync": sync, "tensor": tensor, "vector": vector,
                "scalar": scalar, "gpsimd": gpsimd}

    ctx.phase(body)
    ctx.A.reset(mk)


def emit_l1f(ctx):
    c = ctx.c
    io, dr, sems, small, psum = ctx.io, ctx.dr, ctx.sems, ctx.small, ctx.psum

    mk0 = ctx.A.mark()
    rows = ctx.A.alloc(3 * TG)[0:1, :]
    mB = ctx.A.alloc(TG)
    sB = ctx.A.alloc(TG)
    bbias = ctx.A.alloc(FC)
    xhat = ctx.A.alloc(FC * TG)
    y = ctx.A.alloc(FC * TG)
    hw = ctx.A.alloc(TG)
    r16s = ctx.A.alloc(128)[0:8, :]
    bthrB = ctx.A.alloc(4)

    def body_pre(block):
        b_lda = c.lda
        b_pe = c.pe

        def sync(eng):
            for fcb in range(FC):
                eng.dma_start(out=bbias[:, fcb:fcb + 1],
                              in_=io["w_bbias"][fcb * 128:(fcb + 1) * 128, :]
                              ).then_inc(sems["lda"], 16)
            eng.dma_start(out=r16s, in_=io["c_r16"]).then_inc(sems["lda"], 16)
            eng.dma_start(out=bthrB[0:1, 0:1], in_=io["w_bthr"]).then_inc(sems["lda"], 16)
        c.lda += 16 * (FC + 2)

        def tensor(eng):
            eng.wait_ge(sems["lda"], b_lda + 16 * (FC + 2))
            eng.matmul(psum[2][:, 0:1], ctx.small["c_onesrow"], bthrB[0:1, 0:1],
                       start=True, stop=True).then_inc(sems["pe"], 1)
        c.pe += 1

        def scalar(eng):
            eng.wait_ge(sems["pe"], b_pe + 1)
            eng.activation(bthrB[:, 0:1], psum[2][:, 0:1], AF.Copy
                           ).then_inc(sems["act"], 1)
        c.act += 1

        return {"sync": sync, "tensor": tensor, "scalar": scalar}
    ctx.phase(body_pre)

    for g in range(NG):
        emit_ln_stats_dram(ctx, dr["xdram"], g, mB, sB, rows)
        emit_ln_apply_dram(ctx, dr["xdram"], g, mB, sB, xhat)
        _emit_l1f_gates(ctx, g, xhat, hw, r16s, bthrB)
        emit_proj_sweep(ctx, xhat, io["w_base_t"], FC,
                        lambda oc: y[:, oc * TG:(oc + 1) * TG], bias=bbias,
                        extra_dram=io["w_bw_t"], extra_rhs=hw)
        emit_post_ln_residual(ctx, dr["xdram"], g, y, rows, dr["xdram"])
    ctx.A.reset(mk0)


def _emit_l1f_gates(ctx, g, xhat, hw, r16s, bthrB):
    c, sems, small, psum = ctx.c, ctx.sems, ctx.small, ctx.psum
    io = ctx.io
    mk = ctx.A.mark()
    wg = ctx.A.alloc(NE * FC)
    wthr = ctx.A.alloc(FC)
    awT = ctx.A.alloc(FC * 128)
    geT = ctx.A.alloc(4 * NE)
    eT = ctx.A.alloc(4 * NE)
    thrT = ctx.A.alloc(4)
    thT = ctx.A.alloc(4)
    th2 = ctx.A.alloc(4)
    mx = ctx.A.alloc(4)
    ngm = ctx.A.alloc(4)
    Z = ctx.A.alloc(4)
    wsum = ctx.A.alloc(4)
    wz = ctx.A.alloc(4)
    ws2 = ctx.A.alloc(4)
    Zr = ctx.A.alloc(4)
    wr = ctx.A.alloc(4)
    wT = ctx.A.alloc(TG)[0:8, :]
    h_sb = ctx.A.alloc(TG)
    wexp = ctx.A.alloc(TG)

    def xh_tile(kc, tt):
        return xhat[:, kc * TG + tt * 128:kc * TG + (tt + 1) * 128]

    def body(block):
        b_lda = c.lda
        b_pe, b_act, b_dve = c.pe, c.act, c.dve

        def sync(eng):
            eng.dma_start(out=wg.rearrange("p (kc e) -> p kc e", kc=FC),
                          in_=io["w_gate_t"].rearrange("(kc p) e -> p kc e", p=128)
                          ).then_inc(sems["lda"], 16)
            for fcb in range(FC):
                eng.dma_start(out=wthr[:, fcb:fcb + 1],
                              in_=io["w_thr"][fcb * 128:(fcb + 1) * 128, :]
                              ).then_inc(sems["lda"], 16)
            eng.dma_start(out=awT.rearrange("p (kc e) -> p kc e", kc=FC),
                          in_=io["w_aw_t"].rearrange("(kc p) e -> p kc e", p=128)
                          ).then_inc(sems["lda"], 16)
        c.lda += 16 * (FC + 2)

        def tensor(eng):
            eng.wait_ge(sems["lda"], b_lda + 16 * (FC + 2))
            for tt in range(4):
                if tt >= 2:
                    eng.wait_ge(sems["act"], b_act + tt - 1)
                for kc in range(FC):
                    eng.matmul(psum[tt % 2][:, tt * NE:(tt + 1) * NE],
                               xh_tile(kc, tt), wg[:, kc * NE:(kc + 1) * NE],
                               start=(kc == 0), stop=(kc == FC - 1)
                               ).then_inc(sems["pe"], 1)
            for tt in range(4):
                if tt >= 2:
                    eng.wait_ge(sems["act"], b_act + 4 + tt - 1)
                for kc in range(FC):
                    eng.matmul(psum[2 + tt % 2][:, tt:tt + 1],
                               xh_tile(kc, tt), wthr[:, kc:kc + 1],
                               start=(kc == 0), stop=(kc == FC - 1)
                               ).then_inc(sems["pe"], 1)
            for kc in range(FC):
                eng.matmul(psum[4][:, :TG], awT[:, kc * 128:(kc + 1) * 128],
                           xhat[:, kc * TG:(kc + 1) * TG],
                           start=(kc == 0), stop=(kc == FC - 1)
                           ).then_inc(sems["pe"], 1)
            for tt in range(4):
                eng.wait_ge(sems["dve"], b_dve + 8 + 10 * (tt + 1))
                eng.transpose(psum[5][0:8, tt * 128:(tt + 1) * 128],
                              eT[:, tt * NE:(tt + 1) * NE],
                              small["c_ident"]).then_inc(sems["pe"], 1)
            eng.wait_ge(sems["act"], b_act + 18)
            eng.matmul(psum[5][:, :TG], r16s, wT, start=True, stop=True
                       ).then_inc(sems["pe"], 1)
        c.pe += 293

        def scalar(eng):
            for tt in range(4):
                eng.wait_ge(sems["pe"], b_pe + 32 * (tt + 1))
                eng.activation(geT[:, tt * NE:(tt + 1) * NE],
                               psum[tt % 2][:, tt * NE:(tt + 1) * NE], AF.Copy
                               ).then_inc(sems["act"], 1)
            for tt in range(4):
                eng.wait_ge(sems["pe"], b_pe + 128 + 32 * (tt + 1))
                eng.activation(thrT[:, tt:tt + 1], psum[2 + tt % 2][:, tt:tt + 1],
                               AF.Copy).then_inc(sems["act"], 1)
            for tt in range(4):
                eng.wait_ge(sems["act"], b_act + 4 + tt + 1)
                eng.activation(thT[:, tt:tt + 1], thrT[:, tt:tt + 1], AF.Sigmoid,
                               bias=bthrB[:, 0:1]).then_inc(sems["act"], 1)
            for tt in range(4):
                eng.wait_ge(sems["dve"], b_dve + 2 * (tt + 1))
                eng.activation(eT[:, tt * NE:(tt + 1) * NE],
                               geT[:, tt * NE:(tt + 1) * NE], AF.Exp,
                               bias=ngm[:, tt:tt + 1],
                               accum_out=Z[:, tt:tt + 1]).then_inc(sems["act"], 1)
            eng.wait_ge(sems["pe"], b_pe + 288)
            eng.activation(h_sb, psum[4][:, :TG], AF.Copy).then_inc(sems["act"], 1)
            eng.wait_ge(sems["pe"], b_pe + 292)
            eng.activation(wT, psum[5][0:8, :TG], AF.Copy).then_inc(sems["act"], 1)
            eng.wait_ge(sems["pe"], b_pe + 293)
            eng.activation(wexp, psum[5][:, :TG], AF.Copy).then_inc(sems["act"], 1)
        c.act += 19

        def vector(eng):
            dvn = b_dve
            for tt in range(4):
                eng.wait_ge(sems["act"], b_act + tt + 1)
                eng.tensor_reduce(mx[:, tt:tt + 1], geT[:, tt * NE:(tt + 1) * NE],
                                  mybir.AxisListType.X, ALU.max).then_inc(sems["dve"], 1)
                dvn += 1
                eng.wait_ge(sems["dve"], dvn)
                eng.tensor_scalar_mul(ngm[:, tt:tt + 1], mx[:, tt:tt + 1], -1.0
                                      ).then_inc(sems["dve"], 1)
                dvn += 1
            for tt in range(4):
                esl = eT[:, tt * NE:(tt + 1) * NE]
                eng.wait_ge(sems["act"], b_act + 12 + tt + 1)
                eng.reciprocal(Zr[:, tt:tt + 1], Z[:, tt:tt + 1]).then_inc(sems["dve"], 1)
                dvn += 1
                eng.wait_ge(sems["dve"], dvn)
                eng.tensor_scalar(esl, esl, Zr[:, tt:tt + 1], None, ALU.mult
                                  ).then_inc(sems["dve"], 1)
                dvn += 1
                eng.wait_ge(sems["act"], b_act + 8 + tt + 1)
                eng.tensor_scalar_mul(th2[:, tt:tt + 1], thT[:, tt:tt + 1],
                                      MAX_THR).then_inc(sems["dve"], 1)
                dvn += 1
                eng.wait_ge(sems["dve"], dvn)
                eng.tensor_scalar(esl, esl, th2[:, tt:tt + 1], 0.0,
                                  ALU.subtract, ALU.max).then_inc(sems["dve"], 1)
                dvn += 1
                eng.wait_ge(sems["dve"], dvn)
                eng.tensor_reduce(wsum[:, tt:tt + 1], esl,
                                  mybir.AxisListType.X, ALU.add).then_inc(sems["dve"], 1)
                dvn += 1
                eng.wait_ge(sems["dve"], dvn)
                eng.tensor_scalar(wz[:, tt:tt + 1], wsum[:, tt:tt + 1], 0.0, None,
                                  ALU.is_equal).then_inc(sems["dve"], 1)
                dvn += 1
                eng.wait_ge(sems["dve"], dvn)
                eng.tensor_scalar_mul(ws2[:, tt:tt + 1], wsum[:, tt:tt + 1],
                                      1.0 / MOE_SCALING).then_inc(sems["dve"], 1)
                dvn += 1
                eng.wait_ge(sems["dve"], dvn)
                eng.tensor_tensor(ws2[:, tt:tt + 1], ws2[:, tt:tt + 1],
                                  wz[:, tt:tt + 1], ALU.add).then_inc(sems["dve"], 1)
                dvn += 1
                eng.wait_ge(sems["dve"], dvn)
                eng.reciprocal(wr[:, tt:tt + 1], ws2[:, tt:tt + 1]).then_inc(sems["dve"], 1)
                dvn += 1
                eng.wait_ge(sems["dve"], dvn)
                eng.tensor_scalar(esl, esl, wr[:, tt:tt + 1], None, ALU.mult
                                  ).then_inc(sems["dve"], 1)
                dvn += 1
            eng.wait_ge(sems["act"], b_act + 19)
            eng.tensor_tensor(hw, h_sb, wexp, ALU.mult).then_inc(sems["dve"], 1)
            dvn += 1
        c.dve += 49

        return {"sync": sync, "tensor": tensor, "scalar": scalar, "vector": vector}

    ctx.phase(body)
    ctx.A.reset(mk)


def build_core_program(sublayers):
    nc = bass.Bass("TRN2", target_bir_lowering=False)
    io = {}
    io["xin"] = nc.dram_tensor("xin", [TOK, D], F32, kind="ExternalInput").ap()
    io["xout"] = nc.dram_tensor("xout", [TOK, D], F32, kind="ExternalOutput").ap()
    if DEBUG_TAP is not None:
        io["dbg"] = nc.dram_tensor("dbg", [D, TOK], F32, kind="ExternalOutput").ap()
    for k, shp in CONST_SHAPES.items():
        io[k] = nc.dram_tensor(k, shp, F32, kind="ExternalInput").ap()

    if "l0a" in sublayers:
        io["w_mask"] = nc.dram_tensor("w_mask", [D, D], I32, kind="ExternalInput").ap()
        io["w_weight"] = nc.dram_tensor("w_weight", [D, D], F32, kind="ExternalInput").ap()
        io["w_bias"] = nc.dram_tensor("w_bias", [D, 1], F32, kind="ExternalInput").ap()
    if "l0f" in sublayers:
        io["w_up"] = nc.dram_tensor("w_up", [FH * FD, FM], F32, kind="ExternalInput").ap()
        io["w_down"] = nc.dram_tensor("w_down", [FH * FM, FD], F32, kind="ExternalInput").ap()
        io["w_perm"] = nc.dram_tensor("w_perm", [D, 1], I32, kind="ExternalInput").ap()
    if "l1a" in sublayers:
        io["w_wq"] = nc.dram_tensor("w_wq", [D, D], F32, kind="ExternalInput").ap()
        io["w_wk"] = nc.dram_tensor("w_wk", [D, NKV * HD], F32, kind="ExternalInput").ap()
        io["w_wv"] = nc.dram_tensor("w_wv", [D, NKV * HD], F32, kind="ExternalInput").ap()
        io["w_wo"] = nc.dram_tensor("w_wo", [D, D], F32, kind="ExternalInput").ap()
    if "l1f" in sublayers:
        io["w_base_t"] = nc.dram_tensor("w_base_t", [D, D], F32, kind="ExternalInput").ap()
        io["w_bbias"] = nc.dram_tensor("w_bbias", [D, 1], F32, kind="ExternalInput").ap()
        io["w_gate_t"] = nc.dram_tensor("w_gate_t", [D, NE], F32, kind="ExternalInput").ap()
        io["w_thr"] = nc.dram_tensor("w_thr", [D, 1], F32, kind="ExternalInput").ap()
        io["w_bthr"] = nc.dram_tensor("w_bthr", [1, 1], F32, kind="ExternalInput").ap()
        io["w_aw_t"] = nc.dram_tensor("w_aw_t", [D, NE * NR], F32, kind="ExternalInput").ap()
        io["w_bw_t"] = nc.dram_tensor("w_bw_t", [NE * NR, D], F32, kind="ExternalInput").ap()

    dr = {}
    dr["xdram"] = nc.dram_tensor("xdram", [D, TOK], F32, kind="Internal").ap()
    if "l0a" in sublayers:
        dr["wtdram"] = nc.dram_tensor("wtdram", [D, D], F32, kind="Internal").ap()
    if "l0f" in sublayers or "l1a" in sublayers or "l1f" in sublayers:
        dr["sdram"] = nc.dram_tensor("sdram", [NG * D, TG], F32, kind="Internal").ap()
        dr["s2dram"] = nc.dram_tensor("s2dram", [NG * D, TG], F32, kind="Internal").ap()
    if "l1a" in sublayers:
        dr["qdram"] = nc.dram_tensor("qdram", [D, TOK], F32, kind="Internal").ap()
        dr["ccin"] = nc.dram_tensor("ccin", [2049, TOK], F32, kind="Internal").ap()
        dr["ccout"] = nc.dram_tensor("ccout", [2 * 2049, TOK], F32, kind="Internal").ap()

    AR = nc.alloc_sbuf_tensor("arena", [128, ARENA_COLS], F32).ap()
    ARB = nc.alloc_sbuf_tensor("arenab", [128, ARENAB_COLS], BF16).ap()
    small = {}
    for k in SMALL_CONSTS:
        small[k] = nc.alloc_sbuf_tensor("s" + k, CONST_SHAPES[k], F32).ap()

    psum = [nc.alloc_psum_tensor(f"ps{i}", [128, 512], F32).ap() for i in range(6)]

    cts = Counts()
    sems = {}
    hold = []
    for name in ("ph",) + DMA_SEMS + ENG_SEMS:
        h = nc.semaphore("sem_" + name)
        sems[name] = h.__enter__()
        hold.append(h)

    ctx = Ctx(nc, io, dr, Arena(AR, ARENA_COLS), Arena(ARB, ARENAB_COLS),
              small, psum, cts, sems)

    def ph_consts(block):
        def sync(eng):
            for k in SMALL_CONSTS:
                eng.dma_start(out=small[k], in_=io[k]).then_inc(sems["lda"], 16)
                cts.lda += 16
        return {"sync": sync}
    ctx.phase(ph_consts)

    emit_entry_transpose(ctx, io["xin"], dr["xdram"])
    emitters = {"l0a": emit_l0a, "l0f": emit_l0f, "l1a": emit_l1a, "l1f": emit_l1f}
    for name in sublayers:
        emitters[name](ctx)
    emit_exit_transpose(ctx, dr["xdram"], io["xout"])

    for h in reversed(hold):
        h.__exit__(None, None, None)
    return nc


def _shard_tokens(x_flat):
    out = []
    for core in range(NCORE):
        parts = [x_flat[a:b] for a, b in _token_slices(core)]
        out.append(np.ascontiguousarray(np.concatenate(parts, axis=0)))
    return out


def _unshard_tokens(shards):
    x = np.empty((B * S, D), np.float32)
    for core in range(NCORE):
        pos = 0
        for a, b in _token_slices(core):
            n = b - a
            x[a:b] = shards[core][pos:pos + n]
            pos += n
    return x


def _layer_inputs(name, inputs):
    w = {}
    if name == "l0a":
        w["w_mask"] = np.ascontiguousarray(inputs["L0_attn_mask"].astype(np.int32))
        w["w_weight"] = np.ascontiguousarray(inputs["L0_attn_weight"])
        w["w_bias"] = np.ascontiguousarray(
            np.asarray(inputs["L0_attn_bias"], np.float32).reshape(D, 1))
    elif name == "l0f":
        w["w_up"] = np.ascontiguousarray(
            np.asarray(inputs["L0_ffn_up"], np.float32).reshape(FH * FD, FM))
        w["w_down"] = np.ascontiguousarray(
            np.asarray(inputs["L0_ffn_down"], np.float32).reshape(FH * FM, FD))
        w["w_perm"] = np.ascontiguousarray(
            inputs["L0_ffn_perm"].astype(np.int32).reshape(D, 1))
    elif name == "l1a":
        w["w_wq"] = np.ascontiguousarray(inputs["L1_attn_wq"])
        w["w_wk"] = np.ascontiguousarray(inputs["L1_attn_wk"])
        w["w_wv"] = np.ascontiguousarray(inputs["L1_attn_wv"])
        w["w_wo"] = np.ascontiguousarray(inputs["L1_attn_wo"])
    elif name == "l1f":
        w["w_base_t"] = np.ascontiguousarray(
            np.asarray(inputs["L1_ffn_W_base"], np.float32).T)
        w["w_bbias"] = np.ascontiguousarray(
            np.asarray(inputs["L1_ffn_b_base"], np.float32).reshape(D, 1))
        w["w_gate_t"] = np.ascontiguousarray(
            np.asarray(inputs["L1_ffn_W_gate"], np.float32).T)
        w["w_thr"] = np.ascontiguousarray(
            np.asarray(inputs["L1_ffn_W_thr"], np.float32).reshape(D, 1))
        w["w_bthr"] = np.ascontiguousarray(
            np.asarray(inputs["L1_ffn_b_thr"], np.float32).reshape(1, 1))
        w["w_aw_t"] = np.ascontiguousarray(
            np.asarray(inputs["L1_ffn_A_w"], np.float32).reshape(NE * NR, D).T)
        w["w_bw_t"] = np.ascontiguousarray(
            np.asarray(inputs["L1_ffn_B_w"], np.float32).transpose(0, 2, 1)
            .reshape(NE * NR, D))
    return w


def run_layers(x_shards, layers, inputs, core_ids=None):
    if core_ids is None:
        core_ids = list(range(NCORE))
    nc = build_core_program(layers)
    wmaps = {}
    for name in layers:
        wmaps.update(_layer_inputs(name, inputs))
    in_maps = []
    for core in core_ids:
        m = {"xin": x_shards[core]}
        m.update(make_consts(core))
        m.update(wmaps)
        in_maps.append(m)
    res = run_bass_kernel_spmd(nc, in_maps, core_ids=core_ids)
    return [r["xout"] for r in res.results]


ALL_LAYERS = ["l0a", "l0f", "l1a", "l1f"]


def kernel(**inputs):
    x = np.ascontiguousarray(np.asarray(inputs["L0_attn_data"], np.float32)
                             .reshape(B * S, D))
    shards = _shard_tokens(x)
    shards = run_layers(shards, ALL_LAYERS, inputs)
    out = _unshard_tokens(shards)
    return out.reshape(B, S, D)


# version 03
# speedup vs baseline: 1.7777x; 1.7777x over previous
import numpy as np
import concourse.bass as bass
import concourse.mybir as mybir
from concourse.bass_utils import run_bass_kernel_spmd

F32 = mybir.dt.float32
F32R = mybir.dt.float32r
BF16 = mybir.dt.bfloat16
I32 = mybir.dt.int32
AF = mybir.ActivationFunctionType
ALU = mybir.AluOpType

D = 4096
TOK = 2048
NCORE = 8
TG = 512
NG = TOK // TG
FC = D // 128
B, S = 4, 4096
EPS = 1e-5
NBIG = -1.0e9
NH, NKV, HD = 32, 8, 128
NREP = NH // NKV
THETA = 10000.0
FH, FD, FM = 16, 256, 1024
NE, NR = 8, 16
MOE_SCALING = 2.0
MAX_THR = 0.125

ARENA_COLS = 32768
ARENAB_COLS = 30720

DEBUG_STOP = None
DEBUG_TAP = None
CC_GROUPS = [[0, 1], [2, 3], [4, 5], [6, 7]]
DEBUG_MAXPH = None


def _token_slices(core):
    b = core // 2
    if core % 2 == 0:
        return [(b * S + 0, b * S + 1024), (b * S + 3072, b * S + 4096)]
    return [(b * S + 1024, b * S + 3072)]


def _qpos(core):
    if core % 2 == 0:
        return np.concatenate([np.arange(0, 1024), np.arange(3072, 4096)])
    return np.arange(1024, 3072)


KPOS_GLOBAL = np.concatenate(
    [np.arange(0, 1024), np.arange(3072, 4096), np.arange(1024, 3072)]
).astype(np.float32)


def _rope_tables(pos):
    freqs = 1.0 / THETA ** (np.arange(0, HD, 2, dtype=np.float64) / HD)
    ang = np.outer(freqs, pos.astype(np.float64))
    cs, sn = np.cos(ang), np.sin(ang)
    C = np.repeat(cs, 2, axis=0)
    Sm = np.empty_like(C)
    Sm[0::2] = -sn
    Sm[1::2] = sn
    return C.astype(np.float32), Sm.astype(np.float32)


def make_consts(core):
    c = {}
    c["c_ident"] = np.eye(128, dtype=np.float32)
    c["c_onescol"] = np.ones((128, 1), np.float32)
    c["c_onesrow"] = np.ones((1, 128), np.float32)
    c["c_eps"] = np.full((1, 1), EPS, np.float32)
    c["c_rsqhd"] = np.full((128, 1), 1.0 / np.sqrt(HD), np.float32)
    o = np.zeros((128, FC * 128), np.float32)
    for ncc in range(FC):
        for kl in range(128):
            o[kl, ncc * 128 + 4 * ncc + kl // 32] = 1.0
    c["c_o32a"] = o
    r = np.zeros((128, FC * 128), np.float32)
    for ncc in range(FC):
        for p in range(128):
            r[4 * ncc + p // 32, ncc * 128 + p] = 1.0
    c["c_rsel"] = r
    x = np.zeros((128, D), np.float32)
    for k in range(D):
        x[k // 32, k] = 1.0
    c["c_xex"] = x
    r16 = np.zeros((8, 128), np.float32)
    for p in range(128):
        r16[p // 16, p] = 1.0
    c["c_r16"] = r16
    ps = np.zeros((128, 128), np.float32)
    for i in range(0, 128, 2):
        ps[i + 1, i] = 1.0
        ps[i, i + 1] = 1.0
    c["c_pswap"] = ps
    qp = _qpos(core)
    c["c_qpos"] = qp.astype(np.float32).reshape(1, TOK)
    c["c_kpos"] = KPOS_GLOBAL.reshape(S, 1).copy()
    C_, S_ = _rope_tables(qp)
    c["c_ropec"] = C_
    c["c_ropes"] = S_
    return c


CONST_SHAPES = {
    "c_ident": [128, 128], "c_onescol": [128, 1], "c_onesrow": [1, 128],
    "c_eps": [1, 1], "c_rsqhd": [128, 1],
    "c_o32a": [128, FC * 128], "c_rsel": [128, FC * 128], "c_xex": [128, D],
    "c_pswap": [128, 128], "c_qpos": [1, TOK], "c_kpos": [S, 1],
    "c_r16": [8, 128],
    "c_ropec": [128, TOK], "c_ropes": [128, TOK],
}
SMALL_CONSTS = ("c_ident", "c_onescol", "c_onesrow", "c_eps", "c_rsqhd")

DMA_SEMS = ("ld0", "ld1", "lda", "gp0", "gp1", "gp2", "gp3", "gpd",
            "sc0", "sc1", "scd", "ld0a", "ld1a", "ldaa", "gpda")
ENG_SEMS = ("pe", "dve", "act", "gpc", "cc", "pe2", "dve2", "act2", "gpc2")

ALT_MAP = {"pe": "pe2", "dve": "dve2", "act": "act2", "gpc": "gpc2",
           "ld0": "ld0a", "ld1": "ld1a", "lda": "ldaa", "gpd": "gpda"}


class AltCounts:
    def __init__(self, c):
        object.__setattr__(self, "_c", c)

    def __getattr__(self, n):
        return getattr(self._c, ALT_MAP.get(n, n))

    def __setattr__(self, n, v):
        setattr(self._c, ALT_MAP.get(n, n), v)


class AltSems:
    def __init__(self, sems):
        self._s = sems

    def __getitem__(self, n):
        return self._s[ALT_MAP.get(n, n)]


class Counts:
    def __init__(self):
        for n in DMA_SEMS + ENG_SEMS:
            setattr(self, n, 0)
        self.ph = 0


ENGS = ("sync", "tensor", "vector", "scalar", "gpsimd")


class GroupedDram:

    def __init__(self, base, g):
        self.base = base
        self.g = g

    def __getitem__(self, idx):
        rows, cols = idx
        g = self.g
        return self.base[g * D + rows.start:g * D + rows.stop,
                         cols.start - g * TG:cols.stop - g * TG]


class Arena:
    def __init__(self, ap, cols):
        self.ap = ap
        self.cols = cols
        self.off = 0

    def alloc(self, w):
        assert self.off + w <= self.cols, f"arena overflow {self.off}+{w}"
        sl = self.ap[:, self.off:self.off + w]
        self.off += w
        return sl

    def mark(self):
        return self.off

    def reset(self, off):
        self.off = off


class Ctx:
    def __init__(self, nc, io, dr, arena, arenab, small, psum, c, sems):
        self.nc = nc
        self.io = io
        self.dr = dr
        self.A = arena
        self.AB = arenab
        self.small = small
        self.psum = psum
        self.c = c
        self.sems = sems

    def phase(self, body):
        if not hasattr(self, "_nph"):
            self._nph = 0
        self._nph += 1
        if DEBUG_MAXPH is not None and self._nph > DEBUG_MAXPH:
            return
        nc, c, sems = self.nc, self.c, self.sems
        target = c.ph
        PRE_SEMS = ("pe", "dve", "act", "gpc", "pe2", "dve2", "act2", "gpc2")
        pre = tuple(getattr(c, s) for s in PRE_SEMS)
        with nc.Block() as block:
            emitters = body(block) or {}
            for name in ENGS:
                fn = emitters.get(name)
                sec = getattr(block, name)

                def _make(name=name, fn=fn):
                    def _(eng):
                        eng.wait_ge(sems["ph"], target)
                        if fn is not None:
                            for i, s in enumerate(PRE_SEMS):
                                if pre[i]:
                                    eng.wait_ge(sems[s], pre[i])
                            fn(eng)
                        if name == "sync":
                            for s in ("ld0", "ld1", "lda", "ld0a", "ld1a", "ldaa"):
                                if getattr(c, s):
                                    eng.wait_ge(sems[s], getattr(c, s))
                        if name == "gpsimd":
                            for s in ("gp0", "gp1", "gp2", "gp3", "gpd", "gpda"):
                                if getattr(c, s):
                                    eng.wait_ge(sems[s], getattr(c, s))
                        if name == "scalar":
                            for s in ("sc0", "sc1", "scd"):
                                if getattr(c, s):
                                    eng.wait_ge(sems[s], getattr(c, s))
                        eng.nop().then_inc(sems["ph"], 1)
                    return _
                sec(_make())
        c.ph = target + 5

    def alt(self):
        import copy
        a = object.__new__(Ctx)
        a.__dict__.update(self.__dict__)
        a.c = AltCounts(self.c)
        a.sems = AltSems(self.sems)
        return a

    def phases(self, bodies):
        nc, c, sems = self.nc, self.c, self.sems
        target = c.ph
        PRE_SEMS = ("pe", "dve", "act", "gpc", "pe2", "dve2", "act2", "gpc2")
        pre = tuple(getattr(c, s) for s in PRE_SEMS)
        allq = ("ld0", "ld1", "lda", "gp0", "gp1", "gp2", "gp3", "gpd",
                "sc0", "sc1", "scd", "ld0a", "ld1a", "ldaa", "gpda")
        with nc.Block() as block:
            emitter_list = []
            seam = []
            for body in bodies:
                emitter_list.append(body(block) or {})
                seam.append({s: getattr(c, s) for s in allq})
            for name in ENGS:
                fns = [e.get(name) for e in emitter_list if e.get(name)]
                sec = getattr(block, name)

                qsems = {"sync": ("ld0", "ld1", "lda", "ld0a", "ld1a", "ldaa"),
                         "gpsimd": ("gp0", "gp1", "gp2", "gp3", "gpd", "gpda"),
                         "scalar": ("sc0", "sc1", "scd")}.get(name, ())

                fidx = [i for i, e in enumerate(emitter_list) if e.get(name)]

                def _make(name=name, fns=tuple(fns), qsems=qsems,
                          fidx=tuple(fidx)):
                    def _(eng):
                        eng.wait_ge(sems["ph"], target)
                        if fns:
                            for i, s in enumerate(PRE_SEMS):
                                if pre[i]:
                                    eng.wait_ge(sems[s], pre[i])
                            for fi, fn in enumerate(fns):
                                if fi > 0:
                                    sm = seam[fidx[fi] - 1]
                                    for s in qsems:
                                        if sm[s]:
                                            eng.wait_ge(sems[s], sm[s])
                                fn(eng)
                        if name == "sync":
                            for s in ("ld0", "ld1", "lda", "ld0a", "ld1a", "ldaa"):
                                if getattr(c, s):
                                    eng.wait_ge(sems[s], getattr(c, s))
                        if name == "gpsimd":
                            for s in ("gp0", "gp1", "gp2", "gp3", "gpd", "gpda"):
                                if getattr(c, s):
                                    eng.wait_ge(sems[s], getattr(c, s))
                        if name == "scalar":
                            for s in ("sc0", "sc1", "scd"):
                                if getattr(c, s):
                                    eng.wait_ge(sems[s], getattr(c, s))
                        eng.nop().then_inc(sems["ph"], 1)
                    return _
                sec(_make())
        c.ph = target + 5


def emit_tap_sbuf(ctx, sl, r0=0):
    c, sems, io = ctx.c, ctx.sems, ctx.io
    if "dbg" not in io:
        return

    def body(block):
        def gpsimd(eng):
            eng.dma_start(out=io["dbg"][r0:r0 + sl.shape[0], 0:sl.shape[1]],
                          in_=sl).then_inc(sems["gpd"], 16)
        c.gpd += 16
        return {"gpsimd": gpsimd}
    ctx.phase(body)


def emit_tap_dram(ctx, src, rows, cols):
    c, sems, io = ctx.c, ctx.sems, ctx.io
    if "dbg" not in io:
        return

    def body(block):
        def gpsimd(eng):
            eng.dma_start(out=io["dbg"][0:rows, 0:cols],
                          in_=src[0:rows, 0:cols]).then_inc(sems["gpd"], 16)
        c.gpd += 16
        return {"gpsimd": gpsimd}
    ctx.phase(body)


def emit_entry_transpose(ctx, src_ap, dst_dram):
    c, sems, small, psum = ctx.c, ctx.sems, ctx.small, ctx.psum
    mk = ctx.A.mark()
    xt = [ctx.A.alloc(D) for _ in range(2)]
    ev = [ctx.A.alloc(512) for _ in range(2)]
    NT = TOK // 128

    def body(block):
        b_ld = [c.ld0, c.ld1]
        b_gp = [c.gp0, c.gp1]
        b_pe, b_act = c.pe, c.act

        def sync(eng):
            for tt in range(NT):
                if tt >= 2:
                    eng.wait_ge(sems["pe"], b_pe + (tt - 1) * FC)
                eng.dma_start(out=xt[tt % 2], in_=src_ap[tt * 128:(tt + 1) * 128, :]
                              ).then_inc(sems["ld" + str(tt % 2)], 16)
        c.ld0 += 16 * (NT // 2)
        c.ld1 += 16 * (NT // 2)

        def tensor(eng):
            for tt in range(NT):
                eng.wait_ge(sems["ld" + str(tt % 2)],
                            b_ld[tt % 2] + 16 * (tt // 2 + 1))
                for fc in range(FC):
                    g = tt * 8 + fc // 4
                    if g >= 2 and fc % 4 == 0:
                        eng.wait_ge(sems["act"], b_act + g - 1)
                    eng.transpose(psum[g % 2][:, (fc % 4) * 128:(fc % 4 + 1) * 128],
                                  xt[tt % 2][:, fc * 128:(fc + 1) * 128],
                                  small["c_ident"]).then_inc(sems["pe"], 1)
        c.pe += NT * FC

        def scalar(eng):
            for g in range(NT * 8):
                eng.wait_ge(sems["pe"], b_pe + (g + 1) * 4)
                if g >= 2:
                    eng.wait_ge(sems["gp" + str(g % 2)],
                                b_gp[g % 2] + 64 * (g // 2))
                eng.activation(ev[g % 2], psum[g % 2][:, :], AF.Copy
                               ).then_inc(sems["act"], 1)
        c.act += NT * 8

        def gpsimd(eng):
            for g in range(NT * 8):
                tt, fg = divmod(g, 8)
                eng.wait_ge(sems["act"], b_act + g + 1)
                for q in range(4):
                    fc = fg * 4 + q
                    eng.dma_start(
                        out=dst_dram[fc * 128:(fc + 1) * 128, tt * 128:(tt + 1) * 128],
                        in_=ev[g % 2][:, q * 128:(q + 1) * 128],
                    ).then_inc(sems["gp" + str(g % 2)], 16)
        c.gp0 += 64 * (NT * 8 // 2)
        c.gp1 += 64 * (NT * 8 // 2)

        return {"sync": sync, "tensor": tensor, "scalar": scalar, "gpsimd": gpsimd}

    ctx.phase(body)
    ctx.A.reset(mk)


def emit_exit_transpose(ctx, src_dram, dst_ap):
    c, sems, small, psum = ctx.c, ctx.sems, ctx.small, ctx.psum
    mk = ctx.A.mark()
    ft = [ctx.A.alloc(512) for _ in range(2)]
    ot = [ctx.A.alloc(D) for _ in range(2)]
    NT = TOK // 128

    def body(block):
        b_ld = [c.ld0, c.ld1]
        b_gp = [c.gp0, c.gp1]
        b_pe, b_act = c.pe, c.act

        def sync(eng):
            i = 0
            for tt in range(NT):
                for fg in range(8):
                    if i >= 2:
                        eng.wait_ge(sems["pe"], b_pe + (i - 1) * 4)
                    eng.dma_start(
                        out=ft[i % 2].rearrange("p (q t) -> p q t", q=4),
                        in_=src_dram[fg * 512:(fg + 1) * 512, tt * 128:(tt + 1) * 128]
                            .rearrange("(q p) t -> p q t", p=128),
                    ).then_inc(sems["ld" + str(i % 2)], 16)
                    i += 1
        c.ld0 += 16 * (NT * 8 // 2)
        c.ld1 += 16 * (NT * 8 // 2)

        def tensor(eng):
            i = 0
            for tt in range(NT):
                for fg in range(8):
                    eng.wait_ge(sems["ld" + str(i % 2)],
                                b_ld[i % 2] + 16 * (i // 2 + 1))
                    if i >= 2:
                        eng.wait_ge(sems["act"], b_act + i - 1)
                    for q in range(4):
                        eng.transpose(psum[i % 2][:, q * 128:(q + 1) * 128],
                                      ft[i % 2][:, q * 128:(q + 1) * 128],
                                      small["c_ident"]).then_inc(sems["pe"], 1)
                    i += 1
        c.pe += NT * 8 * 4

        def scalar(eng):
            i = 0
            for tt in range(NT):
                if tt >= 2:
                    eng.wait_ge(sems["gp" + str(tt % 2)],
                                b_gp[tt % 2] + 16 * (tt // 2))
                for fg in range(8):
                    eng.wait_ge(sems["pe"], b_pe + (i + 1) * 4)
                    eng.activation(ot[tt % 2][:, fg * 512:(fg + 1) * 512],
                                   psum[i % 2][:, :], AF.Copy).then_inc(sems["act"], 1)
                    i += 1
        c.act += NT * 8

        def gpsimd(eng):
            for tt in range(NT):
                eng.wait_ge(sems["act"], b_act + (tt + 1) * 8)
                eng.dma_start(out=dst_ap[tt * 128:(tt + 1) * 128, :],
                              in_=ot[tt % 2]).then_inc(sems["gp" + str(tt % 2)], 16)
        c.gp0 += 16 * (NT // 2)
        c.gp1 += 16 * (NT // 2)

        return {"sync": sync, "tensor": tensor, "scalar": scalar, "gpsimd": gpsimd}

    ctx.phase(body)
    ctx.A.reset(mk)


def _row_math2(eng, sems, c, b_dve, b_act, pA, pB, m_row, v_row, t_row,
               s_row, FCn):
    eng.tensor_scalar_mul(m_row, pA[0:1, :TG], 1.0 / D).then_inc(sems["dve"], 1)
    eng.tensor_scalar_mul(v_row, pB[0:1, :TG], 1.0 / D).then_inc(sems["dve"], 1)
    eng.wait_ge(sems["dve"], b_dve + 2)
    eng.tensor_tensor(t_row, m_row, m_row, ALU.mult).then_inc(sems["dve"], 1)
    eng.wait_ge(sems["dve"], b_dve + 3)
    eng.tensor_tensor(v_row, v_row, t_row, ALU.subtract).then_inc(sems["dve"], 1)
    eng.wait_ge(sems["act"], b_act + FCn + 1)
    eng.reciprocal(s_row, s_row).then_inc(sems["dve"], 1)


def emit_ln_stats_dram(ctx, src_dram, g, mB, sB, rows, psb=(2, 3, 4, 5),
                       defer=False, slim=False):
    c, sems, small = ctx.c, ctx.sems, ctx.small
    pA, pB, pC, pD = (ctx.psum[i] for i in psb)
    mk = ctx.A.mark()
    if slim:
        xb = ctx.A.alloc(TG)
        sq = ctx.A.alloc(TG)
    else:
        xb = ctx.A.alloc(2 * TG)
        sq = ctx.A.alloc(2 * TG)
    m_row = rows[0:1, 0:TG]
    v_row = rows[0:1, TG:2 * TG]
    t_row = rows[0:1, 2 * TG:3 * TG]
    s_row = v_row

    def body(block):
        b_ld = [c.ld0, c.ld1]
        b_pe, b_act, b_dve = c.pe, c.act, c.dve

        def xsl_of(fc):
            return xb if slim else xb[:, (fc % 2) * TG:(fc % 2 + 1) * TG]

        def qsl_of(fc):
            return sq if slim else sq[:, (fc % 2) * TG:(fc % 2 + 1) * TG]

        def sync(eng):
            for fc in range(FC):
                if slim:
                    if fc >= 1:
                        eng.wait_ge(sems["pe"], b_pe + 2 * fc)
                        eng.wait_ge(sems["act"], b_act + fc)
                    eng.dma_start(out=xb,
                                  in_=src_dram[fc * 128:(fc + 1) * 128,
                                               g * TG:(g + 1) * TG]
                                  ).then_inc(sems["ld0"], 16)
                else:
                    if fc >= 2:
                        eng.wait_ge(sems["pe"], b_pe + 2 * (fc - 1))
                    eng.dma_start(out=xb[:, (fc % 2) * TG:(fc % 2 + 1) * TG],
                                  in_=src_dram[fc * 128:(fc + 1) * 128,
                                               g * TG:(g + 1) * TG]
                                  ).then_inc(sems["ld" + str(fc % 2)], 16)
        if slim:
            c.ld0 += 16 * FC
        else:
            c.ld0 += 16 * (FC // 2)
            c.ld1 += 16 * (FC // 2)

        def scalar(eng):
            for fc in range(FC):
                if slim:
                    eng.wait_ge(sems["ld0"], b_ld[0] + 16 * (fc + 1))
                    if fc >= 1:
                        eng.wait_ge(sems["pe"], b_pe + 2 * fc)
                else:
                    eng.wait_ge(sems["ld" + str(fc % 2)],
                                b_ld[fc % 2] + 16 * (fc // 2 + 1))
                    if fc >= 2:
                        eng.wait_ge(sems["pe"], b_pe + 2 * (fc - 1))
                eng.activation(qsl_of(fc), xsl_of(fc),
                               AF.Square).then_inc(sems["act"], 1)
            eng.wait_ge(sems["dve"], b_dve + 4)
            eng.activation(s_row, v_row, AF.Sqrt, bias=small["c_eps"]
                           ).then_inc(sems["act"], 1)
            eng.wait_ge(sems["pe"], b_pe + 2 * FC + 2)
            eng.activation(mB, pC[:, :TG], AF.Copy)
            eng.activation(sB, pD[:, :TG], AF.Copy).then_inc(sems["act"], 2)
        c.act += FC + 3

        def tensor(eng):
            for fc in range(FC):
                eng.wait_ge(sems["act"], b_act + fc + 1)
                eng.matmul(pA[0:1, :TG], small["c_onescol"], xsl_of(fc),
                           start=(fc == 0), stop=(fc == FC - 1))
                eng.matmul(pB[0:1, :TG], small["c_onescol"], qsl_of(fc),
                           start=(fc == 0), stop=(fc == FC - 1)).then_inc(sems["pe"], 2)
            eng.wait_ge(sems["dve"], b_dve + 5)
            eng.matmul(pC[:, :TG], small["c_onesrow"], m_row, start=True, stop=True)
            eng.matmul(pD[:, :TG], small["c_onesrow"], s_row,
                       start=True, stop=True).then_inc(sems["pe"], 2)
        c.pe += 2 * FC + 2

        def vector(eng):
            eng.wait_ge(sems["pe"], b_pe + 2 * FC)
            _row_math2(eng, sems, c, b_dve, b_act, pA, pB, m_row, v_row,
                       t_row, s_row, FC)
        c.dve += 5

        return {"sync": sync, "scalar": scalar, "tensor": tensor, "vector": vector}

    if defer:
        return body
    ctx.phase(body)
    ctx.A.reset(mk)


def emit_ln_apply_dram(ctx, src_dram, g, mB, sB, xhat, store_dram=None,
                       defer=False):
    c, sems = ctx.c, ctx.sems
    mk = ctx.A.mark()
    xb = ctx.A.alloc(2 * TG)
    tb = ctx.A.alloc(2 * TG)

    def body(block):
        b_ld = [c.ld0, c.ld1]
        b_dve, b_gpc = c.dve, c.gpc

        def sync(eng):
            for fc in range(FC):
                if fc >= 2:
                    eng.wait_ge(sems["dve"], b_dve + fc - 1)
                eng.dma_start(out=xb[:, (fc % 2) * TG:(fc % 2 + 1) * TG],
                              in_=src_dram[fc * 128:(fc + 1) * 128, g * TG:(g + 1) * TG]
                              ).then_inc(sems["ld" + str(fc % 2)], 16)
        c.ld0 += 16 * (FC // 2)
        c.ld1 += 16 * (FC // 2)

        def vector(eng):
            for fc in range(FC):
                eng.wait_ge(sems["ld" + str(fc % 2)],
                            b_ld[fc % 2] + 16 * (fc // 2 + 1))
                if fc >= 2:
                    eng.wait_ge(sems["gpc"], b_gpc + fc - 1)
                eng.tensor_tensor(tb[:, (fc % 2) * TG:(fc % 2 + 1) * TG],
                                  xb[:, (fc % 2) * TG:(fc % 2 + 1) * TG], mB,
                                  ALU.subtract).then_inc(sems["dve"], 1)
        c.dve += FC

        def gpsimd(eng):
            for fc in range(FC):
                eng.wait_ge(sems["dve"], b_dve + fc + 1)
                eng.tensor_tensor(xhat[:, fc * TG:(fc + 1) * TG],
                                  tb[:, (fc % 2) * TG:(fc % 2 + 1) * TG], sB,
                                  ALU.mult).then_inc(sems["gpc"], 1)
                if store_dram is not None:
                    eng.wait_ge(sems["gpc"], b_gpc + fc + 1)
                    eng.dma_start(
                        out=store_dram[fc * 128:(fc + 1) * 128, g * TG:(g + 1) * TG],
                        in_=xhat[:, fc * TG:(fc + 1) * TG]).then_inc(sems["gpd"], 16)
        c.gpc += FC
        if store_dram is not None:
            c.gpd += 16 * FC

        return {"sync": sync, "vector": vector, "gpsimd": gpsimd}

    if defer:
        return body
    ctx.phase(body)
    ctx.A.reset(mk)


def emit_residual_from_dram(ctx, x_dram, y_dram, g, m2B, s2B, out_dram,
                            defer=False):
    c, sems = ctx.c, ctx.sems
    mk = ctx.A.mark()
    xb = ctx.A.alloc(2 * TG)
    yb = ctx.A.alloc(2 * TG)
    tb = ctx.A.alloc(2 * TG)

    def body(block):
        b_ld = [c.ld0, c.ld1]
        b_sc = [c.sc0, c.sc1]
        b_gp = [c.gp0, c.gp1]
        b_dve, b_gpc = c.dve, c.gpc

        def sync(eng):
            for fc in range(FC):
                if fc >= 2:
                    eng.wait_ge(sems["gpc"], b_gpc + fc - 1)
                eng.dma_start(out=xb[:, (fc % 2) * TG:(fc % 2 + 1) * TG],
                              in_=x_dram[fc * 128:(fc + 1) * 128, g * TG:(g + 1) * TG]
                              ).then_inc(sems["ld" + str(fc % 2)], 16)
        c.ld0 += 16 * (FC // 2)
        c.ld1 += 16 * (FC // 2)

        def scalar(eng):
            for fc in range(FC):
                if fc >= 2:
                    eng.wait_ge(sems["dve"], b_dve + 2 * (fc - 1))
                eng.dma_start(out=yb[:, (fc % 2) * TG:(fc % 2 + 1) * TG],
                              in_=y_dram[fc * 128:(fc + 1) * 128, g * TG:(g + 1) * TG]
                              ).then_inc(sems["sc" + str(fc % 2)], 16)
        c.sc0 += 16 * (FC // 2)
        c.sc1 += 16 * (FC // 2)

        def vector(eng):
            for fc in range(FC):
                eng.wait_ge(sems["sc" + str(fc % 2)],
                            b_sc[fc % 2] + 16 * (fc // 2 + 1))
                if fc >= 2:
                    eng.wait_ge(sems["gp" + str(fc % 2)],
                                b_gp[fc % 2] + 16 * (fc // 2))
                eng.tensor_tensor(tb[:, (fc % 2) * TG:(fc % 2 + 1) * TG],
                                  yb[:, (fc % 2) * TG:(fc % 2 + 1) * TG], m2B,
                                  ALU.subtract).then_inc(sems["dve"], 1)
                eng.wait_ge(sems["dve"], b_dve + 2 * fc + 1)
                eng.tensor_tensor(tb[:, (fc % 2) * TG:(fc % 2 + 1) * TG],
                                  tb[:, (fc % 2) * TG:(fc % 2 + 1) * TG], s2B,
                                  ALU.mult).then_inc(sems["dve"], 1)
        c.dve += 2 * FC

        def gpsimd(eng):
            for fc in range(FC):
                eng.wait_ge(sems["dve"], b_dve + 2 * (fc + 1))
                eng.wait_ge(sems["ld" + str(fc % 2)],
                            b_ld[fc % 2] + 16 * (fc // 2 + 1))
                eng.tensor_tensor(tb[:, (fc % 2) * TG:(fc % 2 + 1) * TG],
                                  tb[:, (fc % 2) * TG:(fc % 2 + 1) * TG],
                                  xb[:, (fc % 2) * TG:(fc % 2 + 1) * TG],
                                  ALU.add).then_inc(sems["gpc"], 1)
                eng.wait_ge(sems["gpc"], b_gpc + fc + 1)
                eng.dma_start(out=out_dram[fc * 128:(fc + 1) * 128, g * TG:(g + 1) * TG],
                              in_=tb[:, (fc % 2) * TG:(fc % 2 + 1) * TG]
                              ).then_inc(sems["gp" + str(fc % 2)], 16)
        c.gpc += FC
        c.gp0 += 16 * (FC // 2)
        c.gp1 += 16 * (FC // 2)

        return {"sync": sync, "scalar": scalar, "vector": vector, "gpsimd": gpsimd}

    if defer:
        return body
    ctx.phase(body)
    ctx.A.reset(mk)


def emit_post_stats(ctx, y, rows, m2B, s2B, defer=False):
    c, sems, small, psum = ctx.c, ctx.sems, ctx.small, ctx.psum
    mk = ctx.A.mark()
    sq = ctx.A.alloc(2 * TG)
    m_row = rows[0:1, 0:TG]
    v_row = rows[0:1, TG:2 * TG]
    t_row = rows[0:1, 2 * TG:3 * TG]
    s_row = v_row

    def body(block):
        b_pe, b_act, b_dve = c.pe, c.act, c.dve

        def scalar(eng):
            for fc in range(FC):
                if fc >= 2:
                    eng.wait_ge(sems["pe"], b_pe + 2 * (fc - 1))
                eng.activation(sq[:, (fc % 2) * TG:(fc % 2 + 1) * TG],
                               y[:, fc * TG:(fc + 1) * TG],
                               AF.Square).then_inc(sems["act"], 1)
            eng.wait_ge(sems["dve"], b_dve + 4)
            eng.activation(s_row, v_row, AF.Sqrt, bias=small["c_eps"]
                           ).then_inc(sems["act"], 1)
            eng.wait_ge(sems["pe"], b_pe + 2 * FC + 2)
            eng.activation(m2B, psum[4][:, :TG], AF.Copy)
            eng.activation(s2B, psum[5][:, :TG], AF.Copy).then_inc(sems["act"], 2)
        c.act += FC + 3

        def tensor(eng):
            for fc in range(FC):
                eng.wait_ge(sems["act"], b_act + fc + 1)
                eng.matmul(psum[2][0:1, :TG], small["c_onescol"],
                           y[:, fc * TG:(fc + 1) * TG],
                           start=(fc == 0), stop=(fc == FC - 1))
                eng.matmul(psum[3][0:1, :TG], small["c_onescol"],
                           sq[:, (fc % 2) * TG:(fc % 2 + 1) * TG],
                           start=(fc == 0), stop=(fc == FC - 1)).then_inc(sems["pe"], 2)
            eng.wait_ge(sems["dve"], b_dve + 5)
            eng.matmul(psum[4][:, :TG], small["c_onesrow"], m_row, start=True, stop=True)
            eng.matmul(psum[5][:, :TG], small["c_onesrow"], s_row,
                       start=True, stop=True).then_inc(sems["pe"], 2)
        c.pe += 2 * FC + 2

        def vector(eng):
            eng.wait_ge(sems["pe"], b_pe + 2 * FC)
            _row_math2(eng, sems, c, b_dve, b_act, psum[2], psum[3],
                       m_row, v_row, t_row, s_row, FC)
        c.dve += 5

        return {"scalar": scalar, "tensor": tensor, "vector": vector}

    if defer:
        return body
    ctx.phase(body)
    ctx.A.reset(mk)


def emit_resid_sbufy(ctx, x_dram, g, y, m2B, s2B, out_dram, defer=False):
    c, sems = ctx.c, ctx.sems
    mk = ctx.A.mark()
    xb = ctx.A.alloc(2 * TG)
    tb = ctx.A.alloc(2 * TG)

    def body(block):
        b_ld = [c.ld0, c.ld1]
        b_dve, b_gpc = c.dve, c.gpc

        def sync(eng):
            for fc in range(FC):
                if fc >= 2:
                    eng.wait_ge(sems["gpc"], b_gpc + fc - 1)
                eng.dma_start(out=xb[:, (fc % 2) * TG:(fc % 2 + 1) * TG],
                              in_=x_dram[fc * 128:(fc + 1) * 128, g * TG:(g + 1) * TG]
                              ).then_inc(sems["ld" + str(fc % 2)], 16)
        c.ld0 += 16 * (FC // 2)
        c.ld1 += 16 * (FC // 2)

        def vector(eng):
            for fc in range(FC):
                if fc >= 2:
                    eng.wait_ge(sems["gpc"], b_gpc + fc - 1)
                eng.tensor_tensor(tb[:, (fc % 2) * TG:(fc % 2 + 1) * TG],
                                  y[:, fc * TG:(fc + 1) * TG], m2B, ALU.subtract
                                  ).then_inc(sems["dve"], 1)
                eng.wait_ge(sems["dve"], b_dve + 2 * fc + 1)
                eng.tensor_tensor(tb[:, (fc % 2) * TG:(fc % 2 + 1) * TG],
                                  tb[:, (fc % 2) * TG:(fc % 2 + 1) * TG], s2B,
                                  ALU.mult).then_inc(sems["dve"], 1)
        c.dve += 2 * FC

        def gpsimd(eng):
            for fc in range(FC):
                eng.wait_ge(sems["dve"], b_dve + 2 * (fc + 1))
                eng.wait_ge(sems["ld" + str(fc % 2)],
                            b_ld[fc % 2] + 16 * (fc // 2 + 1))
                eng.tensor_tensor(y[:, fc * TG:(fc + 1) * TG],
                                  tb[:, (fc % 2) * TG:(fc % 2 + 1) * TG],
                                  xb[:, (fc % 2) * TG:(fc % 2 + 1) * TG],
                                  ALU.add).then_inc(sems["gpc"], 1)
                eng.wait_ge(sems["gpc"], b_gpc + fc + 1)
                eng.dma_start(out=out_dram[fc * 128:(fc + 1) * 128, g * TG:(g + 1) * TG],
                              in_=y[:, fc * TG:(fc + 1) * TG]).then_inc(sems["gpd"], 16)
        c.gpc += FC
        c.gpd += 16 * FC

        return {"sync": sync, "vector": vector, "gpsimd": gpsimd}

    if defer:
        return body
    ctx.phase(body)
    ctx.A.reset(mk)


def emit_post_ln_residual(ctx, x_dram, g, y, rows, out_dram):
    mk = ctx.A.mark()
    m2B = ctx.A.alloc(TG)
    s2B = ctx.A.alloc(TG)
    emit_post_stats(ctx, y, rows, m2B, s2B)
    emit_resid_sbufy(ctx, x_dram, g, y, m2B, s2B, out_dram)
    ctx.A.reset(mk)


def emit_proj_sweep(ctx, xhat, wt_dram, n_out_chunks, out_cb, bias=None,
                    func=None, extra_dram=None, extra_rhs=None, defer=False,
                    w_bf16=False):
    c, sems, psum = ctx.c, ctx.sems, ctx.psum
    mk = ctx.A.mark()
    mkb = ctx.AB.mark()
    wcols = FC * 128 + (128 if extra_dram is not None else 0)
    wb = [ctx.AB.alloc(wcols) for _ in range(2)]
    ws = None if w_bf16 else [ctx.A.alloc(wcols) for _ in range(2)]
    NOC = n_out_chunks
    PEO = FC + (1 if extra_dram is not None else 0)
    NDMA = 1 + (1 if extra_dram is not None else 0)

    def body(block):
        b_ld = [c.ld0, c.ld1]
        b_pe, b_act, b_dve = c.pe, c.act, c.dve

        def sync(eng):
            for oc in range(NOC):
                dst = wb[oc % 2] if w_bf16 else ws[oc % 2]
                if oc >= 2:
                    if w_bf16:
                        eng.wait_ge(sems["pe"], b_pe + (oc - 1) * PEO)
                    else:
                        eng.wait_ge(sems["dve"], b_dve + oc - 1)
                eng.dma_start(
                    out=dst[:, 0:FC * 128]
                        .rearrange("p (kc n) -> p kc n", kc=FC),
                    in_=wt_dram[:, oc * 128:(oc + 1) * 128]
                        .rearrange("(kc p) n -> p kc n", p=128),
                ).then_inc(sems["ld" + str(oc % 2)], 16)
                if extra_dram is not None:
                    eng.dma_start(
                        out=dst[:, FC * 128:FC * 128 + 128],
                        in_=extra_dram[:, oc * 128:(oc + 1) * 128],
                    ).then_inc(sems["ld" + str(oc % 2)], 16)
        c.ld0 += 16 * NDMA * ((NOC + 1) // 2)
        c.ld1 += 16 * NDMA * (NOC // 2)

        def vector(eng):
            for oc in range(NOC):
                eng.wait_ge(sems["ld" + str(oc % 2)],
                            b_ld[oc % 2] + 16 * NDMA * (oc // 2 + 1))
                if oc >= 2:
                    eng.wait_ge(sems["pe"], b_pe + (oc - 1) * PEO)
                eng.tensor_copy(wb[oc % 2], ws[oc % 2]).then_inc(sems["dve"], 1)
        if not w_bf16:
            c.dve += NOC

        def tensor(eng):
            for oc in range(NOC):
                if w_bf16:
                    eng.wait_ge(sems["ld" + str(oc % 2)],
                                b_ld[oc % 2] + 16 * NDMA * (oc // 2 + 1))
                else:
                    eng.wait_ge(sems["dve"], b_dve + oc + 1)
                if oc >= 2:
                    eng.wait_ge(sems["act"], b_act + oc - 1)
                for kc in range(FC):
                    eng.matmul(psum[oc % 2][:, :TG],
                               wb[oc % 2][:, kc * 128:(kc + 1) * 128],
                               xhat[:, kc * TG:(kc + 1) * TG],
                               start=(kc == 0),
                               stop=(kc == FC - 1 and extra_dram is None)
                               ).then_inc(sems["pe"], 1)
                if extra_dram is not None:
                    eng.matmul(psum[oc % 2][:, :TG],
                               wb[oc % 2][:, FC * 128:FC * 128 + 128],
                               extra_rhs,
                               start=False, stop=True).then_inc(sems["pe"], 1)
        c.pe += NOC * PEO

        def scalar(eng):
            for oc in range(NOC):
                eng.wait_ge(sems["pe"], b_pe + (oc + 1) * PEO)
                f = func.get(oc, AF.Copy) if func else AF.Copy
                kw = {}
                if bias is not None:
                    kw["bias"] = bias[:, oc:oc + 1]
                    f = AF.Identity if f == AF.Copy else f
                eng.activation(out_cb(oc), psum[oc % 2][:, :TG], f, **kw
                               ).then_inc(sems["act"], 1)
        c.act += NOC

        secs = {"sync": sync, "tensor": tensor, "scalar": scalar}
        if not w_bf16:
            secs["vector"] = vector
        return secs

    if defer:
        return body
    ctx.phase(body)
    ctx.A.reset(mk)
    ctx.AB.reset(mkb)


def emit_l0a_prep(ctx):
    c, sems, small, psum = ctx.c, ctx.sems, ctx.small, ctx.psum
    io, dr = ctx.io, ctx.dr

    bm01 = ctx.A.alloc(128)
    mk = ctx.A.mark()
    o32a = ctx.A.alloc(FC * 128)
    mtile = ctx.A.alloc(2 * 512)
    mtile_f = ctx.A.alloc(2 * 512)
    t1 = ctx.A.alloc(512)
    tf = ctx.A.alloc(256 + 128 + 64 + 32)

    def body(block):
        b_ld = [c.ld0, c.ld1]
        b_lda = c.lda
        b_pe, b_dve, b_act = c.pe, c.dve, c.act

        def sync(eng):
            eng.dma_start(out=o32a, in_=io["c_o32a"]).then_inc(sems["lda"], 16)
            i = 0
            for kb in range(8):
                for ncc in range(FC):
                    if i >= 2:
                        j = i - 2
                        eng.wait_ge(sems["dve"], b_dve + (j + 1) + 6 * (j // FC))
                    eng.dma_start(
                        out=mtile[:, (i % 2) * 512:(i % 2 + 1) * 512].bitcast(I32),
                        in_=io["w_mask"][ncc * 128:(ncc + 1) * 128,
                                         kb * 512:(kb + 1) * 512],
                    ).then_inc(sems["ld" + str(i % 2)], 16)
                    i += 1
        c.lda += 16
        c.ld0 += 16 * (8 * FC // 2)
        c.ld1 += 16 * (8 * FC // 2)

        def vector(eng):
            i = 0
            for kb in range(8):
                for ncc in range(FC):
                    eng.wait_ge(sems["ld" + str(i % 2)],
                                b_ld[i % 2] + 16 * (i // 2 + 1))
                    if i >= 2:
                        eng.wait_ge(sems["pe"], b_pe + i - 1)
                    eng.tensor_copy(mtile_f[:, (i % 2) * 512:(i % 2 + 1) * 512],
                                    mtile[:, (i % 2) * 512:(i % 2 + 1) * 512]
                                    .bitcast(I32)).then_inc(sems["dve"], 1)
                    i += 1
                eng.wait_ge(sems["act"], b_act + kb + 1)
                dvn = b_dve + (kb + 1) * FC + 6 * kb
                srcs = [t1, tf[:, 0:256], tf[:, 256:384], tf[:, 384:448],
                        tf[:, 448:480]]
                for st in range(5):
                    h = 16 >> st
                    a3 = srcs[st].rearrange("p (k s) -> p k s", s=2 * h)
                    if st > 0:
                        eng.wait_ge(sems["dve"], dvn)
                    if st < 4:
                        d3 = srcs[st + 1].rearrange("p (k s) -> p k s", s=h)
                        eng.tensor_tensor(d3, a3[:, :, 0:h], a3[:, :, h:2 * h],
                                          ALU.add).then_inc(sems["dve"], 1)
                    else:
                        eng.tensor_tensor(bm01[:, kb * 16:(kb + 1) * 16],
                                          a3[:, :, 0:1].squeeze(2),
                                          a3[:, :, 1:2].squeeze(2), ALU.add
                                          ).then_inc(sems["dve"], 1)
                    dvn += 1
                eng.wait_ge(sems["dve"], dvn)
                eng.tensor_scalar(bm01[:, kb * 16:(kb + 1) * 16],
                                  bm01[:, kb * 16:(kb + 1) * 16],
                                  0.0, None, ALU.is_gt).then_inc(sems["dve"], 1)
        c.dve += 8 * FC + 8 * 6

        def tensor(eng):
            eng.wait_ge(sems["lda"], b_lda + 16)
            i = 0
            for kb in range(8):
                if kb >= 1:
                    eng.wait_ge(sems["act"], b_act + kb)
                for ncc in range(FC):
                    eng.wait_ge(sems["dve"], b_dve + i + 1 + 6 * kb)
                    eng.matmul(psum[2][:, :], o32a[:, ncc * 128:(ncc + 1) * 128],
                               mtile_f[:, (i % 2) * 512:(i % 2 + 1) * 512],
                               start=(ncc == 0), stop=(ncc == FC - 1)
                               ).then_inc(sems["pe"], 1)
                    i += 1
        c.pe += 8 * FC

        def scalar(eng):
            for kb in range(8):
                eng.wait_ge(sems["pe"], b_pe + (kb + 1) * FC)
                if kb >= 1:
                    eng.wait_ge(sems["dve"], b_dve + 38 * kb)
                eng.activation(t1, psum[2][:, :], AF.Copy).then_inc(sems["act"], 1)
        c.act += 8

        return {"sync": sync, "vector": vector, "tensor": tensor, "scalar": scalar}

    ctx.phase(body)
    ctx.A.reset(mk)
    if DEBUG_TAP == "bm01":
        emit_tap_sbuf(ctx, bm01)
        return
    if DEBUG_STOP == "bm":
        return

    mk = ctx.A.mark()
    mkb = ctx.AB.mark()
    rsel = ctx.A.alloc(FC * 128)
    xex = ctx.A.alloc(D)
    wc = [ctx.A.alloc(D) for _ in range(2)]
    e1t = ctx.A.alloc(128)
    bmE = ctx.A.alloc(D)
    wm = ctx.A.alloc(D)
    tb = [ctx.AB.alloc(512) for _ in range(2)]

    def body2(block):
        b_ld = [c.ld0, c.ld1]
        b_lda = c.lda
        b_gp = [c.gp0, c.gp1]
        b_pe, b_dve, b_act = c.pe, c.dve, c.act

        def sync(eng):
            eng.dma_start(out=rsel, in_=io["c_rsel"]).then_inc(sems["lda"], 16)
            eng.dma_start(out=xex, in_=io["c_xex"]).then_inc(sems["lda"], 16)
            for ncc in range(FC):
                if ncc >= 2:
                    eng.wait_ge(sems["dve"], b_dve + ncc - 1)
                eng.dma_start(out=wc[ncc % 2],
                              in_=io["w_weight"][ncc * 128:(ncc + 1) * 128, :]
                              ).then_inc(sems["ld" + str(ncc % 2)], 16)
        c.lda += 32
        c.ld0 += 16 * (FC // 2)
        c.ld1 += 16 * (FC // 2)

        def tensor(eng):
            eng.wait_ge(sems["lda"], b_lda + 32)
            for ncc in range(FC):
                pe0 = b_pe + ncc * 41
                act0 = b_act + ncc * 17
                eng.wait_ge(sems["ld" + str(ncc % 2)],
                            b_ld[ncc % 2] + 16 * (ncc // 2 + 1))
                if ncc >= 1:
                    eng.wait_ge(sems["act"], act0)
                eng.matmul(psum[2][:, 0:128], bm01,
                           rsel[:, ncc * 128:(ncc + 1) * 128],
                           start=True, stop=True).then_inc(sems["pe"], 1)
                for kb in range(8):
                    eng.wait_ge(sems["act"], act0 + 1)
                    if kb >= 2:
                        eng.wait_ge(sems["act"], act0 + kb)
                    eng.matmul(psum[3 + kb % 2][:, :], e1t,
                               xex[:, kb * 512:(kb + 1) * 512],
                               start=True, stop=True).then_inc(sems["pe"], 1)
                eng.wait_ge(sems["dve"], b_dve + ncc + 1)
                for sub in range(FC):
                    g = sub // 4
                    if g >= 2 and sub % 4 == 0:
                        eng.wait_ge(sems["act"], act0 + 9 + g - 1)
                    eng.transpose(psum[g % 2][:, (sub % 4) * 128:(sub % 4 + 1) * 128],
                                  wm[:, sub * 128:(sub + 1) * 128],
                                  small["c_ident"]).then_inc(sems["pe"], 1)
        c.pe += FC * 41

        def scalar(eng):
            for ncc in range(FC):
                pe0 = b_pe + ncc * 41
                eng.wait_ge(sems["pe"], pe0 + 1)
                eng.activation(e1t, psum[2][:, 0:128], AF.Copy).then_inc(sems["act"], 1)
                for kb in range(8):
                    eng.wait_ge(sems["pe"], pe0 + 1 + kb + 1)
                    eng.activation(bmE[:, kb * 512:(kb + 1) * 512],
                                   psum[3 + kb % 2][:, :], AF.Copy
                                   ).then_inc(sems["act"], 1)
                for g in range(8):
                    eng.wait_ge(sems["pe"], pe0 + 9 + (g + 1) * 4)
                    sg = ncc * 8 + g
                    if sg >= 2:
                        eng.wait_ge(sems["gp" + str(sg % 2)],
                                    b_gp[sg % 2] + 64 * (sg // 2))
                    eng.activation(tb[g % 2], psum[g % 2][:, :], AF.Copy
                                   ).then_inc(sems["act"], 1)
        c.act += FC * 17

        def vector(eng):
            for ncc in range(FC):
                act0 = b_act + ncc * 17
                eng.wait_ge(sems["act"], act0 + 9)
                if ncc >= 1:
                    eng.wait_ge(sems["pe"], b_pe + ncc * 41)
                eng.tensor_tensor(wm, wc[ncc % 2], bmE, ALU.mult).then_inc(sems["dve"], 1)
        c.dve += FC

        def gpsimd(eng):
            for ncc in range(FC):
                act0 = b_act + ncc * 17
                for g in range(8):
                    sg = ncc * 8 + g
                    eng.wait_ge(sems["act"], act0 + 9 + g + 1)
                    for q in range(4):
                        sub = g * 4 + q
                        eng.dma_start(
                            out=dr["wtdram"][sub * 128:(sub + 1) * 128,
                                             ncc * 128:(ncc + 1) * 128],
                            in_=tb[g % 2][:, q * 128:(q + 1) * 128],
                        ).then_inc(sems["gp" + str(sg % 2)], 16)
        c.gp0 += 64 * (FC * 8 // 2)
        c.gp1 += 64 * (FC * 8 // 2)

        return {"sync": sync, "tensor": tensor, "scalar": scalar,
                "vector": vector, "gpsimd": gpsimd}

    ctx.phase(body2)
    ctx.A.reset(mk)
    ctx.AB.reset(mkb)
    if DEBUG_TAP == "wt":
        emit_tap_dram(ctx, dr["wtdram"], D, TOK)


def emit_l0a(ctx):
    c = ctx.c
    io, dr, sems = ctx.io, ctx.dr, ctx.sems
    mk_pre = ctx.A.mark()
    emit_l0a_prep(ctx)
    ctx.A.reset(mk_pre)
    if DEBUG_STOP in ("prep", "bm") or DEBUG_TAP in ("bm01", "wt"):
        return

    mk0 = ctx.A.mark()
    mkb0 = ctx.AB.mark()
    rows = ctx.A.alloc(3 * TG)[0:1, :]
    biasb = ctx.A.alloc(FC)
    mB = ctx.A.alloc(TG)
    sB = ctx.A.alloc(TG)
    xhat = ctx.AB.alloc(FC * TG)
    y = ctx.A.alloc(FC * TG)

    def body_bias(block):
        def sync(eng):
            for fcb in range(FC):
                eng.dma_start(out=biasb[:, fcb:fcb + 1],
                              in_=io["w_bias"][fcb * 128:(fcb + 1) * 128, :]
                              ).then_inc(sems["lda"], 16)
        c.lda += 16 * FC
        return {"sync": sync}
    ctx.phase(body_bias)

    m2B = ctx.A.alloc(TG)
    s2B = ctx.A.alloc(TG)
    emit_ln_stats_dram(ctx, dr["xdram"], 0, mB, sB, rows)
    emit_ln_apply_dram(ctx, dr["xdram"], 0, mB, sB, xhat)
    for g in range(NG):
        mk = ctx.A.mark()
        mkb = ctx.AB.mark()
        bodies = [emit_proj_sweep(ctx, xhat, dr["wtdram"], FC,
                                  lambda oc: y[:, oc * TG:(oc + 1) * TG],
                                  bias=biasb, defer=True, w_bf16=True)]
        if g + 1 < NG:
            bodies.append(emit_ln_stats_dram(ctx.alt(), dr["xdram"], g + 1, mB,
                                             sB, rows, defer=True, slim=True))
        ctx.phases(bodies)
        ctx.A.reset(mk)
        ctx.AB.reset(mkb)
        emit_post_stats(ctx, y, rows, m2B, s2B)
        mk = ctx.A.mark()
        bodies = [emit_resid_sbufy(ctx, dr["xdram"], g, y, m2B, s2B,
                                   dr["xdram"], defer=True)]
        if g + 1 < NG:
            bodies.append(emit_ln_apply_dram(ctx.alt(), dr["xdram"], g + 1, mB,
                                             sB, xhat, defer=True))
        ctx.phases(bodies)
        ctx.A.reset(mk)
    ctx.A.reset(mk0)
    ctx.AB.reset(mkb0)


def emit_l0f(ctx):
    c = ctx.c
    io, dr, sems, small, psum = ctx.io, ctx.dr, ctx.sems, ctx.small, ctx.psum

    mk0 = ctx.A.mark()
    mkb0 = ctx.AB.mark()
    rows = ctx.A.alloc(3 * TG)[0:1, :]
    mB = ctx.A.alloc(TG)
    sB = ctx.A.alloc(TG)
    idxb = ctx.A.alloc(FC)
    xhat = ctx.AB.alloc(FC * TG)

    def body_idx(block):
        def sync(eng):
            for ncc in range(FC):
                eng.dma_start(out=idxb[:, ncc:ncc + 1].bitcast(I32),
                              in_=io["w_perm"][ncc * 128:(ncc + 1) * 128, :]
                              ).then_inc(sems["lda"], 16)
        c.lda += 16 * FC
        return {"sync": sync}
    ctx.phase(body_idx)

    m2B = ctx.A.alloc(TG)
    s2B = ctx.A.alloc(TG)
    emit_ln_stats_dram(ctx, dr["xdram"], 0, mB, sB, rows)
    emit_ln_apply_dram(ctx, dr["xdram"], 0, mB, sB, xhat,
                       store_dram=GroupedDram(dr["sbdram"], 0))
    for g in range(NG):
        mk = ctx.A.mark()
        mkb = ctx.AB.mark()
        bodies = [_emit_l0f_heads(ctx, g, idxb, defer=True)]
        if g + 1 < NG:
            bodies.append(emit_ln_stats_dram(ctx.alt(), dr["xdram"], g + 1, mB,
                                             sB, rows, psb=(4, 5, 4, 5), defer=True,
                                             slim=True))
        ctx.phases(bodies)
        ctx.A.reset(mk)
        ctx.AB.reset(mkb)
        emit_ln_stats_dram(ctx, GroupedDram(dr["s2dram"], g), g, m2B, s2B, rows)
        mk = ctx.A.mark()
        bodies = [emit_residual_from_dram(ctx, dr["xdram"],
                                          GroupedDram(dr["s2dram"], g),
                                          g, m2B, s2B, dr["xdram"], defer=True)]
        if g + 1 < NG:
            bodies.append(emit_ln_apply_dram(
                ctx.alt(), dr["xdram"], g + 1, mB, sB, xhat,
                store_dram=GroupedDram(dr["sbdram"], g + 1), defer=True))
        ctx.phases(bodies)
        ctx.A.reset(mk)
    ctx.A.reset(mk0)
    ctx.AB.reset(mkb0)


def _emit_l0f_heads(ctx, g, idxb, defer=False):
    c, sems, small, psum = ctx.c, ctx.sems, ctx.small, ctx.psum
    io, dr = ctx.io, ctx.dr
    mk = ctx.A.mark()
    xg = [ctx.AB.alloc(2 * TG) for _ in range(2)]
    hid = ctx.AB.alloc(8 * TG)
    yh = [ctx.A.alloc(2 * TG) for _ in range(2)]
    wub = [ctx.A.alloc(2 * FM) for _ in range(2)]
    wdb = [ctx.A.alloc(8 * FD) for _ in range(2)]
    wubB = [ctx.AB.alloc(2 * FM) for _ in range(2)]
    wdbB = [ctx.AB.alloc(8 * FD) for _ in range(2)]

    def body(block):
        b_ld = [c.ld0, c.ld1]
        b_sc = [c.sc0, c.sc1]
        b_gp = [c.gp0, c.gp1, c.gp2, c.gp3]
        b_pe, b_act, b_gpc, b_dve = c.pe, c.act, c.gpc, c.dve

        def sync(eng):
            for h in range(FH):
                if h >= 2:
                    eng.wait_ge(sems["dve"], b_dve + 2 * (h - 2) + 1)
                eng.dma_start(
                    out=wub[h % 2].rearrange("p (kc m) -> p kc m", kc=2),
                    in_=io["w_up"][h * 256:(h + 1) * 256, :]
                        .rearrange("(kc p) m -> p kc m", p=128),
                ).then_inc(sems["ld" + str(h % 2)], 16)
        c.ld0 += 16 * (FH // 2)
        c.ld1 += 16 * (FH // 2)

        def vector(eng):
            for h in range(FH):
                eng.wait_ge(sems["ld" + str(h % 2)],
                            b_ld[h % 2] + 16 * (h // 2 + 1))
                if h >= 2:
                    eng.wait_ge(sems["pe"], b_pe + (h - 1) * 32)
                eng.tensor_copy(wubB[h % 2], wub[h % 2]).then_inc(sems["dve"], 1)
                eng.wait_ge(sems["sc" + str(h % 2)],
                            b_sc[h % 2] + 16 * (h // 2 + 1))
                eng.tensor_copy(wdbB[h % 2], wdb[h % 2]).then_inc(sems["dve"], 1)
        c.dve += 2 * FH

        def scalar_dma_and_act(eng):
            for h in range(FH):
                if h >= 2:
                    eng.wait_ge(sems["dve"], b_dve + 2 * (h - 2) + 2)
                eng.dma_start(
                    out=wdb[h % 2].rearrange("p (mc d) -> p mc d", mc=8),
                    in_=io["w_down"][h * 1024:(h + 1) * 1024, :]
                        .rearrange("(mc p) d -> p mc d", p=128),
                ).then_inc(sems["sc" + str(h % 2)], 16)
                pe0 = b_pe + h * 32
                act0 = b_act + h * 10
                for mo in range(8):
                    eng.wait_ge(sems["pe"], pe0 + 2 * (mo + 1))
                    eng.activation(hid[:, mo * TG:(mo + 1) * TG],
                                   psum[mo % 2][:, :TG], AF.Gelu
                                   ).then_inc(sems["act"], 1)
                for oc in range(2):
                    eng.wait_ge(sems["pe"], pe0 + 16 + 8 * (oc + 1))
                    if h >= 2 and oc == 0:
                        eng.wait_ge(sems["gp" + str(2 + h % 2)],
                                    b_gp[2 + h % 2] + 32 * (h // 2))
                    eng.activation(yh[h % 2][:, oc * TG:(oc + 1) * TG],
                                   psum[2 + oc][:, :TG], AF.Copy
                                   ).then_inc(sems["act"], 1)
        c.sc0 += 16 * (FH // 2)
        c.sc1 += 16 * (FH // 2)
        c.act += FH * 10

        def tensor(eng):
            for h in range(FH):
                pe0 = b_pe + h * 32
                act0 = b_act + h * 10
                eng.wait_ge(sems["dve"], b_dve + 2 * h + 1)
                eng.wait_ge(sems["gp" + str(h % 2)],
                            b_gp[h % 2] + 32 * (h // 2) + 32)
                if h >= 1:
                    eng.wait_ge(sems["act"], act0)
                for mo in range(8):
                    if mo >= 2:
                        eng.wait_ge(sems["act"], act0 + mo - 1)
                    for kc in range(2):
                        eng.matmul(psum[mo % 2][:, :TG],
                                   wubB[h % 2][:, (kc * 8 + mo) * 128:
                                               (kc * 8 + mo + 1) * 128],
                                   xg[h % 2][:, kc * TG:(kc + 1) * TG],
                                   start=(kc == 0), stop=(kc == 1)
                                   ).then_inc(sems["pe"], 1)
                eng.wait_ge(sems["dve"], b_dve + 2 * h + 2)
                eng.wait_ge(sems["act"], act0 + 8)
                for oc in range(2):
                    for mc in range(8):
                        eng.matmul(psum[2 + oc][:, :TG],
                                   wdbB[h % 2][:, (mc * 2 + oc) * 128:
                                               (mc * 2 + oc + 1) * 128],
                                   hid[:, mc * TG:(mc + 1) * TG],
                                   start=(mc == 0), stop=(mc == 7)
                                   ).then_inc(sems["pe"], 1)
        c.pe += FH * 32

        def gpsimd(eng):
            for h in range(FH):
                act0 = b_act + h * 10
                if h >= 2:
                    eng.wait_ge(sems["pe"], b_pe + (h - 1) * 32)
                for kc in range(2):
                    eng.indirect_dma_start(
                        out=xg[h % 2][:, kc * TG:(kc + 1) * TG],
                        out_offset=None,
                        in_=dr["sbdram"],
                        in_offset=bass.IndirectOffsetOnAxis(
                            idxb[:, 2 * h + kc:2 * h + kc + 1].bitcast(I32), 0),
                        element_offset=g * D * TG,
                    ).then_inc(sems["gp" + str(h % 2)], 16)
                eng.wait_ge(sems["act"], act0 + 10)
                for oc in range(2):
                    eng.indirect_dma_start(
                        out=dr["s2dram"],
                        out_offset=bass.IndirectOffsetOnAxis(
                            idxb[:, 2 * h + oc:2 * h + oc + 1].bitcast(I32), 0),
                        in_=yh[h % 2][:, oc * TG:(oc + 1) * TG],
                        in_offset=None,
                        element_offset=g * D * TG,
                    ).then_inc(sems["gp" + str(2 + h % 2)], 16)
        c.gp0 += 32 * (FH // 2)
        c.gp1 += 32 * (FH // 2)
        c.gp2 += 32 * (FH // 2)
        c.gp3 += 32 * (FH // 2)

        return {"sync": sync, "scalar": scalar_dma_and_act, "tensor": tensor,
                "vector": vector, "gpsimd": gpsimd}

    if defer:
        return body
    ctx.phase(body)
    ctx.A.reset(mk)


def emit_load_arena(ctx, src_gview, g, dst, defer=False):
    c, sems = ctx.c, ctx.sems

    def body(block):
        b_ld = [c.ld0, c.ld1]

        def sync(eng):
            for fc in range(FC):
                eng.dma_start(out=dst[:, fc * TG:(fc + 1) * TG],
                              in_=src_gview[fc * 128:(fc + 1) * 128,
                                            g * TG:(g + 1) * TG]
                              ).then_inc(sems["ld" + str(fc % 2)], 16)
        c.ld0 += 16 * (FC // 2)
        c.ld1 += 16 * (FC // 2)
        return {"sync": sync}
    if defer:
        return body
    ctx.phase(body)


def emit_l1a(ctx):
    c = ctx.c
    io, dr, sems, small, psum = ctx.io, ctx.dr, ctx.sems, ctx.small, ctx.psum

    mk0 = ctx.A.mark()
    mkb0 = ctx.AB.mark()
    rows = ctx.A.alloc(3 * TG)[0:1, :]
    mB = ctx.A.alloc(TG)
    sB = ctx.A.alloc(TG)
    xhat = ctx.AB.alloc(FC * TG)
    mk_attn = ctx.A.mark()
    ropec = ctx.A.alloc(TOK)
    ropes = ctx.A.alloc(TOK)
    pswap = ctx.A.alloc(128)
    kposb = ctx.A.alloc(FC)
    qposb = ctx.A.alloc(TOK)[0:1, :]
    qn2 = ctx.A.alloc(TOK)[0:1, :]
    kn2 = ctx.A.alloc(8)[0:1, :]
    kfm = ctx.A.alloc(4)[0:1, 0:1]

    def body_pre(block):
        def sync(eng):
            eng.dma_start(out=ropec, in_=io["c_ropec"]).then_inc(sems["lda"], 16)
            eng.dma_start(out=ropes, in_=io["c_ropes"]).then_inc(sems["lda"], 16)
            eng.dma_start(out=pswap, in_=io["c_pswap"]).then_inc(sems["lda"], 16)
            eng.dma_start(out=qposb, in_=io["c_qpos"]).then_inc(sems["lda"], 16)
            for fcb in range(FC):
                eng.dma_start(out=kposb[:, fcb:fcb + 1],
                              in_=io["c_kpos"][fcb * 128:(fcb + 1) * 128, :]
                              ).then_inc(sems["lda"], 16)
        c.lda += 16 * (4 + FC)

        def vector(eng):
            eng.memset(kn2, -1.0e30).then_inc(sems["dve"], 1)
        c.dve += 1
        return {"sync": sync, "vector": vector}
    ctx.phase(body_pre)

    for g in range(NG):
        emit_ln_stats_dram(ctx, dr["xdram"], g, mB, sB, rows)
        emit_ln_apply_dram(ctx, dr["xdram"], g, mB, sB, xhat)
        _emit_l1a_qkv(ctx, g, xhat, ropec, ropes, pswap, qn2, kn2)

    _emit_l1a_collective(ctx, kn2, kfm)

    ctx.AB.reset(mkb0)
    cmask = ctx.AB.alloc(FC * TG)
    for g in range(NG):
        _emit_l1a_masks(ctx, g, qposb, kposb, qn2, kfm, cmask)
        for kvh in range(NKV):
            _emit_l1a_attn(ctx, g, kvh, cmask)

    ctx.A.reset(mk_attn)
    ctx.AB.reset(mkb0)
    ahat = ctx.AB.alloc(FC * TG)
    y = ctx.A.alloc(FC * TG)
    m2B = ctx.A.alloc(TG)
    s2B = ctx.A.alloc(TG)
    emit_load_arena(ctx, GroupedDram(dr["sbdram"], 0), 0, ahat)
    for g in range(NG):
        mk = ctx.A.mark()
        mkb = ctx.AB.mark()
        bodies = [emit_proj_sweep(ctx, ahat, io["w_wo"], FC,
                                  lambda oc: y[:, oc * TG:(oc + 1) * TG],
                                  defer=True)]
        ctx.phases(bodies)
        ctx.A.reset(mk)
        ctx.AB.reset(mkb)
        emit_post_stats(ctx, y, rows, m2B, s2B)
        mk = ctx.A.mark()
        bodies = [emit_resid_sbufy(ctx, dr["xdram"], g, y, m2B, s2B,
                                   dr["xdram"], defer=True)]
        if g + 1 < NG:
            bodies.append(emit_load_arena(ctx.alt(), GroupedDram(dr["sbdram"], g + 1),
                                          g + 1, ahat, defer=True))
        ctx.phases(bodies)
        ctx.A.reset(mk)
    ctx.A.reset(mk0)
    ctx.AB.reset(mkb0)


def _emit_l1a_qkv(ctx, g, xhat, ropec, ropes, pswap, qn2, kn2):
    c, sems, small, psum = ctx.c, ctx.sems, ctx.small, ctx.psum
    io, dr = ctx.io, ctx.dr
    mk = ctx.A.mark()
    mkb = ctx.AB.mark()
    wb = [ctx.A.alloc(FC * 128) for _ in range(2)]
    wbB = [ctx.AB.alloc(FC * 128) for _ in range(2)]
    t0 = ctx.A.alloc(TG)
    t1 = ctx.A.alloc(TG)
    t2 = ctx.A.alloc(TG)
    t3 = ctx.A.alloc(TG)
    tfb = [ctx.AB.alloc(TG) for _ in range(2)]
    sq = ctx.A.alloc(TG)
    nrowb = ctx.A.alloc(2 * TG)[0:1, :]
    nredb = ctx.A.alloc(4)[0:1, :]
    vt = [ctx.AB.alloc(TG) for _ in range(2)]
    Cg = ropec[:, g * TG:(g + 1) * TG]
    Sg = ropes[:, g * TG:(g + 1) * TG]
    vv = dr["ccin"].rearrange("r (h c) -> (r h) c", h=2)

    items = ([("k", i) for i in range(NKV)] + [("v", i) for i in range(NKV)]
             + [("q", i) for i in range(FC)])
    def pe_ops(kind):
        return 32 + (2 if kind in ("q", "k") else 4)

    def act_ops(kind):
        return 4 if kind in ("q", "k") else 2

    def dve_ops(kind, it):
        if kind == "k":
            return 6
        if kind == "q":
            return 5
        return 1

    pe_bases = np.cumsum([0] + [pe_ops(k) for k, _ in items]).tolist()
    act_bases = np.cumsum([0] + [act_ops(k) for k, _ in items]).tolist()
    dve_bases = np.cumsum([0] + [dve_ops(k, i) for k, i in items]).tolist()
    st_pref = []
    tot = {0: 0, 1: 0, 2: 0, 3: 0}
    st_sem = []
    for i, (kind, it) in enumerate(items):
        s = (i % 2) if kind in ("q", "k") else (2 + i % 2)
        st_pref.append(tot[s])
        st_sem.append(s)
        tot[s] += 16 if kind in ("q", "k") else 64

    def wsrc(kind, it):
        w = {"q": io["w_wq"], "k": io["w_wk"], "v": io["w_wv"]}[kind]
        return w[:, it * 128:(it + 1) * 128]

    def body(block):
        b_ld = [c.ld0, c.ld1]
        b_gp = [c.gp0, c.gp1, c.gp2, c.gp3]
        b_pe, b_act, b_dve = c.pe, c.act, c.dve

        def sync(eng):
            for i, (kind, it) in enumerate(items):
                if i >= 2:
                    eng.wait_ge(sems["dve"], b_dve + dve_bases[i - 2] + 1)
                eng.dma_start(
                    out=wb[i % 2].rearrange("p (kc n) -> p kc n", kc=FC),
                    in_=wsrc(kind, it).rearrange("(kc p) n -> p kc n", p=128),
                ).then_inc(sems["ld" + str(i % 2)], 16)
        c.ld0 += 16 * ((len(items) + 1) // 2)
        c.ld1 += 16 * (len(items) // 2)

        def tensor(eng):
            for i, (kind, it) in enumerate(items):
                pe0 = b_pe + pe_bases[i]
                act0 = b_act + act_bases[i]
                eng.wait_ge(sems["dve"], b_dve + dve_bases[i] + 1)
                if i >= 2:
                    eng.wait_ge(sems["act"], b_act + act_bases[i - 1] + 1)
                for kc in range(FC):
                    eng.matmul(psum[i % 2][:, :TG],
                               wbB[i % 2][:, kc * 128:(kc + 1) * 128],
                               xhat[:, kc * TG:(kc + 1) * TG],
                               start=(kc == 0), stop=(kc == FC - 1)
                               ).then_inc(sems["pe"], 1)
                if kind in ("q", "k"):
                    eng.wait_ge(sems["act"], act0 + 1)
                    if i >= 2:
                        eng.wait_ge(sems["act"], b_act + act_bases[i - 1] + 2)
                    eng.matmul(psum[2 + i % 2][:, :TG], pswap, t0,
                               start=True, stop=True).then_inc(sems["pe"], 1)
                    eng.wait_ge(sems["act"], act0 + 3)
                    eng.matmul(psum[4][0:1, :TG], small["c_onescol"], sq,
                               start=True, stop=True).then_inc(sems["pe"], 1)
                else:
                    eng.wait_ge(sems["act"], act0 + 1)
                    if i >= 2:
                        eng.wait_ge(sems["act"], b_act + act_bases[i - 1] + 2)
                    for q4 in range(4):
                        eng.transpose(psum[5][:, q4 * 128:(q4 + 1) * 128],
                                      t0[:, q4 * 128:(q4 + 1) * 128],
                                      small["c_ident"]).then_inc(sems["pe"], 1)
        c.pe += pe_bases[-1]

        def scalar(eng):
            for i, (kind, it) in enumerate(items):
                pe0 = b_pe + pe_bases[i]
                act0 = b_act + act_bases[i]
                dve0 = b_dve + dve_bases[i]
                eng.wait_ge(sems["pe"], pe0 + FC)
                if i >= 1:
                    eng.wait_ge(sems["dve"], b_dve + dve_bases[i])
                if kind == "q":
                    eng.activation(t0, psum[i % 2][:, :TG], AF.Identity,
                                   scale=small["c_rsqhd"]).then_inc(sems["act"], 1)
                else:
                    eng.activation(t0, psum[i % 2][:, :TG], AF.Copy
                                   ).then_inc(sems["act"], 1)
                if kind in ("q", "k"):
                    eng.wait_ge(sems["pe"], pe0 + FC + 1)
                    eng.activation(t1, psum[2 + i % 2][:, :TG], AF.Copy
                                   ).then_inc(sems["act"], 1)
                    eng.activation(sq, t0, AF.Square).then_inc(sems["act"], 1)
                    eng.wait_ge(sems["pe"], pe0 + FC + 2)
                    eng.activation(nrowb[0:1, (i % 2) * TG:(i % 2) * TG + TG],
                                   psum[4][0:1, :TG], AF.Copy
                                   ).then_inc(sems["act"], 1)
                else:
                    eng.wait_ge(sems["pe"], pe0 + FC + 4)
                    if i >= 2 and items[i - 2][0] == "v":
                        eng.wait_ge(sems["gp" + str(st_sem[i])],
                                    b_gp[st_sem[i]] + st_pref[i - 2] + 64)
                    eng.activation(vt[i % 2], psum[5][:, :TG], AF.Copy
                                   ).then_inc(sems["act"], 1)
        c.act += act_bases[-1]

        def vector(eng):
            for i, (kind, it) in enumerate(items):
                act0 = b_act + act_bases[i]
                dvn = b_dve + dve_bases[i]
                eng.wait_ge(sems["ld" + str(i % 2)],
                            b_ld[i % 2] + 16 * (i // 2 + 1))
                if i >= 2:
                    eng.wait_ge(sems["pe"], b_pe + pe_bases[i - 1])
                eng.tensor_copy(wbB[i % 2], wb[i % 2]).then_inc(sems["dve"], 1)
                dvn += 1
                if kind not in ("q", "k"):
                    continue
                nsl = nrowb[0:1, (i % 2) * TG:(i % 2) * TG + TG]
                nred = nredb[0:1, (i % 2):(i % 2) + 1]
                eng.wait_ge(sems["dve"], dvn)
                eng.wait_ge(sems["act"], act0 + 2)
                eng.tensor_tensor(t2, t0, Cg, ALU.mult).then_inc(sems["dve"], 1)
                dvn += 1
                eng.tensor_tensor(t3, t1, Sg, ALU.mult).then_inc(sems["dve"], 1)
                dvn += 1
                eng.wait_ge(sems["dve"], dvn)
                if st_pref[i] > 0:
                    eng.wait_ge(sems["gp" + str(st_sem[i])],
                                b_gp[st_sem[i]] + st_pref[i])
                eng.tensor_tensor(tfb[i % 2], t2, t3, ALU.add
                                  ).then_inc(sems["dve"], 1)
                dvn += 1
                eng.wait_ge(sems["act"], act0 + 4)
                if kind == "k":
                    eng.tensor_reduce(nred, nsl, mybir.AxisListType.X, ALU.max
                                      ).then_inc(sems["dve"], 1)
                    dvn += 1
                    eng.wait_ge(sems["dve"], dvn)
                    eng.tensor_tensor(kn2[0:1, it:it + 1], kn2[0:1, it:it + 1],
                                      nred, ALU.max).then_inc(sems["dve"], 1)
                    dvn += 1
                else:
                    if it == 0:
                        eng.tensor_copy(qn2[0:1, g * TG:(g + 1) * TG], nsl
                                        ).then_inc(sems["dve"], 1)
                    else:
                        eng.tensor_tensor(qn2[0:1, g * TG:(g + 1) * TG],
                                          qn2[0:1, g * TG:(g + 1) * TG], nsl,
                                          ALU.max).then_inc(sems["dve"], 1)
                    dvn += 1
        c.dve += dve_bases[-1]

        def gpsimd(eng):
            for i, (kind, it) in enumerate(items):
                act0 = b_act + act_bases[i]
                dve0 = b_dve + dve_bases[i]
                if kind == "q":
                    eng.wait_ge(sems["dve"], dve0 + 4)
                    eng.dma_start(out=dr["qdram"][it * 128:(it + 1) * 128,
                                                  g * TG:(g + 1) * TG],
                                  in_=tfb[i % 2]).then_inc(sems["gp" + str(i % 2)], 16)
                elif kind == "k":
                    eng.wait_ge(sems["dve"], dve0 + 4)
                    eng.dma_start(out=dr["ccin"][it * 128:(it + 1) * 128,
                                                 g * TG:(g + 1) * TG],
                                  in_=tfb[i % 2]).then_inc(sems["gp" + str(i % 2)], 16)
                else:
                    eng.wait_ge(sems["act"], act0 + 2)
                    for q4 in range(4):
                        vrow = 2 * 1024 + g * TG + q4 * 128
                        eng.dma_start(
                            out=vv[vrow:vrow + 128, it * 128:(it + 1) * 128],
                            in_=vt[i % 2][:, q4 * 128:(q4 + 1) * 128],
                        ).then_inc(sems["gp" + str(2 + i % 2)], 16)
        c.gp0 += tot[0]
        c.gp1 += tot[1]
        c.gp2 += tot[2]
        c.gp3 += tot[3]

        return {"sync": sync, "tensor": tensor, "scalar": scalar,
                "vector": vector, "gpsimd": gpsimd}

    ctx.phase(body)
    ctx.A.reset(mk)
    ctx.AB.reset(mkb)


def _emit_l1a_collective(ctx, kn2, kfm):
    c, sems = ctx.c, ctx.sems
    io, dr = ctx.io, ctx.dr
    mk = ctx.A.mark()
    mkb = ctx.AB.mark()
    knr = ctx.AB.alloc(16)[0:1, :]
    knm = ctx.AB.alloc(2)[0:1, 0:1]
    kn2b = ctx.AB.alloc(8)[0:1, :]
    zrowb = ctx.AB.alloc(TOK)[0:1, :]

    def body_cc(block):
        b_gpd, b_cc = c.gpd, c.cc
        b_dve = c.dve

        def vector(eng):
            eng.memset(zrowb, 0.0).then_inc(sems["dve"], 1)
            eng.tensor_copy(kn2b, kn2).then_inc(sems["dve"], 1)
        c.dve += 2

        def gpsimd(eng):
            eng.wait_ge(sems["dve"], b_dve + 2)
            eng.dma_start(out=dr["ccin"][2048:2049, 8:TOK], in_=zrowb[0:1, 8:TOK]
                          ).then_inc(sems["gpd"], 16)
            eng.dma_start(out=dr["ccin"][2048:2049, 0:8], in_=kn2b
                          ).then_inc(sems["gpd"], 16)
            eng.wait_ge(sems["gpd"], b_gpd + 32)
            eng.collective_compute(
                "AllGather", ALU.bypass, replica_groups=CC_GROUPS,
                ins=[dr["ccin"]], outs=[dr["ccout"]],
            ).then_inc(sems["cc"], 1)
            eng.wait_ge(sems["cc"], b_cc + 1)
        c.gpd += 32
        c.cc += 1
        return {"vector": vector, "gpsimd": gpsimd}
    ctx.phase(body_cc)

    def body_fin(block):
        b_lda = c.lda
        b_dve = c.dve
        b_act = c.act

        def sync(eng):
            eng.dma_start(out=knr[0:1, 0:8], in_=dr["ccout"][2048:2049, 0:8]
                          ).then_inc(sems["lda"], 16)
            eng.dma_start(out=knr[0:1, 8:16],
                          in_=dr["ccout"][2049 + 2048:2049 + 2049, 0:8]
                          ).then_inc(sems["lda"], 16)
        c.lda += 32

        def vector(eng):
            eng.wait_ge(sems["lda"], b_lda + 32)
            eng.tensor_tensor(knr[0:1, 0:8], knr[0:1, 0:8], knr[0:1, 8:16],
                              ALU.max).then_inc(sems["dve"], 1)
            eng.wait_ge(sems["dve"], b_dve + 1)
            eng.tensor_reduce(knm, knr[0:1, 0:8], mybir.AxisListType.X,
                              ALU.max).then_inc(sems["dve"], 1)
        c.dve += 2

        def scalar(eng):
            eng.wait_ge(sems["dve"], b_dve + 2)
            eng.activation(kfm, knm, AF.Sqrt).then_inc(sems["act"], 1)
        c.act += 1
        return {"sync": sync, "vector": vector, "scalar": scalar}
    ctx.phase(body_fin)
    ctx.A.reset(mk)
    ctx.AB.reset(mkb)


def _emit_l1a_masks(ctx, g, qposb, kposb, qn2, kfm, cmask):
    c, sems, small, psum = ctx.c, ctx.sems, ctx.small, ctx.psum
    mk = ctx.A.mark()
    qpb = ctx.A.alloc(TG)
    cB = ctx.A.alloc(TG)
    crow = ctx.A.alloc(TG)[0:1, :]
    m01 = ctx.A.alloc(TG)

    def body(block):
        b_pe, b_act, b_dve = c.pe, c.act, c.dve

        def scalar(eng):
            eng.activation(crow, qn2[0:1, g * TG:(g + 1) * TG], AF.Sqrt
                           ).then_inc(sems["act"], 1)
            eng.wait_ge(sems["pe"], b_pe + 2)
            eng.activation(qpb, psum[4][:, :TG], AF.Copy)
            eng.activation(cB, psum[5][:, :TG], AF.Copy).then_inc(sems["act"], 2)
        c.act += 3

        def vector(eng):
            eng.wait_ge(sems["act"], b_act + 1)
            eng.tensor_scalar(crow, crow, kfm, None, ALU.mult).then_inc(sems["dve"], 1)
            dvn = b_dve + 1
            eng.wait_ge(sems["act"], b_act + 3)
            for kc in range(FC):
                eng.wait_ge(sems["dve"], dvn)
                eng.tensor_scalar(m01, qpb, kposb[:, kc:kc + 1], None, ALU.is_ge
                                  ).then_inc(sems["dve"], 1)
                dvn += 1
                eng.wait_ge(sems["dve"], dvn)
                eng.tensor_scalar(m01, m01, 1.0, -NBIG, ALU.subtract, ALU.mult
                                  ).then_inc(sems["dve"], 1)
                dvn += 1
                eng.wait_ge(sems["dve"], dvn)
                eng.tensor_tensor(cmask[:, kc * TG:(kc + 1) * TG], m01, cB,
                                  ALU.subtract).then_inc(sems["dve"], 1)
                dvn += 1
        c.dve += 1 + 3 * FC

        def tensor(eng):
            eng.wait_ge(sems["dve"], b_dve + 1)
            eng.matmul(psum[4][:, :TG], small["c_onesrow"],
                       qposb[0:1, g * TG:(g + 1) * TG], start=True, stop=True)
            eng.matmul(psum[5][:, :TG], small["c_onesrow"], crow,
                       start=True, stop=True).then_inc(sems["pe"], 2)
        c.pe += 2

        return {"scalar": scalar, "vector": vector, "tensor": tensor}

    ctx.phase(body)
    ctx.A.reset(mk)


def _attn_kcs(g):
    qmax = {}
    qmax["A"] = [511, 1023, 3583, 4095][g]
    qmax["B"] = [1535, 2047, 2559, 3071][g]
    need = set()
    for pat in ("A", "B"):
        for kc in range(FC):
            if KPOS_GLOBAL[kc * 128:(kc + 1) * 128].min() <= qmax[pat]:
                need.add(kc)
    return sorted(need)


def _emit_l1a_attn(ctx, g, kvh, cmask):
    c, sems, small, psum = ctx.c, ctx.sems, ctx.small, ctx.psum
    io, dr = ctx.io, ctx.dr
    mk = ctx.A.mark()
    mkb = ctx.AB.mark()
    kh = ctx.AB.alloc(FC * 128)
    vh = ctx.AB.alloc(FC * 128)
    qh = [ctx.AB.alloc(TG) for _ in range(2)]
    sS = [ctx.A.alloc(TG) for _ in range(2)]
    pT = [ctx.AB.alloc(TG) for _ in range(2)]
    zrow = ctx.A.alloc(TG)[0:1, :]
    zb = ctx.A.alloc(TG)
    ao = [ctx.AB.alloc(TG) for _ in range(2)]
    vv = dr["ccout"].rearrange("r (h c) -> (r h) c", h=2)
    kcs = _attn_kcs(g)
    nk = len(kcs)
    PEQ, ACQ, DVQ = 3 * nk + 1, nk + 1, nk + 2

    def body(block):
        b_ld = [c.ld0, c.ld1]
        b_lda = c.lda
        b_gp = [c.gp0, c.gp1]
        b_pe, b_act, b_dve = c.pe, c.act, c.dve

        def sync(eng):
            eng.dma_start(out=kh[:, 0:2048],
                          in_=dr["ccout"][kvh * 128:(kvh + 1) * 128, :]
                          ).then_inc(sems["lda"], 16)
            eng.dma_start(out=kh[:, 2048:4096],
                          in_=dr["ccout"][2049 + kvh * 128:2049 + (kvh + 1) * 128, :]
                          ).then_inc(sems["lda"], 16)
            for r in range(2):
                vr0 = 2 * (r * 2049 + 1024)
                eng.dma_start(
                    out=vh[:, r * 2048:(r + 1) * 2048]
                        .rearrange("p (s d) -> p s d", d=128),
                    in_=vv[vr0:vr0 + 2048, kvh * 128:(kvh + 1) * 128]
                        .rearrange("(s p) d -> p s d", p=128),
                ).then_inc(sems["lda"], 16)
            for qi in range(4):
                hh = kvh * NREP + qi
                if qi >= 2:
                    eng.wait_ge(sems["pe"], b_pe + (qi - 1) * PEQ)
                eng.dma_start(out=qh[qi % 2],
                              in_=dr["qdram"][hh * 128:(hh + 1) * 128,
                                              g * TG:(g + 1) * TG]
                              ).then_inc(sems["ld" + str(qi % 2)], 16)
        c.lda += 64
        c.ld0 += 32
        c.ld1 += 32

        def tensor(eng):
            eng.wait_ge(sems["lda"], b_lda + 64)
            for qi in range(4):
                pe0 = b_pe + qi * PEQ
                act0 = b_act + qi * ACQ
                dve0 = b_dve + qi * DVQ
                eng.wait_ge(sems["ld" + str(qi % 2)],
                            b_ld[qi % 2] + 16 * (qi // 2 + 1))
                if qi >= 1:
                    eng.wait_ge(sems["dve"], dve0)
                def emit_pvz(j):
                    kcj = kcs[j]
                    eng.wait_ge(sems["act"], act0 + j + 1)
                    eng.matmul(psum[2][:, :TG], vh[:, kcj * 128:(kcj + 1) * 128],
                               pT[j % 2], start=(j == 0), stop=(j == nk - 1)
                               ).then_inc(sems["pe"], 1)
                    eng.matmul(psum[3][0:1, :TG], small["c_onescol_b"],
                               pT[j % 2], start=(j == 0), stop=(j == nk - 1)
                               ).then_inc(sems["pe"], 1)

                for j, kc in enumerate(kcs):
                    if j >= 2:
                        eng.wait_ge(sems["dve"], dve0 + j - 1)
                    elif qi >= 1:
                        eng.wait_ge(sems["dve"], dve0)
                    eng.matmul(psum[j % 2][:, :TG], kh[:, kc * 128:(kc + 1) * 128],
                               qh[qi % 2], start=True, stop=True
                               ).then_inc(sems["pe"], 1)
                    if j >= 2:
                        emit_pvz(j - 2)
                emit_pvz(nk - 2)
                emit_pvz(nk - 1)
                eng.wait_ge(sems["dve"], dve0 + nk + 1)
                eng.matmul(psum[4][:, :TG], small["c_onesrow"], zrow,
                           start=True, stop=True).then_inc(sems["pe"], 1)
        c.pe += 4 * PEQ

        def vector(eng):
            for qi in range(4):
                pe0 = b_pe + qi * PEQ
                act0 = b_act + qi * ACQ
                dvn = b_dve + qi * DVQ
                for j, kc in enumerate(kcs):
                    spos = j + 1 if j < 2 else 3 * j - 3
                    eng.wait_ge(sems["pe"], pe0 + spos)
                    if j >= 2:
                        eng.wait_ge(sems["act"], act0 + j - 1)
                    eng.tensor_tensor(sS[j % 2], psum[j % 2][:, :TG],
                                      cmask[:, kc * TG:(kc + 1) * TG], ALU.add
                                      ).then_inc(sems["dve"], 1)
                    dvn += 1
                eng.wait_ge(sems["pe"], pe0 + 3 * nk)

                eng.reciprocal(zrow, psum[3][0:1, :TG]).then_inc(sems["dve"], 1)
                dvn += 1
                eng.wait_ge(sems["act"], act0 + ACQ)
                if qi >= 2:
                    eng.wait_ge(sems["gp" + str(qi % 2)],
                                b_gp[qi % 2] + 16 * (qi // 2))
                eng.tensor_tensor(ao[qi % 2], psum[2][:, :TG], zb, ALU.mult
                                  ).then_inc(sems["dve"], 1)
                dvn += 1
        c.dve += 4 * DVQ

        def scalar(eng):
            for qi in range(4):
                pe0 = b_pe + qi * PEQ
                act0 = b_act + qi * ACQ
                dve0 = b_dve + qi * DVQ
                for j, kc in enumerate(kcs):
                    eng.wait_ge(sems["dve"], dve0 + j + 1)
                    if j >= 2:
                        eng.wait_ge(sems["pe"], pe0 + 3 * j - 1)
                    eng.activation(pT[j % 2], sS[j % 2], AF.Exp
                                   ).then_inc(sems["act"], 1)
                eng.wait_ge(sems["pe"], pe0 + PEQ)
                eng.activation(zb, psum[4][:, :TG], AF.Copy).then_inc(sems["act"], 1)
        c.act += 4 * ACQ

        def gpsimd(eng):
            for qi in range(4):
                hh = kvh * NREP + qi
                dve0 = b_dve + qi * DVQ
                eng.wait_ge(sems["dve"], dve0 + DVQ)
                eng.dma_start(
                    out=dr["sbdram"][g * D + hh * 128:g * D + (hh + 1) * 128, :],
                    in_=ao[qi % 2]).then_inc(sems["gp" + str(qi % 2)], 16)
        c.gp0 += 32
        c.gp1 += 32

        return {"sync": sync, "tensor": tensor, "vector": vector,
                "scalar": scalar, "gpsimd": gpsimd}

    ctx.phase(body)
    ctx.A.reset(mk)
    ctx.AB.reset(mkb)


def emit_l1f(ctx):
    c = ctx.c
    io, dr, sems, small, psum = ctx.io, ctx.dr, ctx.sems, ctx.small, ctx.psum

    mk0 = ctx.A.mark()
    mkb0 = ctx.AB.mark()
    rows = ctx.A.alloc(3 * TG)[0:1, :]
    mB = ctx.A.alloc(TG)
    sB = ctx.A.alloc(TG)
    bbias = ctx.A.alloc(FC)
    xhat = ctx.AB.alloc(FC * TG)
    y = ctx.A.alloc(FC * TG)
    hw = ctx.AB.alloc(TG)
    r16s = ctx.A.alloc(128)[0:8, :]
    bthrB = ctx.A.alloc(4)

    def body_pre(block):
        b_lda = c.lda
        b_pe = c.pe

        def sync(eng):
            for fcb in range(FC):
                eng.dma_start(out=bbias[:, fcb:fcb + 1],
                              in_=io["w_bbias"][fcb * 128:(fcb + 1) * 128, :]
                              ).then_inc(sems["lda"], 16)
            eng.dma_start(out=r16s, in_=io["c_r16"]).then_inc(sems["lda"], 16)
            eng.dma_start(out=bthrB[0:1, 0:1], in_=io["w_bthr"]).then_inc(sems["lda"], 16)
        c.lda += 16 * (FC + 2)

        def tensor(eng):
            eng.wait_ge(sems["lda"], b_lda + 16 * (FC + 2))
            eng.matmul(psum[2][:, 0:1], ctx.small["c_onesrow"], bthrB[0:1, 0:1],
                       start=True, stop=True).then_inc(sems["pe"], 1)
        c.pe += 1

        def scalar(eng):
            eng.wait_ge(sems["pe"], b_pe + 1)
            eng.activation(bthrB[:, 0:1], psum[2][:, 0:1], AF.Copy
                           ).then_inc(sems["act"], 1)
        c.act += 1

        return {"sync": sync, "tensor": tensor, "scalar": scalar}
    ctx.phase(body_pre)

    m2B = ctx.A.alloc(TG)
    s2B = ctx.A.alloc(TG)
    emit_ln_stats_dram(ctx, dr["xdram"], 0, mB, sB, rows)
    emit_ln_apply_dram(ctx, dr["xdram"], 0, mB, sB, xhat)
    _emit_l1f_gates(ctx, 0, xhat, hw, r16s, bthrB)
    for g in range(NG):
        mk = ctx.A.mark()
        mkb = ctx.AB.mark()
        bodies = [emit_proj_sweep(ctx, xhat, io["w_base_t"], FC,
                                  lambda oc: y[:, oc * TG:(oc + 1) * TG],
                                  bias=bbias, extra_dram=io["w_bw_t"],
                                  extra_rhs=hw, defer=True)]
        if g + 1 < NG:
            bodies.append(emit_ln_stats_dram(ctx.alt(), dr["xdram"], g + 1, mB,
                                             sB, rows, defer=True, slim=True))
        ctx.phases(bodies)
        ctx.A.reset(mk)
        ctx.AB.reset(mkb)
        emit_post_stats(ctx, y, rows, m2B, s2B)
        mk = ctx.A.mark()
        bodies = [emit_resid_sbufy(ctx, dr["xdram"], g, y, m2B, s2B,
                                   dr["xdram"], defer=True)]
        if g + 1 < NG:
            bodies.append(emit_ln_apply_dram(ctx.alt(), dr["xdram"], g + 1, mB,
                                             sB, xhat, defer=True))
        ctx.phases(bodies)
        ctx.A.reset(mk)
        if g + 1 < NG:
            _emit_l1f_gates(ctx, g + 1, xhat, hw, r16s, bthrB)
    ctx.A.reset(mk0)
    ctx.AB.reset(mkb0)


def _emit_l1f_gates(ctx, g, xhat, hw, r16s, bthrB):
    c, sems, small, psum = ctx.c, ctx.sems, ctx.small, ctx.psum
    io = ctx.io
    mk = ctx.A.mark()
    mkb = ctx.AB.mark()
    wg = ctx.A.alloc(NE * FC)
    wthr = ctx.A.alloc(FC)
    awT = ctx.A.alloc(FC * 128)
    wgB = ctx.AB.alloc(NE * FC)
    wthrB = ctx.AB.alloc(FC)
    awTB = ctx.AB.alloc(FC * 128)
    geT = ctx.A.alloc(4 * NE)
    eT = ctx.A.alloc(4 * NE)
    thrT = ctx.A.alloc(4)
    thT = ctx.A.alloc(4)
    th2 = ctx.A.alloc(4)
    mx = ctx.A.alloc(4)
    ngm = ctx.A.alloc(4)
    Z = ctx.A.alloc(4)
    wsum = ctx.A.alloc(4)
    wz = ctx.A.alloc(4)
    ws2 = ctx.A.alloc(4)
    Zr = ctx.A.alloc(4)
    wr = ctx.A.alloc(4)
    wT = ctx.A.alloc(TG)[0:8, :]
    h_sb = ctx.A.alloc(TG)
    wexp = ctx.A.alloc(TG)

    def xh_tile(kc, tt):
        return xhat[:, kc * TG + tt * 128:kc * TG + (tt + 1) * 128]

    def body(block):
        b_lda = c.lda
        b_pe, b_act, b_dve = c.pe, c.act, c.dve

        def sync(eng):
            eng.dma_start(out=wg.rearrange("p (kc e) -> p kc e", kc=FC),
                          in_=io["w_gate_t"].rearrange("(kc p) e -> p kc e", p=128)
                          ).then_inc(sems["lda"], 16)
            for fcb in range(FC):
                eng.dma_start(out=wthr[:, fcb:fcb + 1],
                              in_=io["w_thr"][fcb * 128:(fcb + 1) * 128, :]
                              ).then_inc(sems["lda"], 16)
            eng.dma_start(out=awT.rearrange("p (kc e) -> p kc e", kc=FC),
                          in_=io["w_aw_t"].rearrange("(kc p) e -> p kc e", p=128)
                          ).then_inc(sems["lda"], 16)
        c.lda += 16 * (FC + 2)

        def tensor(eng):
            eng.wait_ge(sems["dve"], b_dve + 3)
            for tt in range(4):
                if tt >= 2:
                    eng.wait_ge(sems["act"], b_act + tt - 1)
                for kc in range(FC):
                    eng.matmul(psum[tt % 2][:, tt * NE:(tt + 1) * NE],
                               xh_tile(kc, tt), wgB[:, kc * NE:(kc + 1) * NE],
                               start=(kc == 0), stop=(kc == FC - 1)
                               ).then_inc(sems["pe"], 1)
            for tt in range(4):
                if tt >= 2:
                    eng.wait_ge(sems["act"], b_act + 4 + tt - 1)
                for kc in range(FC):
                    eng.matmul(psum[2 + tt % 2][:, tt:tt + 1],
                               xh_tile(kc, tt), wthrB[:, kc:kc + 1],
                               start=(kc == 0), stop=(kc == FC - 1)
                               ).then_inc(sems["pe"], 1)
            for kc in range(FC):
                eng.matmul(psum[4][:, :TG], awTB[:, kc * 128:(kc + 1) * 128],
                           xhat[:, kc * TG:(kc + 1) * TG],
                           start=(kc == 0), stop=(kc == FC - 1)
                           ).then_inc(sems["pe"], 1)
            for tt in range(4):
                eng.wait_ge(sems["dve"], b_dve + 3 + 8 + 10 * (tt + 1))
                eng.transpose(psum[5][0:8, tt * 128:(tt + 1) * 128],
                              eT[:, tt * NE:(tt + 1) * NE],
                              small["c_ident"]).then_inc(sems["pe"], 1)
            eng.wait_ge(sems["act"], b_act + 18)
            eng.matmul(psum[5][:, :TG], r16s, wT, start=True, stop=True
                       ).then_inc(sems["pe"], 1)
        c.pe += 293

        def scalar(eng):
            for tt in range(4):
                eng.wait_ge(sems["pe"], b_pe + 32 * (tt + 1))
                eng.activation(geT[:, tt * NE:(tt + 1) * NE],
                               psum[tt % 2][:, tt * NE:(tt + 1) * NE], AF.Copy
                               ).then_inc(sems["act"], 1)
            for tt in range(4):
                eng.wait_ge(sems["pe"], b_pe + 128 + 32 * (tt + 1))
                eng.activation(thrT[:, tt:tt + 1], psum[2 + tt % 2][:, tt:tt + 1],
                               AF.Copy).then_inc(sems["act"], 1)
            for tt in range(4):
                eng.wait_ge(sems["act"], b_act + 4 + tt + 1)
                eng.activation(thT[:, tt:tt + 1], thrT[:, tt:tt + 1], AF.Sigmoid,
                               bias=bthrB[:, 0:1]).then_inc(sems["act"], 1)
            for tt in range(4):
                eng.wait_ge(sems["dve"], b_dve + 3 + 2 * (tt + 1))
                eng.activation(eT[:, tt * NE:(tt + 1) * NE],
                               geT[:, tt * NE:(tt + 1) * NE], AF.Exp,
                               bias=ngm[:, tt:tt + 1],
                               accum_out=Z[:, tt:tt + 1]).then_inc(sems["act"], 1)
            eng.wait_ge(sems["pe"], b_pe + 288)
            eng.activation(h_sb, psum[4][:, :TG], AF.Copy).then_inc(sems["act"], 1)
            eng.wait_ge(sems["pe"], b_pe + 292)
            eng.activation(wT, psum[5][0:8, :TG], AF.Copy).then_inc(sems["act"], 1)
            eng.wait_ge(sems["pe"], b_pe + 293)
            eng.activation(wexp, psum[5][:, :TG], AF.Copy).then_inc(sems["act"], 1)
        c.act += 19

        def vector(eng):
            eng.wait_ge(sems["lda"], b_lda + 16 * (FC + 2))
            eng.tensor_copy(wgB, wg).then_inc(sems["dve"], 1)
            eng.tensor_copy(wthrB, wthr).then_inc(sems["dve"], 1)
            eng.tensor_copy(awTB, awT).then_inc(sems["dve"], 1)
            dvn = b_dve + 3
            for tt in range(4):
                eng.wait_ge(sems["act"], b_act + tt + 1)
                eng.tensor_reduce(mx[:, tt:tt + 1], geT[:, tt * NE:(tt + 1) * NE],
                                  mybir.AxisListType.X, ALU.max).then_inc(sems["dve"], 1)
                dvn += 1
                eng.wait_ge(sems["dve"], dvn)
                eng.tensor_scalar_mul(ngm[:, tt:tt + 1], mx[:, tt:tt + 1], -1.0
                                      ).then_inc(sems["dve"], 1)
                dvn += 1
            for tt in range(4):
                esl = eT[:, tt * NE:(tt + 1) * NE]
                eng.wait_ge(sems["act"], b_act + 12 + tt + 1)
                eng.reciprocal(Zr[:, tt:tt + 1], Z[:, tt:tt + 1]).then_inc(sems["dve"], 1)
                dvn += 1
                eng.wait_ge(sems["dve"], dvn)
                eng.tensor_scalar(esl, esl, Zr[:, tt:tt + 1], None, ALU.mult
                                  ).then_inc(sems["dve"], 1)
                dvn += 1
                eng.wait_ge(sems["act"], b_act + 8 + tt + 1)
                eng.tensor_scalar_mul(th2[:, tt:tt + 1], thT[:, tt:tt + 1],
                                      MAX_THR).then_inc(sems["dve"], 1)
                dvn += 1
                eng.wait_ge(sems["dve"], dvn)
                eng.tensor_scalar(esl, esl, th2[:, tt:tt + 1], 0.0,
                                  ALU.subtract, ALU.max).then_inc(sems["dve"], 1)
                dvn += 1
                eng.wait_ge(sems["dve"], dvn)
                eng.tensor_reduce(wsum[:, tt:tt + 1], esl,
                                  mybir.AxisListType.X, ALU.add).then_inc(sems["dve"], 1)
                dvn += 1
                eng.wait_ge(sems["dve"], dvn)
                eng.tensor_scalar(wz[:, tt:tt + 1], wsum[:, tt:tt + 1], 0.0, None,
                                  ALU.is_equal).then_inc(sems["dve"], 1)
                dvn += 1
                eng.wait_ge(sems["dve"], dvn)
                eng.tensor_scalar_mul(ws2[:, tt:tt + 1], wsum[:, tt:tt + 1],
                                      1.0 / MOE_SCALING).then_inc(sems["dve"], 1)
                dvn += 1
                eng.wait_ge(sems["dve"], dvn)
                eng.tensor_tensor(ws2[:, tt:tt + 1], ws2[:, tt:tt + 1],
                                  wz[:, tt:tt + 1], ALU.add).then_inc(sems["dve"], 1)
                dvn += 1
                eng.wait_ge(sems["dve"], dvn)
                eng.reciprocal(wr[:, tt:tt + 1], ws2[:, tt:tt + 1]).then_inc(sems["dve"], 1)
                dvn += 1
                eng.wait_ge(sems["dve"], dvn)
                eng.tensor_scalar(esl, esl, wr[:, tt:tt + 1], None, ALU.mult
                                  ).then_inc(sems["dve"], 1)
                dvn += 1
            eng.wait_ge(sems["act"], b_act + 19)
            eng.tensor_tensor(hw, h_sb, wexp, ALU.mult).then_inc(sems["dve"], 1)
            dvn += 1
        c.dve += 52

        return {"sync": sync, "tensor": tensor, "scalar": scalar, "vector": vector}

    ctx.phase(body)
    ctx.A.reset(mk)
    ctx.AB.reset(mkb)


def build_core_program(sublayers):
    nc = bass.Bass("TRN2", target_bir_lowering=False)
    io = {}
    io["xin"] = nc.dram_tensor("xin", [TOK, D], F32, kind="ExternalInput").ap()
    io["xout"] = nc.dram_tensor("xout", [TOK, D], F32, kind="ExternalOutput").ap()
    if DEBUG_TAP is not None:
        io["dbg"] = nc.dram_tensor("dbg", [D, TOK], F32, kind="ExternalOutput").ap()
    for k, shp in CONST_SHAPES.items():
        io[k] = nc.dram_tensor(k, shp, F32, kind="ExternalInput").ap()

    if "l0a" in sublayers:
        io["w_mask"] = nc.dram_tensor("w_mask", [D, D], I32, kind="ExternalInput").ap()
        io["w_weight"] = nc.dram_tensor("w_weight", [D, D], F32, kind="ExternalInput").ap()
        io["w_bias"] = nc.dram_tensor("w_bias", [D, 1], F32, kind="ExternalInput").ap()
    if "l0f" in sublayers:
        io["w_up"] = nc.dram_tensor("w_up", [FH * FD, FM], F32, kind="ExternalInput").ap()
        io["w_down"] = nc.dram_tensor("w_down", [FH * FM, FD], F32, kind="ExternalInput").ap()
        io["w_perm"] = nc.dram_tensor("w_perm", [D, 1], I32, kind="ExternalInput").ap()
    if "l1a" in sublayers:
        io["w_wq"] = nc.dram_tensor("w_wq", [D, D], F32, kind="ExternalInput").ap()
        io["w_wk"] = nc.dram_tensor("w_wk", [D, NKV * HD], F32, kind="ExternalInput").ap()
        io["w_wv"] = nc.dram_tensor("w_wv", [D, NKV * HD], F32, kind="ExternalInput").ap()
        io["w_wo"] = nc.dram_tensor("w_wo", [D, D], F32, kind="ExternalInput").ap()
    if "l1f" in sublayers:
        io["w_base_t"] = nc.dram_tensor("w_base_t", [D, D], F32, kind="ExternalInput").ap()
        io["w_bbias"] = nc.dram_tensor("w_bbias", [D, 1], F32, kind="ExternalInput").ap()
        io["w_gate_t"] = nc.dram_tensor("w_gate_t", [D, NE], F32, kind="ExternalInput").ap()
        io["w_thr"] = nc.dram_tensor("w_thr", [D, 1], F32, kind="ExternalInput").ap()
        io["w_bthr"] = nc.dram_tensor("w_bthr", [1, 1], F32, kind="ExternalInput").ap()
        io["w_aw_t"] = nc.dram_tensor("w_aw_t", [D, NE * NR], F32, kind="ExternalInput").ap()
        io["w_bw_t"] = nc.dram_tensor("w_bw_t", [NE * NR, D], F32, kind="ExternalInput").ap()

    dr = {}
    dr["xdram"] = nc.dram_tensor("xdram", [D, TOK], F32, kind="Internal").ap()
    if "l0a" in sublayers:
        dr["wtdram"] = nc.dram_tensor("wtdram", [D, D], BF16, kind="Internal").ap()
    if "l0f" in sublayers or "l1a" in sublayers:
        dr["sbdram"] = nc.dram_tensor("sbdram", [NG * D, TG], BF16, kind="Internal").ap()
    if "l0f" in sublayers:
        dr["s2dram"] = nc.dram_tensor("s2dram", [NG * D, TG], F32, kind="Internal").ap()
    if "l1a" in sublayers:
        dr["qdram"] = nc.dram_tensor("qdram", [D, TOK], BF16, kind="Internal").ap()
        dr["ccin"] = nc.dram_tensor("ccin", [2049, TOK], BF16, kind="Internal").ap()
        dr["ccout"] = nc.dram_tensor("ccout", [2 * 2049, TOK], BF16, kind="Internal").ap()

    AR = nc.alloc_sbuf_tensor("arena", [128, ARENA_COLS], F32).ap()
    ARB = nc.alloc_sbuf_tensor("arenab", [128, ARENAB_COLS], BF16).ap()
    small = {}
    for k in SMALL_CONSTS:
        small[k] = nc.alloc_sbuf_tensor("s" + k, CONST_SHAPES[k], F32).ap()
    small["c_onescol_b"] = nc.alloc_sbuf_tensor("sc_onescol_b", [128, 1], BF16).ap()

    psum = [nc.alloc_psum_tensor(f"ps{i}", [128, 512], F32).ap() for i in range(6)]

    cts = Counts()
    sems = {}
    hold = []
    for name in ("ph",) + DMA_SEMS + ENG_SEMS:
        h = nc.semaphore("sem_" + name)
        sems[name] = h.__enter__()
        hold.append(h)

    ctx = Ctx(nc, io, dr, Arena(AR, ARENA_COLS), Arena(ARB, ARENAB_COLS),
              small, psum, cts, sems)

    def ph_consts(block):
        lda0 = cts.lda

        def sync(eng):
            for k in SMALL_CONSTS:
                eng.dma_start(out=small[k], in_=io[k]).then_inc(sems["lda"], 16)
                cts.lda += 16

        def vector(eng):
            eng.wait_ge(sems["lda"], lda0 + 16 * len(SMALL_CONSTS))
            eng.tensor_copy(small["c_onescol_b"], small["c_onescol"]
                            ).then_inc(sems["dve"], 1)
        cts.dve += 1
        return {"sync": sync, "vector": vector}
    ctx.phase(ph_consts)

    emit_entry_transpose(ctx, io["xin"], dr["xdram"])
    emitters = {"l0a": emit_l0a, "l0f": emit_l0f, "l1a": emit_l1a, "l1f": emit_l1f}
    for name in sublayers:
        emitters[name](ctx)
    emit_exit_transpose(ctx, dr["xdram"], io["xout"])

    for h in reversed(hold):
        h.__exit__(None, None, None)
    return nc


def _shard_tokens(x_flat):
    out = []
    for core in range(NCORE):
        parts = [x_flat[a:b] for a, b in _token_slices(core)]
        out.append(np.ascontiguousarray(np.concatenate(parts, axis=0)))
    return out


def _unshard_tokens(shards):
    x = np.empty((B * S, D), np.float32)
    for core in range(NCORE):
        pos = 0
        for a, b in _token_slices(core):
            n = b - a
            x[a:b] = shards[core][pos:pos + n]
            pos += n
    return x


def _layer_inputs(name, inputs):
    w = {}
    if name == "l0a":
        w["w_mask"] = np.ascontiguousarray(inputs["L0_attn_mask"].astype(np.int32))
        w["w_weight"] = np.ascontiguousarray(inputs["L0_attn_weight"])
        w["w_bias"] = np.ascontiguousarray(
            np.asarray(inputs["L0_attn_bias"], np.float32).reshape(D, 1))
    elif name == "l0f":
        w["w_up"] = np.ascontiguousarray(
            np.asarray(inputs["L0_ffn_up"], np.float32).reshape(FH * FD, FM))
        w["w_down"] = np.ascontiguousarray(
            np.asarray(inputs["L0_ffn_down"], np.float32).reshape(FH * FM, FD))
        w["w_perm"] = np.ascontiguousarray(
            inputs["L0_ffn_perm"].astype(np.int32).reshape(D, 1))
    elif name == "l1a":
        w["w_wq"] = np.ascontiguousarray(inputs["L1_attn_wq"])
        w["w_wk"] = np.ascontiguousarray(inputs["L1_attn_wk"])
        w["w_wv"] = np.ascontiguousarray(inputs["L1_attn_wv"])
        w["w_wo"] = np.ascontiguousarray(inputs["L1_attn_wo"])
    elif name == "l1f":
        w["w_base_t"] = np.ascontiguousarray(
            np.asarray(inputs["L1_ffn_W_base"], np.float32).T)
        w["w_bbias"] = np.ascontiguousarray(
            np.asarray(inputs["L1_ffn_b_base"], np.float32).reshape(D, 1))
        w["w_gate_t"] = np.ascontiguousarray(
            np.asarray(inputs["L1_ffn_W_gate"], np.float32).T)
        w["w_thr"] = np.ascontiguousarray(
            np.asarray(inputs["L1_ffn_W_thr"], np.float32).reshape(D, 1))
        w["w_bthr"] = np.ascontiguousarray(
            np.asarray(inputs["L1_ffn_b_thr"], np.float32).reshape(1, 1))
        w["w_aw_t"] = np.ascontiguousarray(
            np.asarray(inputs["L1_ffn_A_w"], np.float32).reshape(NE * NR, D).T)
        w["w_bw_t"] = np.ascontiguousarray(
            np.asarray(inputs["L1_ffn_B_w"], np.float32).transpose(0, 2, 1)
            .reshape(NE * NR, D))
    return w


def run_layers(x_shards, layers, inputs, core_ids=None):
    if core_ids is None:
        core_ids = list(range(NCORE))
    nc = build_core_program(layers)
    wmaps = {}
    for name in layers:
        wmaps.update(_layer_inputs(name, inputs))
    in_maps = []
    for core in core_ids:
        m = {"xin": x_shards[core]}
        m.update(make_consts(core))
        m.update(wmaps)
        in_maps.append(m)
    res = run_bass_kernel_spmd(nc, in_maps, core_ids=core_ids)
    return [r["xout"] for r in res.results]


ALL_LAYERS = ["l0a", "l0f", "l1a", "l1f"]


def kernel(**inputs):
    x = np.ascontiguousarray(np.asarray(inputs["L0_attn_data"], np.float32)
                             .reshape(B * S, D))
    shards = _shard_tokens(x)
    shards = run_layers(shards, ALL_LAYERS, inputs)
    out = _unshard_tokens(shards)
    return out.reshape(B, S, D)


# version 04 (final)
# speedup vs baseline: 1.8831x; 1.0593x over previous
import numpy as np
import concourse.bass as bass
import concourse.mybir as mybir
from concourse.bass_utils import run_bass_kernel_spmd

F32 = mybir.dt.float32
F32R = mybir.dt.float32r
BF16 = mybir.dt.bfloat16
I32 = mybir.dt.int32
AF = mybir.ActivationFunctionType
ALU = mybir.AluOpType

D = 4096
TOK = 2048
NCORE = 8
TG = 512
NG = TOK // TG
FC = D // 128
B, S = 4, 4096
EPS = 1e-5
NBIG = -1.0e9
NH, NKV, HD = 32, 8, 128
NREP = NH // NKV
THETA = 10000.0
FH, FD, FM = 16, 256, 1024
NE, NR = 8, 16
MOE_SCALING = 2.0
MAX_THR = 0.125

ARENA_COLS = 32768
ARENAB_COLS = 30720

DEBUG_STOP = None
DEBUG_TAP = None
CC_GROUPS = [[0, 1], [2, 3], [4, 5], [6, 7]]
DEBUG_MAXPH = None


def _token_slices(core):
    b = core // 2
    if core % 2 == 0:
        return [(b * S + 0, b * S + 1024), (b * S + 3072, b * S + 4096)]
    return [(b * S + 1024, b * S + 3072)]


def _qpos(core):
    if core % 2 == 0:
        return np.concatenate([np.arange(0, 1024), np.arange(3072, 4096)])
    return np.arange(1024, 3072)


KPOS_GLOBAL = np.concatenate(
    [np.arange(0, 1024), np.arange(3072, 4096), np.arange(1024, 3072)]
).astype(np.float32)


def _rope_tables(pos):
    freqs = 1.0 / THETA ** (np.arange(0, HD, 2, dtype=np.float64) / HD)
    ang = np.outer(freqs, pos.astype(np.float64))
    cs, sn = np.cos(ang), np.sin(ang)
    C = np.repeat(cs, 2, axis=0)
    Sm = np.empty_like(C)
    Sm[0::2] = -sn
    Sm[1::2] = sn
    return C.astype(np.float32), Sm.astype(np.float32)


def make_consts(core):
    c = {}
    c["c_ident"] = np.eye(128, dtype=np.float32)
    c["c_onescol"] = np.ones((128, 1), np.float32)
    c["c_onesrow"] = np.ones((1, 128), np.float32)
    c["c_eps"] = np.full((1, 1), EPS, np.float32)
    c["c_rsqhd"] = np.full((128, 1), 1.0 / np.sqrt(HD), np.float32)
    o = np.zeros((128, FC * 128), np.float32)
    for ncc in range(FC):
        for kl in range(128):
            o[kl, ncc * 128 + 4 * ncc + kl // 32] = 1.0
    c["c_o32a"] = o
    r = np.zeros((128, FC * 128), np.float32)
    for ncc in range(FC):
        for p in range(128):
            r[4 * ncc + p // 32, ncc * 128 + p] = 1.0
    c["c_rsel"] = r
    x = np.zeros((128, D), np.float32)
    for k in range(D):
        x[k // 32, k] = 1.0
    c["c_xex"] = x
    r16 = np.zeros((8, 128), np.float32)
    for p in range(128):
        r16[p // 16, p] = 1.0
    c["c_r16"] = r16
    ps = np.zeros((128, 128), np.float32)
    for i in range(0, 128, 2):
        ps[i + 1, i] = 1.0
        ps[i, i + 1] = 1.0
    c["c_pswap"] = ps
    qp = _qpos(core)
    c["c_qpos"] = qp.astype(np.float32).reshape(1, TOK)
    c["c_kpos"] = KPOS_GLOBAL.reshape(S, 1).copy()
    C_, S_ = _rope_tables(qp)
    c["c_ropec"] = C_
    c["c_ropes"] = S_
    return c


CONST_SHAPES = {
    "c_ident": [128, 128], "c_onescol": [128, 1], "c_onesrow": [1, 128],
    "c_eps": [1, 1], "c_rsqhd": [128, 1],
    "c_o32a": [128, FC * 128], "c_rsel": [128, FC * 128], "c_xex": [128, D],
    "c_pswap": [128, 128], "c_qpos": [1, TOK], "c_kpos": [S, 1],
    "c_r16": [8, 128],
    "c_ropec": [128, TOK], "c_ropes": [128, TOK],
}
SMALL_CONSTS = ("c_ident", "c_onescol", "c_onesrow", "c_eps", "c_rsqhd")

DMA_SEMS = ("ld0", "ld1", "lda", "gp0", "gp1", "gp2", "gp3", "gpd",
            "sc0", "sc1", "scd", "ld0a", "ld1a", "ldaa", "gpda")
ENG_SEMS = ("pe", "dve", "act", "gpc", "cc", "pe2", "dve2", "act2", "gpc2")

ALT_MAP = {"pe": "pe2", "dve": "dve2", "act": "act2", "gpc": "gpc2",
           "ld0": "ld0a", "ld1": "ld1a", "lda": "ldaa", "gpd": "gpda"}


class AltCounts:
    def __init__(self, c):
        object.__setattr__(self, "_c", c)

    def __getattr__(self, n):
        return getattr(self._c, ALT_MAP.get(n, n))

    def __setattr__(self, n, v):
        setattr(self._c, ALT_MAP.get(n, n), v)


class AltSems:
    def __init__(self, sems):
        self._s = sems

    def __getitem__(self, n):
        return self._s[ALT_MAP.get(n, n)]


class Counts:
    def __init__(self):
        for n in DMA_SEMS + ENG_SEMS:
            setattr(self, n, 0)
        self.ph = 0


ENGS = ("sync", "tensor", "vector", "scalar", "gpsimd")


class GroupedDram:

    def __init__(self, base, g):
        self.base = base
        self.g = g

    def __getitem__(self, idx):
        rows, cols = idx
        g = self.g
        return self.base[g * D + rows.start:g * D + rows.stop,
                         cols.start - g * TG:cols.stop - g * TG]


class Arena:
    def __init__(self, ap, cols):
        self.ap = ap
        self.cols = cols
        self.off = 0

    def alloc(self, w):
        assert self.off + w <= self.cols, f"arena overflow {self.off}+{w}"
        sl = self.ap[:, self.off:self.off + w]
        self.off += w
        return sl

    def mark(self):
        return self.off

    def reset(self, off):
        self.off = off


class Ctx:
    def __init__(self, nc, io, dr, arena, arenab, small, psum, c, sems):
        self.nc = nc
        self.io = io
        self.dr = dr
        self.A = arena
        self.AB = arenab
        self.small = small
        self.psum = psum
        self.c = c
        self.sems = sems

    def phase(self, body):
        if not hasattr(self, "_nph"):
            self._nph = 0
        self._nph += 1
        if DEBUG_MAXPH is not None and self._nph > DEBUG_MAXPH:
            return
        nc, c, sems = self.nc, self.c, self.sems
        target = c.ph
        PRE_SEMS = ("pe", "dve", "act", "gpc", "pe2", "dve2", "act2", "gpc2")
        pre = tuple(getattr(c, s) for s in PRE_SEMS)
        with nc.Block() as block:
            emitters = body(block) or {}
            for name in ENGS:
                fn = emitters.get(name)
                sec = getattr(block, name)

                def _make(name=name, fn=fn):
                    def _(eng):
                        eng.wait_ge(sems["ph"], target)
                        if fn is not None:
                            for i, s in enumerate(PRE_SEMS):
                                if pre[i]:
                                    eng.wait_ge(sems[s], pre[i])
                            fn(eng)
                        if name == "sync":
                            for s in ("ld0", "ld1", "lda", "ld0a", "ld1a", "ldaa"):
                                if getattr(c, s):
                                    eng.wait_ge(sems[s], getattr(c, s))
                        if name == "gpsimd":
                            for s in ("gp0", "gp1", "gp2", "gp3", "gpd", "gpda"):
                                if getattr(c, s):
                                    eng.wait_ge(sems[s], getattr(c, s))
                        if name == "scalar":
                            for s in ("sc0", "sc1", "scd"):
                                if getattr(c, s):
                                    eng.wait_ge(sems[s], getattr(c, s))
                        eng.nop().then_inc(sems["ph"], 1)
                    return _
                sec(_make())
        c.ph = target + 5

    def alt(self):
        import copy
        a = object.__new__(Ctx)
        a.__dict__.update(self.__dict__)
        a.c = AltCounts(self.c)
        a.sems = AltSems(self.sems)
        return a

    def phases(self, bodies):
        nc, c, sems = self.nc, self.c, self.sems
        target = c.ph
        PRE_SEMS = ("pe", "dve", "act", "gpc", "pe2", "dve2", "act2", "gpc2")
        pre = tuple(getattr(c, s) for s in PRE_SEMS)
        allq = ("ld0", "ld1", "lda", "gp0", "gp1", "gp2", "gp3", "gpd",
                "sc0", "sc1", "scd", "ld0a", "ld1a", "ldaa", "gpda")
        with nc.Block() as block:
            emitter_list = []
            seam = []
            for body in bodies:
                emitter_list.append(body(block) or {})
                seam.append({s: getattr(c, s) for s in allq})
            for name in ENGS:
                fns = [e.get(name) for e in emitter_list if e.get(name)]
                sec = getattr(block, name)

                qsems = {"sync": ("ld0", "ld1", "lda", "ld0a", "ld1a", "ldaa"),
                         "gpsimd": ("gp0", "gp1", "gp2", "gp3", "gpd", "gpda"),
                         "scalar": ("sc0", "sc1", "scd")}.get(name, ())

                fidx = [i for i, e in enumerate(emitter_list) if e.get(name)]

                def _make(name=name, fns=tuple(fns), qsems=qsems,
                          fidx=tuple(fidx)):
                    def _(eng):
                        eng.wait_ge(sems["ph"], target)
                        if fns:
                            for i, s in enumerate(PRE_SEMS):
                                if pre[i]:
                                    eng.wait_ge(sems[s], pre[i])
                            for fi, fn in enumerate(fns):
                                if fi > 0:
                                    sm = seam[fidx[fi] - 1]
                                    for s in qsems:
                                        if sm[s]:
                                            eng.wait_ge(sems[s], sm[s])
                                fn(eng)
                        if name == "sync":
                            for s in ("ld0", "ld1", "lda", "ld0a", "ld1a", "ldaa"):
                                if getattr(c, s):
                                    eng.wait_ge(sems[s], getattr(c, s))
                        if name == "gpsimd":
                            for s in ("gp0", "gp1", "gp2", "gp3", "gpd", "gpda"):
                                if getattr(c, s):
                                    eng.wait_ge(sems[s], getattr(c, s))
                        if name == "scalar":
                            for s in ("sc0", "sc1", "scd"):
                                if getattr(c, s):
                                    eng.wait_ge(sems[s], getattr(c, s))
                        eng.nop().then_inc(sems["ph"], 1)
                    return _
                sec(_make())
        c.ph = target + 5


def emit_tap_sbuf(ctx, sl, r0=0):
    c, sems, io = ctx.c, ctx.sems, ctx.io
    if "dbg" not in io:
        return

    def body(block):
        def gpsimd(eng):
            eng.dma_start(out=io["dbg"][r0:r0 + sl.shape[0], 0:sl.shape[1]],
                          in_=sl).then_inc(sems["gpd"], 16)
        c.gpd += 16
        return {"gpsimd": gpsimd}
    ctx.phase(body)


def emit_tap_dram(ctx, src, rows, cols):
    c, sems, io = ctx.c, ctx.sems, ctx.io
    if "dbg" not in io:
        return

    def body(block):
        def gpsimd(eng):
            eng.dma_start(out=io["dbg"][0:rows, 0:cols],
                          in_=src[0:rows, 0:cols]).then_inc(sems["gpd"], 16)
        c.gpd += 16
        return {"gpsimd": gpsimd}
    ctx.phase(body)


def emit_entry_transpose(ctx, src_ap, dst_dram):
    c, sems, small, psum = ctx.c, ctx.sems, ctx.small, ctx.psum
    mk = ctx.A.mark()
    xt = [ctx.A.alloc(D) for _ in range(2)]
    ev = [ctx.A.alloc(512) for _ in range(2)]
    NT = TOK // 128

    def body(block):
        b_ld = [c.ld0, c.ld1]
        b_gp = [c.gp0, c.gp1]
        b_pe, b_act = c.pe, c.act

        def sync(eng):
            for tt in range(NT):
                if tt >= 2:
                    eng.wait_ge(sems["pe"], b_pe + (tt - 1) * FC)
                eng.dma_start(out=xt[tt % 2], in_=src_ap[tt * 128:(tt + 1) * 128, :]
                              ).then_inc(sems["ld" + str(tt % 2)], 16)
        c.ld0 += 16 * (NT // 2)
        c.ld1 += 16 * (NT // 2)

        def tensor(eng):
            for tt in range(NT):
                eng.wait_ge(sems["ld" + str(tt % 2)],
                            b_ld[tt % 2] + 16 * (tt // 2 + 1))
                for fc in range(FC):
                    g = tt * 8 + fc // 4
                    if g >= 2 and fc % 4 == 0:
                        eng.wait_ge(sems["act"], b_act + g - 1)
                    eng.transpose(psum[g % 2][:, (fc % 4) * 128:(fc % 4 + 1) * 128],
                                  xt[tt % 2][:, fc * 128:(fc + 1) * 128],
                                  small["c_ident"]).then_inc(sems["pe"], 1)
        c.pe += NT * FC

        def scalar(eng):
            for g in range(NT * 8):
                eng.wait_ge(sems["pe"], b_pe + (g + 1) * 4)
                if g >= 2:
                    eng.wait_ge(sems["gp" + str(g % 2)],
                                b_gp[g % 2] + 64 * (g // 2))
                eng.activation(ev[g % 2], psum[g % 2][:, :], AF.Copy
                               ).then_inc(sems["act"], 1)
        c.act += NT * 8

        def gpsimd(eng):
            for g in range(NT * 8):
                tt, fg = divmod(g, 8)
                eng.wait_ge(sems["act"], b_act + g + 1)
                for q in range(4):
                    fc = fg * 4 + q
                    eng.dma_start(
                        out=dst_dram[fc * 128:(fc + 1) * 128, tt * 128:(tt + 1) * 128],
                        in_=ev[g % 2][:, q * 128:(q + 1) * 128],
                    ).then_inc(sems["gp" + str(g % 2)], 16)
        c.gp0 += 64 * (NT * 8 // 2)
        c.gp1 += 64 * (NT * 8 // 2)

        return {"sync": sync, "tensor": tensor, "scalar": scalar, "gpsimd": gpsimd}

    ctx.phase(body)
    ctx.A.reset(mk)


def emit_exit_transpose(ctx, src_dram, dst_ap):
    c, sems, small, psum = ctx.c, ctx.sems, ctx.small, ctx.psum
    mk = ctx.A.mark()
    ft = [ctx.A.alloc(512) for _ in range(2)]
    ot = [ctx.A.alloc(D) for _ in range(2)]
    NT = TOK // 128

    def body(block):
        b_ld = [c.ld0, c.ld1]
        b_gp = [c.gp0, c.gp1]
        b_pe, b_act = c.pe, c.act

        def sync(eng):
            i = 0
            for tt in range(NT):
                for fg in range(8):
                    if i >= 2:
                        eng.wait_ge(sems["pe"], b_pe + (i - 1) * 4)
                    eng.dma_start(
                        out=ft[i % 2].rearrange("p (q t) -> p q t", q=4),
                        in_=src_dram[fg * 512:(fg + 1) * 512, tt * 128:(tt + 1) * 128]
                            .rearrange("(q p) t -> p q t", p=128),
                    ).then_inc(sems["ld" + str(i % 2)], 16)
                    i += 1
        c.ld0 += 16 * (NT * 8 // 2)
        c.ld1 += 16 * (NT * 8 // 2)

        def tensor(eng):
            i = 0
            for tt in range(NT):
                for fg in range(8):
                    eng.wait_ge(sems["ld" + str(i % 2)],
                                b_ld[i % 2] + 16 * (i // 2 + 1))
                    if i >= 2:
                        eng.wait_ge(sems["act"], b_act + i - 1)
                    for q in range(4):
                        eng.transpose(psum[i % 2][:, q * 128:(q + 1) * 128],
                                      ft[i % 2][:, q * 128:(q + 1) * 128],
                                      small["c_ident"]).then_inc(sems["pe"], 1)
                    i += 1
        c.pe += NT * 8 * 4

        def scalar(eng):
            i = 0
            for tt in range(NT):
                if tt >= 2:
                    eng.wait_ge(sems["gp" + str(tt % 2)],
                                b_gp[tt % 2] + 16 * (tt // 2))
                for fg in range(8):
                    eng.wait_ge(sems["pe"], b_pe + (i + 1) * 4)
                    eng.activation(ot[tt % 2][:, fg * 512:(fg + 1) * 512],
                                   psum[i % 2][:, :], AF.Copy).then_inc(sems["act"], 1)
                    i += 1
        c.act += NT * 8

        def gpsimd(eng):
            for tt in range(NT):
                eng.wait_ge(sems["act"], b_act + (tt + 1) * 8)
                eng.dma_start(out=dst_ap[tt * 128:(tt + 1) * 128, :],
                              in_=ot[tt % 2]).then_inc(sems["gp" + str(tt % 2)], 16)
        c.gp0 += 16 * (NT // 2)
        c.gp1 += 16 * (NT // 2)

        return {"sync": sync, "tensor": tensor, "scalar": scalar, "gpsimd": gpsimd}

    ctx.phase(body)
    ctx.A.reset(mk)


def _row_math2(eng, sems, c, b_dve, b_act, pA, pB, m_row, v_row, t_row,
               s_row, FCn):
    eng.tensor_scalar_mul(m_row, pA[0:1, :TG], 1.0 / D).then_inc(sems["dve"], 1)
    eng.tensor_scalar_mul(v_row, pB[0:1, :TG], 1.0 / D).then_inc(sems["dve"], 1)
    eng.wait_ge(sems["dve"], b_dve + 2)
    eng.tensor_tensor(t_row, m_row, m_row, ALU.mult).then_inc(sems["dve"], 1)
    eng.wait_ge(sems["dve"], b_dve + 3)
    eng.tensor_tensor(v_row, v_row, t_row, ALU.subtract).then_inc(sems["dve"], 1)
    eng.wait_ge(sems["act"], b_act + FCn + 1)
    eng.reciprocal(s_row, s_row).then_inc(sems["dve"], 1)


def emit_ln_stats_dram(ctx, src_dram, g, mB, sB, rows, psb=(2, 3, 4, 5),
                       defer=False, slim=False):
    c, sems, small = ctx.c, ctx.sems, ctx.small
    pA, pB, pC, pD = (ctx.psum[i] for i in psb)
    mk = ctx.A.mark()
    if slim:
        xb = ctx.A.alloc(TG)
        sq = ctx.A.alloc(TG)
    else:
        xb = ctx.A.alloc(2 * TG)
        sq = ctx.A.alloc(2 * TG)
    m_row = rows[0:1, 0:TG]
    v_row = rows[0:1, TG:2 * TG]
    t_row = rows[0:1, 2 * TG:3 * TG]
    s_row = v_row

    def body(block):
        b_ld = [c.ld0, c.ld1]
        b_pe, b_act, b_dve = c.pe, c.act, c.dve

        def xsl_of(fc):
            return xb if slim else xb[:, (fc % 2) * TG:(fc % 2 + 1) * TG]

        def qsl_of(fc):
            return sq if slim else sq[:, (fc % 2) * TG:(fc % 2 + 1) * TG]

        def sync(eng):
            for fc in range(FC):
                if slim:
                    if fc >= 1:
                        eng.wait_ge(sems["pe"], b_pe + 2 * fc)
                        eng.wait_ge(sems["act"], b_act + fc)
                    eng.dma_start(out=xb,
                                  in_=src_dram[fc * 128:(fc + 1) * 128,
                                               g * TG:(g + 1) * TG]
                                  ).then_inc(sems["ld0"], 16)
                else:
                    if fc >= 2:
                        eng.wait_ge(sems["pe"], b_pe + 2 * (fc - 1))
                    eng.dma_start(out=xb[:, (fc % 2) * TG:(fc % 2 + 1) * TG],
                                  in_=src_dram[fc * 128:(fc + 1) * 128,
                                               g * TG:(g + 1) * TG]
                                  ).then_inc(sems["ld" + str(fc % 2)], 16)
        if slim:
            c.ld0 += 16 * FC
        else:
            c.ld0 += 16 * (FC // 2)
            c.ld1 += 16 * (FC // 2)

        def scalar(eng):
            for fc in range(FC):
                if slim:
                    eng.wait_ge(sems["ld0"], b_ld[0] + 16 * (fc + 1))
                    if fc >= 1:
                        eng.wait_ge(sems["pe"], b_pe + 2 * fc)
                else:
                    eng.wait_ge(sems["ld" + str(fc % 2)],
                                b_ld[fc % 2] + 16 * (fc // 2 + 1))
                    if fc >= 2:
                        eng.wait_ge(sems["pe"], b_pe + 2 * (fc - 1))
                eng.activation(qsl_of(fc), xsl_of(fc),
                               AF.Square).then_inc(sems["act"], 1)
            eng.wait_ge(sems["dve"], b_dve + 4)
            eng.activation(s_row, v_row, AF.Sqrt, bias=small["c_eps"]
                           ).then_inc(sems["act"], 1)
            eng.wait_ge(sems["pe"], b_pe + 2 * FC + 2)
            eng.activation(mB, pC[:, :TG], AF.Copy)
            eng.activation(sB, pD[:, :TG], AF.Copy).then_inc(sems["act"], 2)
        c.act += FC + 3

        def tensor(eng):
            for fc in range(FC):
                eng.wait_ge(sems["act"], b_act + fc + 1)
                eng.matmul(pA[0:1, :TG], small["c_onescol"], xsl_of(fc),
                           start=(fc == 0), stop=(fc == FC - 1))
                eng.matmul(pB[0:1, :TG], small["c_onescol"], qsl_of(fc),
                           start=(fc == 0), stop=(fc == FC - 1)).then_inc(sems["pe"], 2)
            eng.wait_ge(sems["dve"], b_dve + 5)
            eng.matmul(pC[:, :TG], small["c_onesrow"], m_row, start=True, stop=True)
            eng.matmul(pD[:, :TG], small["c_onesrow"], s_row,
                       start=True, stop=True).then_inc(sems["pe"], 2)
        c.pe += 2 * FC + 2

        def vector(eng):
            eng.wait_ge(sems["pe"], b_pe + 2 * FC)
            _row_math2(eng, sems, c, b_dve, b_act, pA, pB, m_row, v_row,
                       t_row, s_row, FC)
        c.dve += 5

        return {"sync": sync, "scalar": scalar, "tensor": tensor, "vector": vector}

    if defer:
        return body
    ctx.phase(body)
    ctx.A.reset(mk)


def emit_ln_apply_dram(ctx, src_dram, g, mB, sB, xhat, store_dram=None,
                       defer=False):
    c, sems = ctx.c, ctx.sems
    mk = ctx.A.mark()
    xb = ctx.A.alloc(2 * TG)
    tb = ctx.A.alloc(2 * TG)

    def body(block):
        b_ld = [c.ld0, c.ld1]
        b_dve, b_gpc = c.dve, c.gpc

        def sync(eng):
            for fc in range(FC):
                if fc >= 2:
                    eng.wait_ge(sems["dve"], b_dve + fc - 1)
                eng.dma_start(out=xb[:, (fc % 2) * TG:(fc % 2 + 1) * TG],
                              in_=src_dram[fc * 128:(fc + 1) * 128, g * TG:(g + 1) * TG]
                              ).then_inc(sems["ld" + str(fc % 2)], 16)
        c.ld0 += 16 * (FC // 2)
        c.ld1 += 16 * (FC // 2)

        def vector(eng):
            for fc in range(FC):
                eng.wait_ge(sems["ld" + str(fc % 2)],
                            b_ld[fc % 2] + 16 * (fc // 2 + 1))
                if fc >= 2:
                    eng.wait_ge(sems["gpc"], b_gpc + fc - 1)
                eng.tensor_tensor(tb[:, (fc % 2) * TG:(fc % 2 + 1) * TG],
                                  xb[:, (fc % 2) * TG:(fc % 2 + 1) * TG], mB,
                                  ALU.subtract).then_inc(sems["dve"], 1)
        c.dve += FC

        def gpsimd(eng):
            for fc in range(FC):
                eng.wait_ge(sems["dve"], b_dve + fc + 1)
                eng.tensor_tensor(xhat[:, fc * TG:(fc + 1) * TG],
                                  tb[:, (fc % 2) * TG:(fc % 2 + 1) * TG], sB,
                                  ALU.mult).then_inc(sems["gpc"], 1)
                if store_dram is not None:
                    eng.wait_ge(sems["gpc"], b_gpc + fc + 1)
                    eng.dma_start(
                        out=store_dram[fc * 128:(fc + 1) * 128, g * TG:(g + 1) * TG],
                        in_=xhat[:, fc * TG:(fc + 1) * TG]).then_inc(sems["gpd"], 16)
        c.gpc += FC
        if store_dram is not None:
            c.gpd += 16 * FC

        return {"sync": sync, "vector": vector, "gpsimd": gpsimd}

    if defer:
        return body
    ctx.phase(body)
    ctx.A.reset(mk)


def emit_residual_from_dram(ctx, x_dram, y_dram, g, m2B, s2B, out_dram,
                            defer=False):
    c, sems = ctx.c, ctx.sems
    mk = ctx.A.mark()
    xb = ctx.A.alloc(2 * TG)
    yb = ctx.A.alloc(2 * TG)
    tb = ctx.A.alloc(2 * TG)

    def body(block):
        b_ld = [c.ld0, c.ld1]
        b_sc = [c.sc0, c.sc1]
        b_gp = [c.gp0, c.gp1]
        b_dve, b_gpc = c.dve, c.gpc

        def sync(eng):
            for fc in range(FC):
                if fc >= 2:
                    eng.wait_ge(sems["gpc"], b_gpc + fc - 1)
                eng.dma_start(out=xb[:, (fc % 2) * TG:(fc % 2 + 1) * TG],
                              in_=x_dram[fc * 128:(fc + 1) * 128, g * TG:(g + 1) * TG]
                              ).then_inc(sems["ld" + str(fc % 2)], 16)
        c.ld0 += 16 * (FC // 2)
        c.ld1 += 16 * (FC // 2)

        def scalar(eng):
            for fc in range(FC):
                if fc >= 2:
                    eng.wait_ge(sems["dve"], b_dve + 2 * (fc - 1))
                eng.dma_start(out=yb[:, (fc % 2) * TG:(fc % 2 + 1) * TG],
                              in_=y_dram[fc * 128:(fc + 1) * 128, g * TG:(g + 1) * TG]
                              ).then_inc(sems["sc" + str(fc % 2)], 16)
        c.sc0 += 16 * (FC // 2)
        c.sc1 += 16 * (FC // 2)

        def vector(eng):
            for fc in range(FC):
                eng.wait_ge(sems["sc" + str(fc % 2)],
                            b_sc[fc % 2] + 16 * (fc // 2 + 1))
                if fc >= 2:
                    eng.wait_ge(sems["gp" + str(fc % 2)],
                                b_gp[fc % 2] + 16 * (fc // 2))
                eng.tensor_tensor(tb[:, (fc % 2) * TG:(fc % 2 + 1) * TG],
                                  yb[:, (fc % 2) * TG:(fc % 2 + 1) * TG], m2B,
                                  ALU.subtract).then_inc(sems["dve"], 1)
                eng.wait_ge(sems["dve"], b_dve + 2 * fc + 1)
                eng.tensor_tensor(tb[:, (fc % 2) * TG:(fc % 2 + 1) * TG],
                                  tb[:, (fc % 2) * TG:(fc % 2 + 1) * TG], s2B,
                                  ALU.mult).then_inc(sems["dve"], 1)
        c.dve += 2 * FC

        def gpsimd(eng):
            for fc in range(FC):
                eng.wait_ge(sems["dve"], b_dve + 2 * (fc + 1))
                eng.wait_ge(sems["ld" + str(fc % 2)],
                            b_ld[fc % 2] + 16 * (fc // 2 + 1))
                eng.tensor_tensor(tb[:, (fc % 2) * TG:(fc % 2 + 1) * TG],
                                  tb[:, (fc % 2) * TG:(fc % 2 + 1) * TG],
                                  xb[:, (fc % 2) * TG:(fc % 2 + 1) * TG],
                                  ALU.add).then_inc(sems["gpc"], 1)
                eng.wait_ge(sems["gpc"], b_gpc + fc + 1)
                eng.dma_start(out=out_dram[fc * 128:(fc + 1) * 128, g * TG:(g + 1) * TG],
                              in_=tb[:, (fc % 2) * TG:(fc % 2 + 1) * TG]
                              ).then_inc(sems["gp" + str(fc % 2)], 16)
        c.gpc += FC
        c.gp0 += 16 * (FC // 2)
        c.gp1 += 16 * (FC // 2)

        return {"sync": sync, "scalar": scalar, "vector": vector, "gpsimd": gpsimd}

    if defer:
        return body
    ctx.phase(body)
    ctx.A.reset(mk)


def emit_post_stats(ctx, y, rows, m2B, s2B, defer=False):
    c, sems, small, psum = ctx.c, ctx.sems, ctx.small, ctx.psum
    mk = ctx.A.mark()
    sq = ctx.A.alloc(2 * TG)
    m_row = rows[0:1, 0:TG]
    v_row = rows[0:1, TG:2 * TG]
    t_row = rows[0:1, 2 * TG:3 * TG]
    s_row = v_row

    def body(block):
        b_pe, b_act, b_dve = c.pe, c.act, c.dve

        def scalar(eng):
            for fc in range(FC):
                if fc >= 2:
                    eng.wait_ge(sems["pe"], b_pe + 2 * (fc - 1))
                eng.activation(sq[:, (fc % 2) * TG:(fc % 2 + 1) * TG],
                               y[:, fc * TG:(fc + 1) * TG],
                               AF.Square).then_inc(sems["act"], 1)
            eng.wait_ge(sems["dve"], b_dve + 4)
            eng.activation(s_row, v_row, AF.Sqrt, bias=small["c_eps"]
                           ).then_inc(sems["act"], 1)
            eng.wait_ge(sems["pe"], b_pe + 2 * FC + 2)
            eng.activation(m2B, psum[4][:, :TG], AF.Copy)
            eng.activation(s2B, psum[5][:, :TG], AF.Copy).then_inc(sems["act"], 2)
        c.act += FC + 3

        def tensor(eng):
            for fc in range(FC):
                eng.wait_ge(sems["act"], b_act + fc + 1)
                eng.matmul(psum[2][0:1, :TG], small["c_onescol"],
                           y[:, fc * TG:(fc + 1) * TG],
                           start=(fc == 0), stop=(fc == FC - 1))
                eng.matmul(psum[3][0:1, :TG], small["c_onescol"],
                           sq[:, (fc % 2) * TG:(fc % 2 + 1) * TG],
                           start=(fc == 0), stop=(fc == FC - 1)).then_inc(sems["pe"], 2)
            eng.wait_ge(sems["dve"], b_dve + 5)
            eng.matmul(psum[4][:, :TG], small["c_onesrow"], m_row, start=True, stop=True)
            eng.matmul(psum[5][:, :TG], small["c_onesrow"], s_row,
                       start=True, stop=True).then_inc(sems["pe"], 2)
        c.pe += 2 * FC + 2

        def vector(eng):
            eng.wait_ge(sems["pe"], b_pe + 2 * FC)
            _row_math2(eng, sems, c, b_dve, b_act, psum[2], psum[3],
                       m_row, v_row, t_row, s_row, FC)
        c.dve += 5

        return {"scalar": scalar, "tensor": tensor, "vector": vector}

    if defer:
        return body
    ctx.phase(body)
    ctx.A.reset(mk)


def emit_resid_sbufy(ctx, x_dram, g, y, m2B, s2B, out_dram, defer=False):
    c, sems = ctx.c, ctx.sems
    mk = ctx.A.mark()
    xb = ctx.A.alloc(2 * TG)
    tb = ctx.A.alloc(2 * TG)

    def body(block):
        b_ld = [c.ld0, c.ld1]
        b_dve, b_gpc = c.dve, c.gpc

        def sync(eng):
            for fc in range(FC):
                if fc >= 2:
                    eng.wait_ge(sems["gpc"], b_gpc + fc - 1)
                eng.dma_start(out=xb[:, (fc % 2) * TG:(fc % 2 + 1) * TG],
                              in_=x_dram[fc * 128:(fc + 1) * 128, g * TG:(g + 1) * TG]
                              ).then_inc(sems["ld" + str(fc % 2)], 16)
        c.ld0 += 16 * (FC // 2)
        c.ld1 += 16 * (FC // 2)

        def vector(eng):
            for fc in range(FC):
                if fc >= 2:
                    eng.wait_ge(sems["gpc"], b_gpc + fc - 1)
                eng.tensor_tensor(tb[:, (fc % 2) * TG:(fc % 2 + 1) * TG],
                                  y[:, fc * TG:(fc + 1) * TG], m2B, ALU.subtract
                                  ).then_inc(sems["dve"], 1)
                eng.wait_ge(sems["dve"], b_dve + 2 * fc + 1)
                eng.tensor_tensor(tb[:, (fc % 2) * TG:(fc % 2 + 1) * TG],
                                  tb[:, (fc % 2) * TG:(fc % 2 + 1) * TG], s2B,
                                  ALU.mult).then_inc(sems["dve"], 1)
        c.dve += 2 * FC

        def gpsimd(eng):
            for fc in range(FC):
                eng.wait_ge(sems["dve"], b_dve + 2 * (fc + 1))
                eng.wait_ge(sems["ld" + str(fc % 2)],
                            b_ld[fc % 2] + 16 * (fc // 2 + 1))
                eng.tensor_tensor(y[:, fc * TG:(fc + 1) * TG],
                                  tb[:, (fc % 2) * TG:(fc % 2 + 1) * TG],
                                  xb[:, (fc % 2) * TG:(fc % 2 + 1) * TG],
                                  ALU.add).then_inc(sems["gpc"], 1)
                eng.wait_ge(sems["gpc"], b_gpc + fc + 1)
                eng.dma_start(out=out_dram[fc * 128:(fc + 1) * 128, g * TG:(g + 1) * TG],
                              in_=y[:, fc * TG:(fc + 1) * TG]).then_inc(sems["gpd"], 16)
        c.gpc += FC
        c.gpd += 16 * FC

        return {"sync": sync, "vector": vector, "gpsimd": gpsimd}

    if defer:
        return body
    ctx.phase(body)
    ctx.A.reset(mk)


def emit_post_ln_residual(ctx, x_dram, g, y, rows, out_dram):
    mk = ctx.A.mark()
    m2B = ctx.A.alloc(TG)
    s2B = ctx.A.alloc(TG)
    emit_post_stats(ctx, y, rows, m2B, s2B)
    emit_resid_sbufy(ctx, x_dram, g, y, m2B, s2B, out_dram)
    ctx.A.reset(mk)


def emit_proj_sweep(ctx, xhat, wt_dram, n_out_chunks, out_cb, bias=None,
                    func=None, extra_dram=None, extra_rhs=None, defer=False,
                    w_bf16=False):
    c, sems, psum = ctx.c, ctx.sems, ctx.psum
    mk = ctx.A.mark()
    mkb = ctx.AB.mark()
    wcols = FC * 128 + (128 if extra_dram is not None else 0)
    wb = [ctx.AB.alloc(wcols) for _ in range(2)]
    ws = None if w_bf16 else [ctx.A.alloc(wcols) for _ in range(2)]
    NOC = n_out_chunks
    PEO = FC + (1 if extra_dram is not None else 0)
    NDMA = 1 + (1 if extra_dram is not None else 0)

    def body(block):
        b_ld = [c.ld0, c.ld1]
        b_pe, b_act, b_dve = c.pe, c.act, c.dve

        def sync(eng):
            for oc in range(NOC):
                dst = wb[oc % 2] if w_bf16 else ws[oc % 2]
                if oc >= 2:
                    if w_bf16:
                        eng.wait_ge(sems["pe"], b_pe + (oc - 1) * PEO)
                    else:
                        eng.wait_ge(sems["dve"], b_dve + oc - 1)
                eng.dma_start(
                    out=dst[:, 0:FC * 128],
                    in_=wt_dram[oc * 128:(oc + 1) * 128, :],
                ).then_inc(sems["ld" + str(oc % 2)], 16)
                if extra_dram is not None:
                    eng.dma_start(
                        out=dst[:, FC * 128:FC * 128 + 128],
                        in_=extra_dram[:, oc * 128:(oc + 1) * 128],
                    ).then_inc(sems["ld" + str(oc % 2)], 16)
        c.ld0 += 16 * NDMA * ((NOC + 1) // 2)
        c.ld1 += 16 * NDMA * (NOC // 2)

        def vector(eng):
            for oc in range(NOC):
                eng.wait_ge(sems["ld" + str(oc % 2)],
                            b_ld[oc % 2] + 16 * NDMA * (oc // 2 + 1))
                if oc >= 2:
                    eng.wait_ge(sems["pe"], b_pe + (oc - 1) * PEO)
                eng.tensor_copy(wb[oc % 2], ws[oc % 2]).then_inc(sems["dve"], 1)
        if not w_bf16:
            c.dve += NOC

        def tensor(eng):
            for oc in range(NOC):
                if w_bf16:
                    eng.wait_ge(sems["ld" + str(oc % 2)],
                                b_ld[oc % 2] + 16 * NDMA * (oc // 2 + 1))
                else:
                    eng.wait_ge(sems["dve"], b_dve + oc + 1)
                if oc >= 2:
                    eng.wait_ge(sems["act"], b_act + oc - 1)
                for kc in range(FC):
                    eng.matmul(psum[oc % 2][:, :TG],
                               wb[oc % 2][:, kc * 128:(kc + 1) * 128],
                               xhat[:, kc * TG:(kc + 1) * TG],
                               start=(kc == 0),
                               stop=(kc == FC - 1 and extra_dram is None)
                               ).then_inc(sems["pe"], 1)
                if extra_dram is not None:
                    eng.matmul(psum[oc % 2][:, :TG],
                               wb[oc % 2][:, FC * 128:FC * 128 + 128],
                               extra_rhs,
                               start=False, stop=True).then_inc(sems["pe"], 1)
        c.pe += NOC * PEO

        def scalar(eng):
            for oc in range(NOC):
                eng.wait_ge(sems["pe"], b_pe + (oc + 1) * PEO)
                f = func.get(oc, AF.Copy) if func else AF.Copy
                kw = {}
                if bias is not None:
                    kw["bias"] = bias[:, oc:oc + 1]
                    f = AF.Identity if f == AF.Copy else f
                eng.activation(out_cb(oc), psum[oc % 2][:, :TG], f, **kw
                               ).then_inc(sems["act"], 1)
        c.act += NOC

        secs = {"sync": sync, "tensor": tensor, "scalar": scalar}
        if not w_bf16:
            secs["vector"] = vector
        return secs

    if defer:
        return body
    ctx.phase(body)
    ctx.A.reset(mk)
    ctx.AB.reset(mkb)


def emit_l0a_prep(ctx):
    c, sems, small, psum = ctx.c, ctx.sems, ctx.small, ctx.psum
    io, dr = ctx.io, ctx.dr

    bm01 = ctx.A.alloc(128)
    mk = ctx.A.mark()
    o32a = ctx.A.alloc(FC * 128)
    mtile = ctx.A.alloc(2 * 512)
    mtile_f = ctx.A.alloc(2 * 512)
    t1 = ctx.A.alloc(512)
    tf = ctx.A.alloc(256 + 128 + 64 + 32)

    def body(block):
        b_ld = [c.ld0, c.ld1]
        b_lda = c.lda
        b_pe, b_dve, b_act = c.pe, c.dve, c.act

        def sync(eng):
            eng.dma_start(out=o32a, in_=io["c_o32a"]).then_inc(sems["lda"], 16)
            i = 0
            for kb in range(8):
                for ncc in range(FC):
                    if i >= 2:
                        j = i - 2
                        eng.wait_ge(sems["dve"], b_dve + (j + 1) + 6 * (j // FC))
                    eng.dma_start(
                        out=mtile[:, (i % 2) * 512:(i % 2 + 1) * 512].bitcast(I32),
                        in_=io["w_mask"][ncc * 128:(ncc + 1) * 128,
                                         kb * 512:(kb + 1) * 512],
                    ).then_inc(sems["ld" + str(i % 2)], 16)
                    i += 1
        c.lda += 16
        c.ld0 += 16 * (8 * FC // 2)
        c.ld1 += 16 * (8 * FC // 2)

        def vector(eng):
            i = 0
            for kb in range(8):
                for ncc in range(FC):
                    eng.wait_ge(sems["ld" + str(i % 2)],
                                b_ld[i % 2] + 16 * (i // 2 + 1))
                    if i >= 2:
                        eng.wait_ge(sems["pe"], b_pe + i - 1)
                    eng.tensor_copy(mtile_f[:, (i % 2) * 512:(i % 2 + 1) * 512],
                                    mtile[:, (i % 2) * 512:(i % 2 + 1) * 512]
                                    .bitcast(I32)).then_inc(sems["dve"], 1)
                    i += 1
                eng.wait_ge(sems["act"], b_act + kb + 1)
                dvn = b_dve + (kb + 1) * FC + 6 * kb
                srcs = [t1, tf[:, 0:256], tf[:, 256:384], tf[:, 384:448],
                        tf[:, 448:480]]
                for st in range(5):
                    h = 16 >> st
                    a3 = srcs[st].rearrange("p (k s) -> p k s", s=2 * h)
                    if st > 0:
                        eng.wait_ge(sems["dve"], dvn)
                    if st < 4:
                        d3 = srcs[st + 1].rearrange("p (k s) -> p k s", s=h)
                        eng.tensor_tensor(d3, a3[:, :, 0:h], a3[:, :, h:2 * h],
                                          ALU.add).then_inc(sems["dve"], 1)
                    else:
                        eng.tensor_tensor(bm01[:, kb * 16:(kb + 1) * 16],
                                          a3[:, :, 0:1].squeeze(2),
                                          a3[:, :, 1:2].squeeze(2), ALU.add
                                          ).then_inc(sems["dve"], 1)
                    dvn += 1
                eng.wait_ge(sems["dve"], dvn)
                eng.tensor_scalar(bm01[:, kb * 16:(kb + 1) * 16],
                                  bm01[:, kb * 16:(kb + 1) * 16],
                                  0.0, None, ALU.is_gt).then_inc(sems["dve"], 1)
        c.dve += 8 * FC + 8 * 6

        def tensor(eng):
            eng.wait_ge(sems["lda"], b_lda + 16)
            i = 0
            for kb in range(8):
                if kb >= 1:
                    eng.wait_ge(sems["act"], b_act + kb)
                for ncc in range(FC):
                    eng.wait_ge(sems["dve"], b_dve + i + 1 + 6 * kb)
                    eng.matmul(psum[2][:, :], o32a[:, ncc * 128:(ncc + 1) * 128],
                               mtile_f[:, (i % 2) * 512:(i % 2 + 1) * 512],
                               start=(ncc == 0), stop=(ncc == FC - 1)
                               ).then_inc(sems["pe"], 1)
                    i += 1
        c.pe += 8 * FC

        def scalar(eng):
            for kb in range(8):
                eng.wait_ge(sems["pe"], b_pe + (kb + 1) * FC)
                if kb >= 1:
                    eng.wait_ge(sems["dve"], b_dve + 38 * kb)
                eng.activation(t1, psum[2][:, :], AF.Copy).then_inc(sems["act"], 1)
        c.act += 8

        return {"sync": sync, "vector": vector, "tensor": tensor, "scalar": scalar}

    ctx.phase(body)
    ctx.A.reset(mk)
    if DEBUG_TAP == "bm01":
        emit_tap_sbuf(ctx, bm01)
        return
    if DEBUG_STOP == "bm":
        return

    mk = ctx.A.mark()
    mkb = ctx.AB.mark()
    rsel = ctx.A.alloc(FC * 128)
    xex = ctx.A.alloc(D)
    wc = [ctx.A.alloc(D) for _ in range(2)]
    e1t = ctx.A.alloc(128)
    bmE = ctx.A.alloc(D)
    wm = ctx.A.alloc(D)
    tb = [ctx.AB.alloc(512) for _ in range(2)]

    def body2(block):
        b_ld = [c.ld0, c.ld1]
        b_lda = c.lda
        b_gp = [c.gp0, c.gp1]
        b_pe, b_dve, b_act = c.pe, c.dve, c.act

        def sync(eng):
            eng.dma_start(out=rsel, in_=io["c_rsel"]).then_inc(sems["lda"], 16)
            eng.dma_start(out=xex, in_=io["c_xex"]).then_inc(sems["lda"], 16)
            for ncc in range(FC):
                if ncc >= 2:
                    eng.wait_ge(sems["dve"], b_dve + ncc - 1)
                eng.dma_start(out=wc[ncc % 2],
                              in_=io["w_weight"][ncc * 128:(ncc + 1) * 128, :]
                              ).then_inc(sems["ld" + str(ncc % 2)], 16)
        c.lda += 32
        c.ld0 += 16 * (FC // 2)
        c.ld1 += 16 * (FC // 2)

        def tensor(eng):
            eng.wait_ge(sems["lda"], b_lda + 32)
            for ncc in range(FC):
                pe0 = b_pe + ncc * 41
                act0 = b_act + ncc * 17
                eng.wait_ge(sems["ld" + str(ncc % 2)],
                            b_ld[ncc % 2] + 16 * (ncc // 2 + 1))
                if ncc >= 1:
                    eng.wait_ge(sems["act"], act0)
                eng.matmul(psum[2][:, 0:128], bm01,
                           rsel[:, ncc * 128:(ncc + 1) * 128],
                           start=True, stop=True).then_inc(sems["pe"], 1)
                for kb in range(8):
                    eng.wait_ge(sems["act"], act0 + 1)
                    if kb >= 2:
                        eng.wait_ge(sems["act"], act0 + kb)
                    eng.matmul(psum[3 + kb % 2][:, :], e1t,
                               xex[:, kb * 512:(kb + 1) * 512],
                               start=True, stop=True).then_inc(sems["pe"], 1)
                eng.wait_ge(sems["dve"], b_dve + ncc + 1)
                for sub in range(FC):
                    g = sub // 4
                    if g >= 2 and sub % 4 == 0:
                        eng.wait_ge(sems["act"], act0 + 9 + g - 1)
                    eng.transpose(psum[g % 2][:, (sub % 4) * 128:(sub % 4 + 1) * 128],
                                  wm[:, sub * 128:(sub + 1) * 128],
                                  small["c_ident"]).then_inc(sems["pe"], 1)
        c.pe += FC * 41

        def scalar(eng):
            for ncc in range(FC):
                pe0 = b_pe + ncc * 41
                eng.wait_ge(sems["pe"], pe0 + 1)
                eng.activation(e1t, psum[2][:, 0:128], AF.Copy).then_inc(sems["act"], 1)
                for kb in range(8):
                    eng.wait_ge(sems["pe"], pe0 + 1 + kb + 1)
                    eng.activation(bmE[:, kb * 512:(kb + 1) * 512],
                                   psum[3 + kb % 2][:, :], AF.Copy
                                   ).then_inc(sems["act"], 1)
                for g in range(8):
                    eng.wait_ge(sems["pe"], pe0 + 9 + (g + 1) * 4)
                    sg = ncc * 8 + g
                    if sg >= 2:
                        eng.wait_ge(sems["gp" + str(sg % 2)],
                                    b_gp[sg % 2] + 64 * (sg // 2))
                    eng.activation(tb[g % 2], psum[g % 2][:, :], AF.Copy
                                   ).then_inc(sems["act"], 1)
        c.act += FC * 17

        def vector(eng):
            for ncc in range(FC):
                act0 = b_act + ncc * 17
                eng.wait_ge(sems["act"], act0 + 9)
                if ncc >= 1:
                    eng.wait_ge(sems["pe"], b_pe + ncc * 41)
                eng.tensor_tensor(wm, wc[ncc % 2], bmE, ALU.mult).then_inc(sems["dve"], 1)
        c.dve += FC

        def gpsimd(eng):
            for ncc in range(FC):
                act0 = b_act + ncc * 17
                for g in range(8):
                    sg = ncc * 8 + g
                    eng.wait_ge(sems["act"], act0 + 9 + g + 1)
                    for q in range(4):
                        sub = g * 4 + q
                        eng.dma_start(
                            out=dr["wtdram"][ncc * 128:(ncc + 1) * 128,
                                             sub * 128:(sub + 1) * 128],
                            in_=tb[g % 2][:, q * 128:(q + 1) * 128],
                        ).then_inc(sems["gp" + str(sg % 2)], 16)
        c.gp0 += 64 * (FC * 8 // 2)
        c.gp1 += 64 * (FC * 8 // 2)

        return {"sync": sync, "tensor": tensor, "scalar": scalar,
                "vector": vector, "gpsimd": gpsimd}

    ctx.phase(body2)
    ctx.A.reset(mk)
    ctx.AB.reset(mkb)
    if DEBUG_TAP == "wt":
        emit_tap_dram(ctx, dr["wtdram"], D, TOK)


def emit_l0a(ctx):
    c = ctx.c
    io, dr, sems = ctx.io, ctx.dr, ctx.sems
    mk_pre = ctx.A.mark()
    emit_l0a_prep(ctx)
    ctx.A.reset(mk_pre)
    if DEBUG_STOP in ("prep", "bm") or DEBUG_TAP in ("bm01", "wt"):
        return

    mk0 = ctx.A.mark()
    mkb0 = ctx.AB.mark()
    rows = ctx.A.alloc(3 * TG)[0:1, :]
    biasb = ctx.A.alloc(FC)
    mB = ctx.A.alloc(TG)
    sB = ctx.A.alloc(TG)
    xhat = ctx.AB.alloc(FC * TG)
    y = ctx.A.alloc(FC * TG)

    def body_bias(block):
        def sync(eng):
            for fcb in range(FC):
                eng.dma_start(out=biasb[:, fcb:fcb + 1],
                              in_=io["w_bias"][fcb * 128:(fcb + 1) * 128, :]
                              ).then_inc(sems["lda"], 16)
        c.lda += 16 * FC
        return {"sync": sync}
    ctx.phase(body_bias)

    m2B = ctx.A.alloc(TG)
    s2B = ctx.A.alloc(TG)
    emit_ln_stats_dram(ctx, dr["xdram"], 0, mB, sB, rows)
    emit_ln_apply_dram(ctx, dr["xdram"], 0, mB, sB, xhat)
    for g in range(NG):
        mk = ctx.A.mark()
        mkb = ctx.AB.mark()
        bodies = [emit_proj_sweep(ctx, xhat, dr["wtdram"], FC,
                                  lambda oc: y[:, oc * TG:(oc + 1) * TG],
                                  bias=biasb, defer=True, w_bf16=True)]
        if g + 1 < NG:
            bodies.append(emit_ln_stats_dram(ctx.alt(), dr["xdram"], g + 1, mB,
                                             sB, rows, defer=True, slim=True))
        ctx.phases(bodies)
        ctx.A.reset(mk)
        ctx.AB.reset(mkb)
        emit_post_stats(ctx, y, rows, m2B, s2B)
        mk = ctx.A.mark()
        bodies = [emit_resid_sbufy(ctx, dr["xdram"], g, y, m2B, s2B,
                                   dr["xdram"], defer=True)]
        if g + 1 < NG:
            bodies.append(emit_ln_apply_dram(ctx.alt(), dr["xdram"], g + 1, mB,
                                             sB, xhat, defer=True))
        ctx.phases(bodies)
        ctx.A.reset(mk)
    ctx.A.reset(mk0)
    ctx.AB.reset(mkb0)


def emit_l0f(ctx):
    c = ctx.c
    io, dr, sems, small, psum = ctx.io, ctx.dr, ctx.sems, ctx.small, ctx.psum

    mk0 = ctx.A.mark()
    mkb0 = ctx.AB.mark()
    rows = ctx.A.alloc(3 * TG)[0:1, :]
    mB = ctx.A.alloc(TG)
    sB = ctx.A.alloc(TG)
    idxb = ctx.A.alloc(FC)
    xhat = ctx.AB.alloc(FC * TG)

    def body_idx(block):
        def sync(eng):
            for ncc in range(FC):
                eng.dma_start(out=idxb[:, ncc:ncc + 1].bitcast(I32),
                              in_=io["w_perm"][ncc * 128:(ncc + 1) * 128, :]
                              ).then_inc(sems["lda"], 16)
        c.lda += 16 * FC
        return {"sync": sync}
    ctx.phase(body_idx)

    m2B = ctx.A.alloc(TG)
    s2B = ctx.A.alloc(TG)
    emit_ln_stats_dram(ctx, dr["xdram"], 0, mB, sB, rows)
    emit_ln_apply_dram(ctx, dr["xdram"], 0, mB, sB, xhat,
                       store_dram=GroupedDram(dr["sbdram"], 0))
    for g in range(NG):
        mk = ctx.A.mark()
        mkb = ctx.AB.mark()
        bodies = [_emit_l0f_heads(ctx, g, idxb, defer=True)]
        if g + 1 < NG:
            bodies.append(emit_ln_stats_dram(ctx.alt(), dr["xdram"], g + 1, mB,
                                             sB, rows, psb=(4, 5, 4, 5), defer=True,
                                             slim=True))
        ctx.phases(bodies)
        ctx.A.reset(mk)
        ctx.AB.reset(mkb)
        emit_ln_stats_dram(ctx, GroupedDram(dr["s2dram"], g), g, m2B, s2B, rows)
        mk = ctx.A.mark()
        bodies = [emit_residual_from_dram(ctx, dr["xdram"],
                                          GroupedDram(dr["s2dram"], g),
                                          g, m2B, s2B, dr["xdram"], defer=True)]
        if g + 1 < NG:
            bodies.append(emit_ln_apply_dram(
                ctx.alt(), dr["xdram"], g + 1, mB, sB, xhat,
                store_dram=GroupedDram(dr["sbdram"], g + 1), defer=True))
        ctx.phases(bodies)
        ctx.A.reset(mk)
    ctx.A.reset(mk0)
    ctx.AB.reset(mkb0)


def _emit_l0f_heads(ctx, g, idxb, defer=False):
    c, sems, small, psum = ctx.c, ctx.sems, ctx.small, ctx.psum
    io, dr = ctx.io, ctx.dr
    mk = ctx.A.mark()
    xg = [ctx.AB.alloc(2 * TG) for _ in range(2)]
    hid = ctx.AB.alloc(8 * TG)
    yh = [ctx.A.alloc(2 * TG) for _ in range(2)]
    wub = [ctx.A.alloc(2 * FM) for _ in range(2)]
    wdb = [ctx.A.alloc(8 * FD) for _ in range(2)]
    wubB = [ctx.AB.alloc(2 * FM) for _ in range(2)]
    wdbB = [ctx.AB.alloc(8 * FD) for _ in range(2)]

    def body(block):
        b_ld = [c.ld0, c.ld1]
        b_sc = [c.sc0, c.sc1]
        b_gp = [c.gp0, c.gp1, c.gp2, c.gp3]
        b_pe, b_act, b_gpc, b_dve = c.pe, c.act, c.gpc, c.dve

        def sync(eng):
            for h in range(FH):
                if h >= 2:
                    eng.wait_ge(sems["dve"], b_dve + 2 * (h - 2) + 1)
                eng.dma_start(
                    out=wub[h % 2].rearrange("p (kc m) -> p kc m", kc=2),
                    in_=io["w_up"][h * 256:(h + 1) * 256, :]
                        .rearrange("(kc p) m -> p kc m", p=128),
                ).then_inc(sems["ld" + str(h % 2)], 16)
        c.ld0 += 16 * (FH // 2)
        c.ld1 += 16 * (FH // 2)

        def vector(eng):
            for h in range(FH):
                eng.wait_ge(sems["ld" + str(h % 2)],
                            b_ld[h % 2] + 16 * (h // 2 + 1))
                if h >= 2:
                    eng.wait_ge(sems["pe"], b_pe + (h - 1) * 32)
                eng.tensor_copy(wubB[h % 2], wub[h % 2]).then_inc(sems["dve"], 1)
                eng.wait_ge(sems["sc" + str(h % 2)],
                            b_sc[h % 2] + 16 * (h // 2 + 1))
                eng.tensor_copy(wdbB[h % 2], wdb[h % 2]).then_inc(sems["dve"], 1)
        c.dve += 2 * FH

        def scalar_dma_and_act(eng):
            for h in range(FH):
                if h >= 2:
                    eng.wait_ge(sems["dve"], b_dve + 2 * (h - 2) + 2)
                eng.dma_start(
                    out=wdb[h % 2].rearrange("p (mc d) -> p mc d", mc=8),
                    in_=io["w_down"][h * 1024:(h + 1) * 1024, :]
                        .rearrange("(mc p) d -> p mc d", p=128),
                ).then_inc(sems["sc" + str(h % 2)], 16)
                pe0 = b_pe + h * 32
                act0 = b_act + h * 10
                for mo in range(8):
                    eng.wait_ge(sems["pe"], pe0 + 2 * (mo + 1))
                    eng.activation(hid[:, mo * TG:(mo + 1) * TG],
                                   psum[mo % 2][:, :TG], AF.Gelu
                                   ).then_inc(sems["act"], 1)
                for oc in range(2):
                    eng.wait_ge(sems["pe"], pe0 + 16 + 8 * (oc + 1))
                    if h >= 2 and oc == 0:
                        eng.wait_ge(sems["gp" + str(2 + h % 2)],
                                    b_gp[2 + h % 2] + 32 * (h // 2))
                    eng.activation(yh[h % 2][:, oc * TG:(oc + 1) * TG],
                                   psum[2 + oc][:, :TG], AF.Copy
                                   ).then_inc(sems["act"], 1)
        c.sc0 += 16 * (FH // 2)
        c.sc1 += 16 * (FH // 2)
        c.act += FH * 10

        def tensor(eng):
            for h in range(FH):
                pe0 = b_pe + h * 32
                act0 = b_act + h * 10
                eng.wait_ge(sems["dve"], b_dve + 2 * h + 1)
                eng.wait_ge(sems["gp" + str(h % 2)],
                            b_gp[h % 2] + 32 * (h // 2) + 32)
                if h >= 1:
                    eng.wait_ge(sems["act"], act0)
                for mo in range(8):
                    if mo >= 2:
                        eng.wait_ge(sems["act"], act0 + mo - 1)
                    for kc in range(2):
                        eng.matmul(psum[mo % 2][:, :TG],
                                   wubB[h % 2][:, (kc * 8 + mo) * 128:
                                               (kc * 8 + mo + 1) * 128],
                                   xg[h % 2][:, kc * TG:(kc + 1) * TG],
                                   start=(kc == 0), stop=(kc == 1)
                                   ).then_inc(sems["pe"], 1)
                eng.wait_ge(sems["dve"], b_dve + 2 * h + 2)
                eng.wait_ge(sems["act"], act0 + 8)
                for oc in range(2):
                    for mc in range(8):
                        eng.matmul(psum[2 + oc][:, :TG],
                                   wdbB[h % 2][:, (mc * 2 + oc) * 128:
                                               (mc * 2 + oc + 1) * 128],
                                   hid[:, mc * TG:(mc + 1) * TG],
                                   start=(mc == 0), stop=(mc == 7)
                                   ).then_inc(sems["pe"], 1)
        c.pe += FH * 32

        def gpsimd(eng):
            for h in range(FH):
                act0 = b_act + h * 10
                if h >= 2:
                    eng.wait_ge(sems["pe"], b_pe + (h - 1) * 32)
                for kc in range(2):
                    eng.indirect_dma_start(
                        out=xg[h % 2][:, kc * TG:(kc + 1) * TG],
                        out_offset=None,
                        in_=dr["sbdram"],
                        in_offset=bass.IndirectOffsetOnAxis(
                            idxb[:, 2 * h + kc:2 * h + kc + 1].bitcast(I32), 0),
                        element_offset=g * D * TG,
                    ).then_inc(sems["gp" + str(h % 2)], 16)
                eng.wait_ge(sems["act"], act0 + 10)
                for oc in range(2):
                    eng.indirect_dma_start(
                        out=dr["s2dram"],
                        out_offset=bass.IndirectOffsetOnAxis(
                            idxb[:, 2 * h + oc:2 * h + oc + 1].bitcast(I32), 0),
                        in_=yh[h % 2][:, oc * TG:(oc + 1) * TG],
                        in_offset=None,
                        element_offset=g * D * TG,
                    ).then_inc(sems["gp" + str(2 + h % 2)], 16)
        c.gp0 += 32 * (FH // 2)
        c.gp1 += 32 * (FH // 2)
        c.gp2 += 32 * (FH // 2)
        c.gp3 += 32 * (FH // 2)

        return {"sync": sync, "scalar": scalar_dma_and_act, "tensor": tensor,
                "vector": vector, "gpsimd": gpsimd}

    if defer:
        return body
    ctx.phase(body)
    ctx.A.reset(mk)


def emit_load_arena(ctx, src_gview, g, dst, defer=False):
    c, sems = ctx.c, ctx.sems

    def body(block):
        b_ld = [c.ld0, c.ld1]

        def sync(eng):
            for fc in range(FC):
                eng.dma_start(out=dst[:, fc * TG:(fc + 1) * TG],
                              in_=src_gview[fc * 128:(fc + 1) * 128,
                                            g * TG:(g + 1) * TG]
                              ).then_inc(sems["ld" + str(fc % 2)], 16)
        c.ld0 += 16 * (FC // 2)
        c.ld1 += 16 * (FC // 2)
        return {"sync": sync}
    if defer:
        return body
    ctx.phase(body)


def emit_l1a(ctx):
    c = ctx.c
    io, dr, sems, small, psum = ctx.io, ctx.dr, ctx.sems, ctx.small, ctx.psum

    mk0 = ctx.A.mark()
    mkb0 = ctx.AB.mark()
    rows = ctx.A.alloc(3 * TG)[0:1, :]
    mB = ctx.A.alloc(TG)
    sB = ctx.A.alloc(TG)
    xhat = ctx.AB.alloc(FC * TG)
    mk_attn = ctx.A.mark()
    ropec = ctx.A.alloc(TOK)
    ropes = ctx.A.alloc(TOK)
    pswap = ctx.A.alloc(128)
    kposb = ctx.A.alloc(FC)
    qposb = ctx.A.alloc(TOK)[0:1, :]
    qn2 = ctx.A.alloc(TOK)[0:1, :]
    kn2 = ctx.A.alloc(8)[0:1, :]
    kfm = ctx.A.alloc(4)[0:1, 0:1]

    def body_pre(block):
        def sync(eng):
            eng.dma_start(out=ropec, in_=io["c_ropec"]).then_inc(sems["lda"], 16)
            eng.dma_start(out=ropes, in_=io["c_ropes"]).then_inc(sems["lda"], 16)
            eng.dma_start(out=pswap, in_=io["c_pswap"]).then_inc(sems["lda"], 16)
            eng.dma_start(out=qposb, in_=io["c_qpos"]).then_inc(sems["lda"], 16)
            for fcb in range(FC):
                eng.dma_start(out=kposb[:, fcb:fcb + 1],
                              in_=io["c_kpos"][fcb * 128:(fcb + 1) * 128, :]
                              ).then_inc(sems["lda"], 16)
        c.lda += 16 * (4 + FC)

        def vector(eng):
            eng.memset(kn2, -1.0e30).then_inc(sems["dve"], 1)
        c.dve += 1
        return {"sync": sync, "vector": vector}
    ctx.phase(body_pre)

    for g in range(NG):
        emit_ln_stats_dram(ctx, dr["xdram"], g, mB, sB, rows)
        emit_ln_apply_dram(ctx, dr["xdram"], g, mB, sB, xhat)
        _emit_l1a_qkv(ctx, g, xhat, ropec, ropes, pswap, qn2, kn2)

    _emit_l1a_collective(ctx, kn2, kfm)

    ctx.AB.reset(mkb0)
    cmask = ctx.AB.alloc(FC * TG)
    for g in range(NG):
        _emit_l1a_masks(ctx, g, qposb, kposb, qn2, kfm, cmask)
        for kvh in range(NKV):
            _emit_l1a_attn(ctx, g, kvh, cmask)

    ctx.A.reset(mk_attn)
    ctx.AB.reset(mkb0)
    ahat = ctx.AB.alloc(FC * TG)
    y = ctx.A.alloc(FC * TG)
    m2B = ctx.A.alloc(TG)
    s2B = ctx.A.alloc(TG)
    emit_load_arena(ctx, GroupedDram(dr["sbdram"], 0), 0, ahat)
    for g in range(NG):
        mk = ctx.A.mark()
        mkb = ctx.AB.mark()
        bodies = [emit_proj_sweep(ctx, ahat, io["w_wo"], FC,
                                  lambda oc: y[:, oc * TG:(oc + 1) * TG],
                                  defer=True)]
        ctx.phases(bodies)
        ctx.A.reset(mk)
        ctx.AB.reset(mkb)
        emit_post_stats(ctx, y, rows, m2B, s2B)
        mk = ctx.A.mark()
        bodies = [emit_resid_sbufy(ctx, dr["xdram"], g, y, m2B, s2B,
                                   dr["xdram"], defer=True)]
        if g + 1 < NG:
            bodies.append(emit_load_arena(ctx.alt(), GroupedDram(dr["sbdram"], g + 1),
                                          g + 1, ahat, defer=True))
        ctx.phases(bodies)
        ctx.A.reset(mk)
    ctx.A.reset(mk0)
    ctx.AB.reset(mkb0)


def _emit_l1a_qkv(ctx, g, xhat, ropec, ropes, pswap, qn2, kn2):
    c, sems, small, psum = ctx.c, ctx.sems, ctx.small, ctx.psum
    io, dr = ctx.io, ctx.dr
    mk = ctx.A.mark()
    mkb = ctx.AB.mark()
    wb = [ctx.A.alloc(FC * 128) for _ in range(2)]
    wbB = [ctx.AB.alloc(FC * 128) for _ in range(2)]
    t0 = ctx.A.alloc(TG)
    t1 = ctx.A.alloc(TG)
    t2 = ctx.A.alloc(TG)
    t3 = ctx.A.alloc(TG)
    tfb = [ctx.AB.alloc(TG) for _ in range(2)]
    sq = ctx.A.alloc(TG)
    nrowb = ctx.A.alloc(2 * TG)[0:1, :]
    nredb = ctx.A.alloc(4)[0:1, :]
    vt = [ctx.AB.alloc(TG) for _ in range(2)]
    Cg = ropec[:, g * TG:(g + 1) * TG]
    Sg = ropes[:, g * TG:(g + 1) * TG]
    vv = dr["ccin"].rearrange("r (h c) -> (r h) c", h=2)

    items = ([("k", i) for i in range(NKV)] + [("v", i) for i in range(NKV)]
             + [("q", i) for i in range(FC)])
    def pe_ops(kind):
        return 32 + (2 if kind in ("q", "k") else 4)

    def act_ops(kind):
        return 4 if kind in ("q", "k") else 2

    def dve_ops(kind, it):
        if kind == "k":
            return 6
        if kind == "q":
            return 5
        return 1

    pe_bases = np.cumsum([0] + [pe_ops(k) for k, _ in items]).tolist()
    act_bases = np.cumsum([0] + [act_ops(k) for k, _ in items]).tolist()
    dve_bases = np.cumsum([0] + [dve_ops(k, i) for k, i in items]).tolist()
    st_pref = []
    tot = {0: 0, 1: 0, 2: 0, 3: 0}
    st_sem = []
    for i, (kind, it) in enumerate(items):
        s = (i % 2) if kind in ("q", "k") else (2 + i % 2)
        st_pref.append(tot[s])
        st_sem.append(s)
        tot[s] += 16 if kind in ("q", "k") else 64

    def wsrc(kind, it):
        w = {"q": io["w_wq"], "k": io["w_wk"], "v": io["w_wv"]}[kind]
        return w[it * 128:(it + 1) * 128, :]

    def body(block):
        b_ld = [c.ld0, c.ld1]
        b_gp = [c.gp0, c.gp1, c.gp2, c.gp3]
        b_pe, b_act, b_dve = c.pe, c.act, c.dve

        def sync(eng):
            for i, (kind, it) in enumerate(items):
                if i >= 2:
                    eng.wait_ge(sems["dve"], b_dve + dve_bases[i - 2] + 1)
                eng.dma_start(
                    out=wb[i % 2],
                    in_=wsrc(kind, it),
                ).then_inc(sems["ld" + str(i % 2)], 16)
        c.ld0 += 16 * ((len(items) + 1) // 2)
        c.ld1 += 16 * (len(items) // 2)

        def tensor(eng):
            for i, (kind, it) in enumerate(items):
                pe0 = b_pe + pe_bases[i]
                act0 = b_act + act_bases[i]
                eng.wait_ge(sems["dve"], b_dve + dve_bases[i] + 1)
                if i >= 2:
                    eng.wait_ge(sems["act"], b_act + act_bases[i - 1] + 1)
                for kc in range(FC):
                    eng.matmul(psum[i % 2][:, :TG],
                               wbB[i % 2][:, kc * 128:(kc + 1) * 128],
                               xhat[:, kc * TG:(kc + 1) * TG],
                               start=(kc == 0), stop=(kc == FC - 1)
                               ).then_inc(sems["pe"], 1)
                if kind in ("q", "k"):
                    eng.wait_ge(sems["act"], act0 + 1)
                    if i >= 2:
                        eng.wait_ge(sems["act"], b_act + act_bases[i - 1] + 2)
                    eng.matmul(psum[2 + i % 2][:, :TG], pswap, t0,
                               start=True, stop=True).then_inc(sems["pe"], 1)
                    eng.wait_ge(sems["act"], act0 + 3)
                    eng.matmul(psum[4][0:1, :TG], small["c_onescol"], sq,
                               start=True, stop=True).then_inc(sems["pe"], 1)
                else:
                    eng.wait_ge(sems["act"], act0 + 1)
                    if i >= 2:
                        eng.wait_ge(sems["act"], b_act + act_bases[i - 1] + 2)
                    for q4 in range(4):
                        eng.transpose(psum[5][:, q4 * 128:(q4 + 1) * 128],
                                      t0[:, q4 * 128:(q4 + 1) * 128],
                                      small["c_ident"]).then_inc(sems["pe"], 1)
        c.pe += pe_bases[-1]

        def scalar(eng):
            for i, (kind, it) in enumerate(items):
                pe0 = b_pe + pe_bases[i]
                act0 = b_act + act_bases[i]
                dve0 = b_dve + dve_bases[i]
                eng.wait_ge(sems["pe"], pe0 + FC)
                if i >= 1:
                    eng.wait_ge(sems["dve"], b_dve + dve_bases[i])
                if kind == "q":
                    eng.activation(t0, psum[i % 2][:, :TG], AF.Identity,
                                   scale=small["c_rsqhd"]).then_inc(sems["act"], 1)
                else:
                    eng.activation(t0, psum[i % 2][:, :TG], AF.Copy
                                   ).then_inc(sems["act"], 1)
                if kind in ("q", "k"):
                    eng.wait_ge(sems["pe"], pe0 + FC + 1)
                    eng.activation(t1, psum[2 + i % 2][:, :TG], AF.Copy
                                   ).then_inc(sems["act"], 1)
                    eng.activation(sq, t0, AF.Square).then_inc(sems["act"], 1)
                    eng.wait_ge(sems["pe"], pe0 + FC + 2)
                    eng.activation(nrowb[0:1, (i % 2) * TG:(i % 2) * TG + TG],
                                   psum[4][0:1, :TG], AF.Copy
                                   ).then_inc(sems["act"], 1)
                else:
                    eng.wait_ge(sems["pe"], pe0 + FC + 4)
                    if i >= 2 and items[i - 2][0] == "v":
                        eng.wait_ge(sems["gp" + str(st_sem[i])],
                                    b_gp[st_sem[i]] + st_pref[i - 2] + 64)
                    eng.activation(vt[i % 2], psum[5][:, :TG], AF.Copy
                                   ).then_inc(sems["act"], 1)
        c.act += act_bases[-1]

        def vector(eng):
            for i, (kind, it) in enumerate(items):
                act0 = b_act + act_bases[i]
                dvn = b_dve + dve_bases[i]
                eng.wait_ge(sems["ld" + str(i % 2)],
                            b_ld[i % 2] + 16 * (i // 2 + 1))
                if i >= 2:
                    eng.wait_ge(sems["pe"], b_pe + pe_bases[i - 1])
                eng.tensor_copy(wbB[i % 2], wb[i % 2]).then_inc(sems["dve"], 1)
                dvn += 1
                if kind not in ("q", "k"):
                    continue
                nsl = nrowb[0:1, (i % 2) * TG:(i % 2) * TG + TG]
                nred = nredb[0:1, (i % 2):(i % 2) + 1]
                eng.wait_ge(sems["dve"], dvn)
                eng.wait_ge(sems["act"], act0 + 2)
                eng.tensor_tensor(t2, t0, Cg, ALU.mult).then_inc(sems["dve"], 1)
                dvn += 1
                eng.tensor_tensor(t3, t1, Sg, ALU.mult).then_inc(sems["dve"], 1)
                dvn += 1
                eng.wait_ge(sems["dve"], dvn)
                if st_pref[i] > 0:
                    eng.wait_ge(sems["gp" + str(st_sem[i])],
                                b_gp[st_sem[i]] + st_pref[i])
                eng.tensor_tensor(tfb[i % 2], t2, t3, ALU.add
                                  ).then_inc(sems["dve"], 1)
                dvn += 1
                eng.wait_ge(sems["act"], act0 + 4)
                if kind == "k":
                    eng.tensor_reduce(nred, nsl, mybir.AxisListType.X, ALU.max
                                      ).then_inc(sems["dve"], 1)
                    dvn += 1
                    eng.wait_ge(sems["dve"], dvn)
                    eng.tensor_tensor(kn2[0:1, it:it + 1], kn2[0:1, it:it + 1],
                                      nred, ALU.max).then_inc(sems["dve"], 1)
                    dvn += 1
                else:
                    if it == 0:
                        eng.tensor_copy(qn2[0:1, g * TG:(g + 1) * TG], nsl
                                        ).then_inc(sems["dve"], 1)
                    else:
                        eng.tensor_tensor(qn2[0:1, g * TG:(g + 1) * TG],
                                          qn2[0:1, g * TG:(g + 1) * TG], nsl,
                                          ALU.max).then_inc(sems["dve"], 1)
                    dvn += 1
        c.dve += dve_bases[-1]

        def gpsimd(eng):
            for i, (kind, it) in enumerate(items):
                act0 = b_act + act_bases[i]
                dve0 = b_dve + dve_bases[i]
                if kind == "q":
                    eng.wait_ge(sems["dve"], dve0 + 4)
                    eng.dma_start(out=dr["qdram"][it * 128:(it + 1) * 128,
                                                  g * TG:(g + 1) * TG],
                                  in_=tfb[i % 2]).then_inc(sems["gp" + str(i % 2)], 16)
                elif kind == "k":
                    eng.wait_ge(sems["dve"], dve0 + 4)
                    eng.dma_start(out=dr["ccin"][it * 128:(it + 1) * 128,
                                                 g * TG:(g + 1) * TG],
                                  in_=tfb[i % 2]).then_inc(sems["gp" + str(i % 2)], 16)
                else:
                    eng.wait_ge(sems["act"], act0 + 2)
                    for q4 in range(4):
                        vrow = 2 * 1024 + g * TG + q4 * 128
                        eng.dma_start(
                            out=vv[vrow:vrow + 128, it * 128:(it + 1) * 128],
                            in_=vt[i % 2][:, q4 * 128:(q4 + 1) * 128],
                        ).then_inc(sems["gp" + str(2 + i % 2)], 16)
        c.gp0 += tot[0]
        c.gp1 += tot[1]
        c.gp2 += tot[2]
        c.gp3 += tot[3]

        return {"sync": sync, "tensor": tensor, "scalar": scalar,
                "vector": vector, "gpsimd": gpsimd}

    ctx.phase(body)
    ctx.A.reset(mk)
    ctx.AB.reset(mkb)


def _emit_l1a_collective(ctx, kn2, kfm):
    c, sems = ctx.c, ctx.sems
    io, dr = ctx.io, ctx.dr
    mk = ctx.A.mark()
    mkb = ctx.AB.mark()
    knr = ctx.AB.alloc(16)[0:1, :]
    knm = ctx.AB.alloc(2)[0:1, 0:1]
    kn2b = ctx.AB.alloc(8)[0:1, :]
    zrowb = ctx.AB.alloc(TOK)[0:1, :]

    def body_cc(block):
        b_gpd, b_cc = c.gpd, c.cc
        b_dve = c.dve

        def vector(eng):
            eng.memset(zrowb, 0.0).then_inc(sems["dve"], 1)
            eng.tensor_copy(kn2b, kn2).then_inc(sems["dve"], 1)
        c.dve += 2

        def gpsimd(eng):
            eng.wait_ge(sems["dve"], b_dve + 2)
            eng.dma_start(out=dr["ccin"][2048:2049, 8:TOK], in_=zrowb[0:1, 8:TOK]
                          ).then_inc(sems["gpd"], 16)
            eng.dma_start(out=dr["ccin"][2048:2049, 0:8], in_=kn2b
                          ).then_inc(sems["gpd"], 16)
            eng.wait_ge(sems["gpd"], b_gpd + 32)
            eng.collective_compute(
                "AllGather", ALU.bypass, replica_groups=CC_GROUPS,
                ins=[dr["ccin"]], outs=[dr["ccout"]],
            ).then_inc(sems["cc"], 1)
            eng.wait_ge(sems["cc"], b_cc + 1)
        c.gpd += 32
        c.cc += 1
        return {"vector": vector, "gpsimd": gpsimd}
    ctx.phase(body_cc)

    def body_fin(block):
        b_lda = c.lda
        b_dve = c.dve
        b_act = c.act

        def sync(eng):
            eng.dma_start(out=knr[0:1, 0:8], in_=dr["ccout"][2048:2049, 0:8]
                          ).then_inc(sems["lda"], 16)
            eng.dma_start(out=knr[0:1, 8:16],
                          in_=dr["ccout"][2049 + 2048:2049 + 2049, 0:8]
                          ).then_inc(sems["lda"], 16)
        c.lda += 32

        def vector(eng):
            eng.wait_ge(sems["lda"], b_lda + 32)
            eng.tensor_tensor(knr[0:1, 0:8], knr[0:1, 0:8], knr[0:1, 8:16],
                              ALU.max).then_inc(sems["dve"], 1)
            eng.wait_ge(sems["dve"], b_dve + 1)
            eng.tensor_reduce(knm, knr[0:1, 0:8], mybir.AxisListType.X,
                              ALU.max).then_inc(sems["dve"], 1)
        c.dve += 2

        def scalar(eng):
            eng.wait_ge(sems["dve"], b_dve + 2)
            eng.activation(kfm, knm, AF.Sqrt).then_inc(sems["act"], 1)
        c.act += 1
        return {"sync": sync, "vector": vector, "scalar": scalar}
    ctx.phase(body_fin)
    ctx.A.reset(mk)
    ctx.AB.reset(mkb)


def _emit_l1a_masks(ctx, g, qposb, kposb, qn2, kfm, cmask):
    c, sems, small, psum = ctx.c, ctx.sems, ctx.small, ctx.psum
    mk = ctx.A.mark()
    qpb = ctx.A.alloc(TG)
    cB = ctx.A.alloc(TG)
    crow = ctx.A.alloc(TG)[0:1, :]
    m01 = ctx.A.alloc(TG)

    def body(block):
        b_pe, b_act, b_dve = c.pe, c.act, c.dve

        def scalar(eng):
            eng.activation(crow, qn2[0:1, g * TG:(g + 1) * TG], AF.Sqrt
                           ).then_inc(sems["act"], 1)
            eng.wait_ge(sems["pe"], b_pe + 2)
            eng.activation(qpb, psum[4][:, :TG], AF.Copy)
            eng.activation(cB, psum[5][:, :TG], AF.Copy).then_inc(sems["act"], 2)
        c.act += 3

        def vector(eng):
            eng.wait_ge(sems["act"], b_act + 1)
            eng.tensor_scalar(crow, crow, kfm, None, ALU.mult).then_inc(sems["dve"], 1)
            dvn = b_dve + 1
            eng.wait_ge(sems["act"], b_act + 3)
            for kc in range(FC):
                eng.wait_ge(sems["dve"], dvn)
                eng.tensor_scalar(m01, qpb, kposb[:, kc:kc + 1], None, ALU.is_ge
                                  ).then_inc(sems["dve"], 1)
                dvn += 1
                eng.wait_ge(sems["dve"], dvn)
                eng.tensor_scalar(m01, m01, 1.0, -NBIG, ALU.subtract, ALU.mult
                                  ).then_inc(sems["dve"], 1)
                dvn += 1
                eng.wait_ge(sems["dve"], dvn)
                eng.tensor_tensor(cmask[:, kc * TG:(kc + 1) * TG], m01, cB,
                                  ALU.subtract).then_inc(sems["dve"], 1)
                dvn += 1
        c.dve += 1 + 3 * FC

        def tensor(eng):
            eng.wait_ge(sems["dve"], b_dve + 1)
            eng.matmul(psum[4][:, :TG], small["c_onesrow"],
                       qposb[0:1, g * TG:(g + 1) * TG], start=True, stop=True)
            eng.matmul(psum[5][:, :TG], small["c_onesrow"], crow,
                       start=True, stop=True).then_inc(sems["pe"], 2)
        c.pe += 2

        return {"scalar": scalar, "vector": vector, "tensor": tensor}

    ctx.phase(body)
    ctx.A.reset(mk)


def _attn_kcs(g):
    qmax = {}
    qmax["A"] = [511, 1023, 3583, 4095][g]
    qmax["B"] = [1535, 2047, 2559, 3071][g]
    need = set()
    for pat in ("A", "B"):
        for kc in range(FC):
            if KPOS_GLOBAL[kc * 128:(kc + 1) * 128].min() <= qmax[pat]:
                need.add(kc)
    return sorted(need)


def _emit_l1a_attn(ctx, g, kvh, cmask):
    c, sems, small, psum = ctx.c, ctx.sems, ctx.small, ctx.psum
    io, dr = ctx.io, ctx.dr
    mk = ctx.A.mark()
    mkb = ctx.AB.mark()
    kh = ctx.AB.alloc(FC * 128)
    vh = ctx.AB.alloc(FC * 128)
    qh = [ctx.AB.alloc(TG) for _ in range(2)]
    sS = [ctx.A.alloc(TG) for _ in range(2)]
    pT = [ctx.AB.alloc(TG) for _ in range(2)]
    zrow = ctx.A.alloc(TG)[0:1, :]
    zb = ctx.A.alloc(TG)
    ao = [ctx.AB.alloc(TG) for _ in range(2)]
    vv = dr["ccout"].rearrange("r (h c) -> (r h) c", h=2)
    kcs = _attn_kcs(g)
    nk = len(kcs)
    PEQ, ACQ, DVQ = 3 * nk + 1, nk + 1, nk + 2

    def body(block):
        b_ld = [c.ld0, c.ld1]
        b_lda = c.lda
        b_gp = [c.gp0, c.gp1]
        b_pe, b_act, b_dve = c.pe, c.act, c.dve

        def sync(eng):
            eng.dma_start(out=kh[:, 0:2048],
                          in_=dr["ccout"][kvh * 128:(kvh + 1) * 128, :]
                          ).then_inc(sems["lda"], 16)
            eng.dma_start(out=kh[:, 2048:4096],
                          in_=dr["ccout"][2049 + kvh * 128:2049 + (kvh + 1) * 128, :]
                          ).then_inc(sems["lda"], 16)
            for r in range(2):
                vr0 = 2 * (r * 2049 + 1024)
                eng.dma_start(
                    out=vh[:, r * 2048:(r + 1) * 2048]
                        .rearrange("p (s d) -> p s d", d=128),
                    in_=vv[vr0:vr0 + 2048, kvh * 128:(kvh + 1) * 128]
                        .rearrange("(s p) d -> p s d", p=128),
                ).then_inc(sems["lda"], 16)
            for qi in range(4):
                hh = kvh * NREP + qi
                if qi >= 2:
                    eng.wait_ge(sems["pe"], b_pe + (qi - 1) * PEQ)
                eng.dma_start(out=qh[qi % 2],
                              in_=dr["qdram"][hh * 128:(hh + 1) * 128,
                                              g * TG:(g + 1) * TG]
                              ).then_inc(sems["ld" + str(qi % 2)], 16)
        c.lda += 64
        c.ld0 += 32
        c.ld1 += 32

        def tensor(eng):
            eng.wait_ge(sems["lda"], b_lda + 64)
            for qi in range(4):
                pe0 = b_pe + qi * PEQ
                act0 = b_act + qi * ACQ
                dve0 = b_dve + qi * DVQ
                eng.wait_ge(sems["ld" + str(qi % 2)],
                            b_ld[qi % 2] + 16 * (qi // 2 + 1))
                if qi >= 1:
                    eng.wait_ge(sems["dve"], dve0)
                def emit_pvz(j):
                    kcj = kcs[j]
                    eng.wait_ge(sems["act"], act0 + j + 1)
                    eng.matmul(psum[2][:, :TG], vh[:, kcj * 128:(kcj + 1) * 128],
                               pT[j % 2], start=(j == 0), stop=(j == nk - 1)
                               ).then_inc(sems["pe"], 1)
                    eng.matmul(psum[3][0:1, :TG], small["c_onescol_b"],
                               pT[j % 2], start=(j == 0), stop=(j == nk - 1)
                               ).then_inc(sems["pe"], 1)

                for j, kc in enumerate(kcs):
                    if j >= 2:
                        eng.wait_ge(sems["dve"], dve0 + j - 1)
                    elif qi >= 1:
                        eng.wait_ge(sems["dve"], dve0)
                    eng.matmul(psum[j % 2][:, :TG], kh[:, kc * 128:(kc + 1) * 128],
                               qh[qi % 2], start=True, stop=True
                               ).then_inc(sems["pe"], 1)
                    if j >= 2:
                        emit_pvz(j - 2)
                emit_pvz(nk - 2)
                emit_pvz(nk - 1)
                eng.wait_ge(sems["dve"], dve0 + nk + 1)
                eng.matmul(psum[4][:, :TG], small["c_onesrow"], zrow,
                           start=True, stop=True).then_inc(sems["pe"], 1)
        c.pe += 4 * PEQ

        def vector(eng):
            for qi in range(4):
                pe0 = b_pe + qi * PEQ
                act0 = b_act + qi * ACQ
                dvn = b_dve + qi * DVQ
                for j, kc in enumerate(kcs):
                    spos = j + 1 if j < 2 else 3 * j - 3
                    eng.wait_ge(sems["pe"], pe0 + spos)
                    if j >= 2:
                        eng.wait_ge(sems["act"], act0 + j - 1)
                    eng.tensor_tensor(sS[j % 2], psum[j % 2][:, :TG],
                                      cmask[:, kc * TG:(kc + 1) * TG], ALU.add
                                      ).then_inc(sems["dve"], 1)
                    dvn += 1
                eng.wait_ge(sems["pe"], pe0 + 3 * nk)

                eng.reciprocal(zrow, psum[3][0:1, :TG]).then_inc(sems["dve"], 1)
                dvn += 1
                eng.wait_ge(sems["act"], act0 + ACQ)
                if qi >= 2:
                    eng.wait_ge(sems["gp" + str(qi % 2)],
                                b_gp[qi % 2] + 16 * (qi // 2))
                eng.tensor_tensor(ao[qi % 2], psum[2][:, :TG], zb, ALU.mult
                                  ).then_inc(sems["dve"], 1)
                dvn += 1
        c.dve += 4 * DVQ

        def scalar(eng):
            for qi in range(4):
                pe0 = b_pe + qi * PEQ
                act0 = b_act + qi * ACQ
                dve0 = b_dve + qi * DVQ
                for j, kc in enumerate(kcs):
                    eng.wait_ge(sems["dve"], dve0 + j + 1)
                    if j >= 2:
                        eng.wait_ge(sems["pe"], pe0 + 3 * j - 1)
                    eng.activation(pT[j % 2], sS[j % 2], AF.Exp
                                   ).then_inc(sems["act"], 1)
                eng.wait_ge(sems["pe"], pe0 + PEQ)
                eng.activation(zb, psum[4][:, :TG], AF.Copy).then_inc(sems["act"], 1)
        c.act += 4 * ACQ

        def gpsimd(eng):
            for qi in range(4):
                hh = kvh * NREP + qi
                dve0 = b_dve + qi * DVQ
                eng.wait_ge(sems["dve"], dve0 + DVQ)
                eng.dma_start(
                    out=dr["sbdram"][g * D + hh * 128:g * D + (hh + 1) * 128, :],
                    in_=ao[qi % 2]).then_inc(sems["gp" + str(qi % 2)], 16)
        c.gp0 += 32
        c.gp1 += 32

        return {"sync": sync, "tensor": tensor, "vector": vector,
                "scalar": scalar, "gpsimd": gpsimd}

    ctx.phase(body)
    ctx.A.reset(mk)
    ctx.AB.reset(mkb)


def emit_l1f(ctx):
    c = ctx.c
    io, dr, sems, small, psum = ctx.io, ctx.dr, ctx.sems, ctx.small, ctx.psum

    mk0 = ctx.A.mark()
    mkb0 = ctx.AB.mark()
    rows = ctx.A.alloc(3 * TG)[0:1, :]
    mB = ctx.A.alloc(TG)
    sB = ctx.A.alloc(TG)
    bbias = ctx.A.alloc(FC)
    xhat = ctx.AB.alloc(FC * TG)
    y = ctx.A.alloc(FC * TG)
    hw = ctx.AB.alloc(TG)
    r16s = ctx.A.alloc(128)[0:8, :]
    bthrB = ctx.A.alloc(4)

    def body_pre(block):
        b_lda = c.lda
        b_pe = c.pe

        def sync(eng):
            for fcb in range(FC):
                eng.dma_start(out=bbias[:, fcb:fcb + 1],
                              in_=io["w_bbias"][fcb * 128:(fcb + 1) * 128, :]
                              ).then_inc(sems["lda"], 16)
            eng.dma_start(out=r16s, in_=io["c_r16"]).then_inc(sems["lda"], 16)
            eng.dma_start(out=bthrB[0:1, 0:1], in_=io["w_bthr"]).then_inc(sems["lda"], 16)
        c.lda += 16 * (FC + 2)

        def tensor(eng):
            eng.wait_ge(sems["lda"], b_lda + 16 * (FC + 2))
            eng.matmul(psum[2][:, 0:1], ctx.small["c_onesrow"], bthrB[0:1, 0:1],
                       start=True, stop=True).then_inc(sems["pe"], 1)
        c.pe += 1

        def scalar(eng):
            eng.wait_ge(sems["pe"], b_pe + 1)
            eng.activation(bthrB[:, 0:1], psum[2][:, 0:1], AF.Copy
                           ).then_inc(sems["act"], 1)
        c.act += 1

        return {"sync": sync, "tensor": tensor, "scalar": scalar}
    ctx.phase(body_pre)

    m2B = ctx.A.alloc(TG)
    s2B = ctx.A.alloc(TG)
    emit_ln_stats_dram(ctx, dr["xdram"], 0, mB, sB, rows)
    emit_ln_apply_dram(ctx, dr["xdram"], 0, mB, sB, xhat)
    _emit_l1f_gates(ctx, 0, xhat, hw, r16s, bthrB)
    for g in range(NG):
        mk = ctx.A.mark()
        mkb = ctx.AB.mark()
        bodies = [emit_proj_sweep(ctx, xhat, io["w_base_t"], FC,
                                  lambda oc: y[:, oc * TG:(oc + 1) * TG],
                                  bias=bbias, extra_dram=io["w_bw_t"],
                                  extra_rhs=hw, defer=True)]
        if g + 1 < NG:
            bodies.append(emit_ln_stats_dram(ctx.alt(), dr["xdram"], g + 1, mB,
                                             sB, rows, defer=True, slim=True))
        ctx.phases(bodies)
        ctx.A.reset(mk)
        ctx.AB.reset(mkb)
        emit_post_stats(ctx, y, rows, m2B, s2B)
        mk = ctx.A.mark()
        bodies = [emit_resid_sbufy(ctx, dr["xdram"], g, y, m2B, s2B,
                                   dr["xdram"], defer=True)]
        if g + 1 < NG:
            bodies.append(emit_ln_apply_dram(ctx.alt(), dr["xdram"], g + 1, mB,
                                             sB, xhat, defer=True))
        ctx.phases(bodies)
        ctx.A.reset(mk)
        if g + 1 < NG:
            _emit_l1f_gates(ctx, g + 1, xhat, hw, r16s, bthrB)
    ctx.A.reset(mk0)
    ctx.AB.reset(mkb0)


def _emit_l1f_gates(ctx, g, xhat, hw, r16s, bthrB):
    c, sems, small, psum = ctx.c, ctx.sems, ctx.small, ctx.psum
    io = ctx.io
    mk = ctx.A.mark()
    mkb = ctx.AB.mark()
    wg = ctx.A.alloc(NE * FC)
    wthr = ctx.A.alloc(FC)
    awT = ctx.A.alloc(FC * 128)
    wgB = ctx.AB.alloc(NE * FC)
    wthrB = ctx.AB.alloc(FC)
    awTB = ctx.AB.alloc(FC * 128)
    geT = ctx.A.alloc(4 * NE)
    eT = ctx.A.alloc(4 * NE)
    thrT = ctx.A.alloc(4)
    thT = ctx.A.alloc(4)
    th2 = ctx.A.alloc(4)
    mx = ctx.A.alloc(4)
    ngm = ctx.A.alloc(4)
    Z = ctx.A.alloc(4)
    wsum = ctx.A.alloc(4)
    wz = ctx.A.alloc(4)
    ws2 = ctx.A.alloc(4)
    Zr = ctx.A.alloc(4)
    wr = ctx.A.alloc(4)
    wT = ctx.A.alloc(TG)[0:8, :]
    h_sb = ctx.A.alloc(TG)
    wexp = ctx.A.alloc(TG)

    def xh_tile(kc, tt):
        return xhat[:, kc * TG + tt * 128:kc * TG + (tt + 1) * 128]

    def body(block):
        b_lda = c.lda
        b_pe, b_act, b_dve = c.pe, c.act, c.dve

        def sync(eng):
            eng.dma_start(out=wg.rearrange("p (kc e) -> p kc e", kc=FC),
                          in_=io["w_gate_t"].rearrange("(kc p) e -> p kc e", p=128)
                          ).then_inc(sems["lda"], 16)
            for fcb in range(FC):
                eng.dma_start(out=wthr[:, fcb:fcb + 1],
                              in_=io["w_thr"][fcb * 128:(fcb + 1) * 128, :]
                              ).then_inc(sems["lda"], 16)
            eng.dma_start(out=awT.rearrange("p (kc e) -> p kc e", kc=FC),
                          in_=io["w_aw_t"].rearrange("(kc p) e -> p kc e", p=128)
                          ).then_inc(sems["lda"], 16)
        c.lda += 16 * (FC + 2)

        def tensor(eng):
            eng.wait_ge(sems["dve"], b_dve + 3)
            for tt in range(4):
                if tt >= 2:
                    eng.wait_ge(sems["act"], b_act + tt - 1)
                for kc in range(FC):
                    eng.matmul(psum[tt % 2][:, tt * NE:(tt + 1) * NE],
                               xh_tile(kc, tt), wgB[:, kc * NE:(kc + 1) * NE],
                               start=(kc == 0), stop=(kc == FC - 1)
                               ).then_inc(sems["pe"], 1)
            for tt in range(4):
                if tt >= 2:
                    eng.wait_ge(sems["act"], b_act + 4 + tt - 1)
                for kc in range(FC):
                    eng.matmul(psum[2 + tt % 2][:, tt:tt + 1],
                               xh_tile(kc, tt), wthrB[:, kc:kc + 1],
                               start=(kc == 0), stop=(kc == FC - 1)
                               ).then_inc(sems["pe"], 1)
            for kc in range(FC):
                eng.matmul(psum[4][:, :TG], awTB[:, kc * 128:(kc + 1) * 128],
                           xhat[:, kc * TG:(kc + 1) * TG],
                           start=(kc == 0), stop=(kc == FC - 1)
                           ).then_inc(sems["pe"], 1)
            for tt in range(4):
                eng.wait_ge(sems["dve"], b_dve + 3 + 8 + 10 * (tt + 1))
                eng.transpose(psum[5][0:8, tt * 128:(tt + 1) * 128],
                              eT[:, tt * NE:(tt + 1) * NE],
                              small["c_ident"]).then_inc(sems["pe"], 1)
            eng.wait_ge(sems["act"], b_act + 18)
            eng.matmul(psum[5][:, :TG], r16s, wT, start=True, stop=True
                       ).then_inc(sems["pe"], 1)
        c.pe += 293

        def scalar(eng):
            for tt in range(4):
                eng.wait_ge(sems["pe"], b_pe + 32 * (tt + 1))
                eng.activation(geT[:, tt * NE:(tt + 1) * NE],
                               psum[tt % 2][:, tt * NE:(tt + 1) * NE], AF.Copy
                               ).then_inc(sems["act"], 1)
            for tt in range(4):
                eng.wait_ge(sems["pe"], b_pe + 128 + 32 * (tt + 1))
                eng.activation(thrT[:, tt:tt + 1], psum[2 + tt % 2][:, tt:tt + 1],
                               AF.Copy).then_inc(sems["act"], 1)
            for tt in range(4):
                eng.wait_ge(sems["act"], b_act + 4 + tt + 1)
                eng.activation(thT[:, tt:tt + 1], thrT[:, tt:tt + 1], AF.Sigmoid,
                               bias=bthrB[:, 0:1]).then_inc(sems["act"], 1)
            for tt in range(4):
                eng.wait_ge(sems["dve"], b_dve + 3 + 2 * (tt + 1))
                eng.activation(eT[:, tt * NE:(tt + 1) * NE],
                               geT[:, tt * NE:(tt + 1) * NE], AF.Exp,
                               bias=ngm[:, tt:tt + 1],
                               accum_out=Z[:, tt:tt + 1]).then_inc(sems["act"], 1)
            eng.wait_ge(sems["pe"], b_pe + 288)
            eng.activation(h_sb, psum[4][:, :TG], AF.Copy).then_inc(sems["act"], 1)
            eng.wait_ge(sems["pe"], b_pe + 292)
            eng.activation(wT, psum[5][0:8, :TG], AF.Copy).then_inc(sems["act"], 1)
            eng.wait_ge(sems["pe"], b_pe + 293)
            eng.activation(wexp, psum[5][:, :TG], AF.Copy).then_inc(sems["act"], 1)
        c.act += 19

        def vector(eng):
            eng.wait_ge(sems["lda"], b_lda + 16 * (FC + 2))
            eng.tensor_copy(wgB, wg).then_inc(sems["dve"], 1)
            eng.tensor_copy(wthrB, wthr).then_inc(sems["dve"], 1)
            eng.tensor_copy(awTB, awT).then_inc(sems["dve"], 1)
            dvn = b_dve + 3
            for tt in range(4):
                eng.wait_ge(sems["act"], b_act + tt + 1)
                eng.tensor_reduce(mx[:, tt:tt + 1], geT[:, tt * NE:(tt + 1) * NE],
                                  mybir.AxisListType.X, ALU.max).then_inc(sems["dve"], 1)
                dvn += 1
                eng.wait_ge(sems["dve"], dvn)
                eng.tensor_scalar_mul(ngm[:, tt:tt + 1], mx[:, tt:tt + 1], -1.0
                                      ).then_inc(sems["dve"], 1)
                dvn += 1
            for tt in range(4):
                esl = eT[:, tt * NE:(tt + 1) * NE]
                eng.wait_ge(sems["act"], b_act + 12 + tt + 1)
                eng.reciprocal(Zr[:, tt:tt + 1], Z[:, tt:tt + 1]).then_inc(sems["dve"], 1)
                dvn += 1
                eng.wait_ge(sems["dve"], dvn)
                eng.tensor_scalar(esl, esl, Zr[:, tt:tt + 1], None, ALU.mult
                                  ).then_inc(sems["dve"], 1)
                dvn += 1
                eng.wait_ge(sems["act"], b_act + 8 + tt + 1)
                eng.tensor_scalar_mul(th2[:, tt:tt + 1], thT[:, tt:tt + 1],
                                      MAX_THR).then_inc(sems["dve"], 1)
                dvn += 1
                eng.wait_ge(sems["dve"], dvn)
                eng.tensor_scalar(esl, esl, th2[:, tt:tt + 1], 0.0,
                                  ALU.subtract, ALU.max).then_inc(sems["dve"], 1)
                dvn += 1
                eng.wait_ge(sems["dve"], dvn)
                eng.tensor_reduce(wsum[:, tt:tt + 1], esl,
                                  mybir.AxisListType.X, ALU.add).then_inc(sems["dve"], 1)
                dvn += 1
                eng.wait_ge(sems["dve"], dvn)
                eng.tensor_scalar(wz[:, tt:tt + 1], wsum[:, tt:tt + 1], 0.0, None,
                                  ALU.is_equal).then_inc(sems["dve"], 1)
                dvn += 1
                eng.wait_ge(sems["dve"], dvn)
                eng.tensor_scalar_mul(ws2[:, tt:tt + 1], wsum[:, tt:tt + 1],
                                      1.0 / MOE_SCALING).then_inc(sems["dve"], 1)
                dvn += 1
                eng.wait_ge(sems["dve"], dvn)
                eng.tensor_tensor(ws2[:, tt:tt + 1], ws2[:, tt:tt + 1],
                                  wz[:, tt:tt + 1], ALU.add).then_inc(sems["dve"], 1)
                dvn += 1
                eng.wait_ge(sems["dve"], dvn)
                eng.reciprocal(wr[:, tt:tt + 1], ws2[:, tt:tt + 1]).then_inc(sems["dve"], 1)
                dvn += 1
                eng.wait_ge(sems["dve"], dvn)
                eng.tensor_scalar(esl, esl, wr[:, tt:tt + 1], None, ALU.mult
                                  ).then_inc(sems["dve"], 1)
                dvn += 1
            eng.wait_ge(sems["act"], b_act + 19)
            eng.tensor_tensor(hw, h_sb, wexp, ALU.mult).then_inc(sems["dve"], 1)
            dvn += 1
        c.dve += 52

        return {"sync": sync, "tensor": tensor, "scalar": scalar, "vector": vector}

    ctx.phase(body)
    ctx.A.reset(mk)
    ctx.AB.reset(mkb)


def build_core_program(sublayers):
    nc = bass.Bass("TRN2", target_bir_lowering=False)
    io = {}
    io["xin"] = nc.dram_tensor("xin", [TOK, D], F32, kind="ExternalInput").ap()
    io["xout"] = nc.dram_tensor("xout", [TOK, D], F32, kind="ExternalOutput").ap()
    if DEBUG_TAP is not None:
        io["dbg"] = nc.dram_tensor("dbg", [D, TOK], F32, kind="ExternalOutput").ap()
    for k, shp in CONST_SHAPES.items():
        io[k] = nc.dram_tensor(k, shp, F32, kind="ExternalInput").ap()

    if "l0a" in sublayers:
        io["w_mask"] = nc.dram_tensor("w_mask", [D, D], I32, kind="ExternalInput").ap()
        io["w_weight"] = nc.dram_tensor("w_weight", [D, D], F32, kind="ExternalInput").ap()
        io["w_bias"] = nc.dram_tensor("w_bias", [D, 1], F32, kind="ExternalInput").ap()
    if "l0f" in sublayers:
        io["w_up"] = nc.dram_tensor("w_up", [FH * FD, FM], F32, kind="ExternalInput").ap()
        io["w_down"] = nc.dram_tensor("w_down", [FH * FM, FD], F32, kind="ExternalInput").ap()
        io["w_perm"] = nc.dram_tensor("w_perm", [D, 1], I32, kind="ExternalInput").ap()
    if "l1a" in sublayers:
        io["w_wq"] = nc.dram_tensor("w_wq", [D, D], F32, kind="ExternalInput").ap()
        io["w_wk"] = nc.dram_tensor("w_wk", [NKV * HD, D], F32, kind="ExternalInput").ap()
        io["w_wv"] = nc.dram_tensor("w_wv", [NKV * HD, D], F32, kind="ExternalInput").ap()
        io["w_wo"] = nc.dram_tensor("w_wo", [D, D], F32, kind="ExternalInput").ap()
    if "l1f" in sublayers:
        io["w_base_t"] = nc.dram_tensor("w_base_t", [D, D], F32, kind="ExternalInput").ap()
        io["w_bbias"] = nc.dram_tensor("w_bbias", [D, 1], F32, kind="ExternalInput").ap()
        io["w_gate_t"] = nc.dram_tensor("w_gate_t", [D, NE], F32, kind="ExternalInput").ap()
        io["w_thr"] = nc.dram_tensor("w_thr", [D, 1], F32, kind="ExternalInput").ap()
        io["w_bthr"] = nc.dram_tensor("w_bthr", [1, 1], F32, kind="ExternalInput").ap()
        io["w_aw_t"] = nc.dram_tensor("w_aw_t", [D, NE * NR], F32, kind="ExternalInput").ap()
        io["w_bw_t"] = nc.dram_tensor("w_bw_t", [NE * NR, D], F32, kind="ExternalInput").ap()

    dr = {}
    dr["xdram"] = nc.dram_tensor("xdram", [D, TOK], F32, kind="Internal").ap()
    if "l0a" in sublayers:
        dr["wtdram"] = nc.dram_tensor("wtdram", [D, D], BF16, kind="Internal").ap()
    if "l0f" in sublayers or "l1a" in sublayers:
        dr["sbdram"] = nc.dram_tensor("sbdram", [NG * D, TG], BF16, kind="Internal").ap()
    if "l0f" in sublayers:
        dr["s2dram"] = nc.dram_tensor("s2dram", [NG * D, TG], F32, kind="Internal").ap()
    if "l1a" in sublayers:
        dr["qdram"] = nc.dram_tensor("qdram", [D, TOK], BF16, kind="Internal").ap()
        dr["ccin"] = nc.dram_tensor("ccin", [2049, TOK], BF16, kind="Internal").ap()
        dr["ccout"] = nc.dram_tensor("ccout", [2 * 2049, TOK], BF16, kind="Internal").ap()

    AR = nc.alloc_sbuf_tensor("arena", [128, ARENA_COLS], F32).ap()
    ARB = nc.alloc_sbuf_tensor("arenab", [128, ARENAB_COLS], BF16).ap()
    small = {}
    for k in SMALL_CONSTS:
        small[k] = nc.alloc_sbuf_tensor("s" + k, CONST_SHAPES[k], F32).ap()
    small["c_onescol_b"] = nc.alloc_sbuf_tensor("sc_onescol_b", [128, 1], BF16).ap()

    psum = [nc.alloc_psum_tensor(f"ps{i}", [128, 512], F32).ap() for i in range(6)]

    cts = Counts()
    sems = {}
    hold = []
    for name in ("ph",) + DMA_SEMS + ENG_SEMS:
        h = nc.semaphore("sem_" + name)
        sems[name] = h.__enter__()
        hold.append(h)

    ctx = Ctx(nc, io, dr, Arena(AR, ARENA_COLS), Arena(ARB, ARENAB_COLS),
              small, psum, cts, sems)

    def ph_consts(block):
        lda0 = cts.lda

        def sync(eng):
            for k in SMALL_CONSTS:
                eng.dma_start(out=small[k], in_=io[k]).then_inc(sems["lda"], 16)
                cts.lda += 16

        def vector(eng):
            eng.wait_ge(sems["lda"], lda0 + 16 * len(SMALL_CONSTS))
            eng.tensor_copy(small["c_onescol_b"], small["c_onescol"]
                            ).then_inc(sems["dve"], 1)
        cts.dve += 1
        return {"sync": sync, "vector": vector}
    ctx.phase(ph_consts)

    emit_entry_transpose(ctx, io["xin"], dr["xdram"])
    emitters = {"l0a": emit_l0a, "l0f": emit_l0f, "l1a": emit_l1a, "l1f": emit_l1f}
    for name in sublayers:
        emitters[name](ctx)
    emit_exit_transpose(ctx, dr["xdram"], io["xout"])

    for h in reversed(hold):
        h.__exit__(None, None, None)
    return nc


def _shard_tokens(x_flat):
    out = []
    for core in range(NCORE):
        parts = [x_flat[a:b] for a, b in _token_slices(core)]
        out.append(np.ascontiguousarray(np.concatenate(parts, axis=0)))
    return out


def _unshard_tokens(shards):
    x = np.empty((B * S, D), np.float32)
    for core in range(NCORE):
        pos = 0
        for a, b in _token_slices(core):
            n = b - a
            x[a:b] = shards[core][pos:pos + n]
            pos += n
    return x


def _tileize(w):
    K, N = w.shape
    return np.ascontiguousarray(
        np.asarray(w, np.float32).reshape(K // 128, 128, N // 128, 128)
        .transpose(2, 1, 0, 3).reshape(N, K))


def _layer_inputs(name, inputs):
    w = {}
    if name == "l0a":
        w["w_mask"] = np.ascontiguousarray(inputs["L0_attn_mask"].astype(np.int32))
        w["w_weight"] = np.ascontiguousarray(inputs["L0_attn_weight"])
        w["w_bias"] = np.ascontiguousarray(
            np.asarray(inputs["L0_attn_bias"], np.float32).reshape(D, 1))
    elif name == "l0f":
        w["w_up"] = np.ascontiguousarray(
            np.asarray(inputs["L0_ffn_up"], np.float32).reshape(FH * FD, FM))
        w["w_down"] = np.ascontiguousarray(
            np.asarray(inputs["L0_ffn_down"], np.float32).reshape(FH * FM, FD))
        w["w_perm"] = np.ascontiguousarray(
            inputs["L0_ffn_perm"].astype(np.int32).reshape(D, 1))
    elif name == "l1a":
        w["w_wq"] = _tileize(inputs["L1_attn_wq"])
        w["w_wk"] = _tileize(inputs["L1_attn_wk"])
        w["w_wv"] = _tileize(inputs["L1_attn_wv"])
        w["w_wo"] = _tileize(inputs["L1_attn_wo"])
    elif name == "l1f":
        w["w_base_t"] = _tileize(
            np.asarray(inputs["L1_ffn_W_base"], np.float32).T)
        w["w_bbias"] = np.ascontiguousarray(
            np.asarray(inputs["L1_ffn_b_base"], np.float32).reshape(D, 1))
        w["w_gate_t"] = np.ascontiguousarray(
            np.asarray(inputs["L1_ffn_W_gate"], np.float32).T)
        w["w_thr"] = np.ascontiguousarray(
            np.asarray(inputs["L1_ffn_W_thr"], np.float32).reshape(D, 1))
        w["w_bthr"] = np.ascontiguousarray(
            np.asarray(inputs["L1_ffn_b_thr"], np.float32).reshape(1, 1))
        w["w_aw_t"] = np.ascontiguousarray(
            np.asarray(inputs["L1_ffn_A_w"], np.float32).reshape(NE * NR, D).T)
        w["w_bw_t"] = np.ascontiguousarray(
            np.asarray(inputs["L1_ffn_B_w"], np.float32).transpose(0, 2, 1)
            .reshape(NE * NR, D))
    return w


def run_layers(x_shards, layers, inputs, core_ids=None):
    if core_ids is None:
        core_ids = list(range(NCORE))
    nc = build_core_program(layers)
    wmaps = {}
    for name in layers:
        wmaps.update(_layer_inputs(name, inputs))
    in_maps = []
    for core in core_ids:
        m = {"xin": x_shards[core]}
        m.update(make_consts(core))
        m.update(wmaps)
        in_maps.append(m)
    res = run_bass_kernel_spmd(nc, in_maps, core_ids=core_ids)
    return [r["xout"] for r in res.results]


ALL_LAYERS = ["l0a", "l0f", "l1a", "l1f"]


def kernel(**inputs):
    x = np.ascontiguousarray(np.asarray(inputs["L0_attn_data"], np.float32)
                             .reshape(B * S, D))
    shards = _shard_tokens(x)
    shards = run_layers(shards, ALL_LAYERS, inputs)
    out = _unshard_tokens(shards)
    return out.reshape(B, S, D)
